# version 08
# speedup vs baseline: 1.1379x; 1.0035x over previous
import numpy as np
import ml_dtypes
from contextlib import ExitStack

import concourse.bass as bass
import concourse.mybir as mybir
from concourse.bass_utils import run_bass_kernel_spmd
from concourse.aot_env import allow_ici_all_to_all

F32 = mybir.dt.float32
BF16 = mybir.dt.bfloat16
AF = mybir.ActivationFunctionType
ALU = mybir.AluOpType
AX = mybir.AxisListType
BFNP = ml_dtypes.bfloat16

NT = 16
NHALO = 4
D = 1024
EPS = 1e-5


class KC:
    def __init__(self, nc, es, block):
        self.nc, self.es, self.block = nc, es, block
        self.sem, self.n, self.t = {}, {}, {}

    def mksem(self, *names):
        for name in names:
            self.sem[name] = self.es.enter_context(self.nc.semaphore(name))
            self.n[name] = 0

    def sb(self, name, shape, dtype, stack=None):
        self.t[name] = (stack or self.es).enter_context(self.nc.sbuf_tensor(name, shape, dtype))
        return self.t[name]

    def reg(self, name, off_b, cols, dtype, rows=128):
        assert off_b % 64 == 0
        if dtype == BF16:
            v = self.arena[0:rows, off_b // 2: off_b // 2 + cols]
        else:
            v = self.arena[0:rows, off_b // 2: off_b // 2 + 2 * cols].bitcast(F32)
        self.t[name] = v
        return v

    def ps(self, name, shape, dtype=F32):
        self.t[name] = self.es.enter_context(self.nc.psum_tensor(name, shape, dtype))
        return self.t[name]

    def inc(self, instr, s, k=1):
        instr.then_inc(self.sem[s], k)
        self.n[s] += k
        return self.n[s]

    def inc2(self, instr, s1, s2):
        instr.then_inc(self.sem[s1], 1).then_inc(self.sem[s2], 1)
        self.n[s1] += 1
        self.n[s2] += 1
        return self.n[s2]

    def w(self, eng, s, v=None):
        eng.wait_ge(self.sem[s], self.n[s] if v is None else v)


def emit_ln_stats(kc, xbuf, tiles, scol, sqd='sqd'):
    block, st = kc.block, kc.t['stats']
    nt = len(tiles)
    S0, S1, S2, S3, S4 = (scol, scol + nt, scol + 2 * nt, scol + 3 * nt, scol + 4 * nt)

    @block.vector
    def _(v):
        kc.w(v, 'sv')
        for i, tl in enumerate(tiles):
            ins = v.reduce_sum(out=st[:, S0 + i:S0 + i + 1],
                               in_=kc.t[xbuf][:, 1024 * tl:1024 * (tl + 1)], axis=AX.X)
        kc.inc(ins, 'sv')

    @block.scalar
    def _(s):
        kc.w(s, 'sv')
        kc.w(s, 'ss')
        for i, tl in enumerate(tiles):
            ins = s.activation(out=kc.t[sqd][:, :], in_=kc.t[xbuf][:, 1024 * tl:1024 * (tl + 1)],
                               func=AF.Square, accum_out=st[:, S1 + i:S1 + i + 1])
        kc.inc(ins, 'ss')

    @block.vector
    def _(v):
        kc.w(v, 'sv')
        kc.w(v, 'ss')
        v.tensor_scalar(out=st[:, S2:S2 + nt], in0=st[:, S0:S0 + nt],
                        scalar1=1.0 / D, scalar2=None, op0=ALU.mult)
        ins = v.tensor_scalar(out=st[:, S3:S3 + nt], in0=st[:, S1:S1 + nt],
                              scalar1=1.0 / D, scalar2=None, op0=ALU.mult)
        kc.inc(ins, 'sv')
        kc.w(v, 'sv')
        ins = v.tensor_tensor(out=st[:, S4:S4 + nt], in0=st[:, S2:S2 + nt],
                              in1=st[:, S2:S2 + nt], op=ALU.mult)
        kc.inc(ins, 'sv')
        kc.w(v, 'sv')
        ins = v.tensor_tensor(out=st[:, S3:S3 + nt], in0=st[:, S3:S3 + nt],
                              in1=st[:, S4:S4 + nt], op=ALU.subtract)
        kc.inc(ins, 'sv')
        kc.w(v, 'sv')
        ins = v.tensor_scalar(out=st[:, S3:S3 + nt], in0=st[:, S3:S3 + nt],
                              scalar1=0.0, scalar2=None, op0=ALU.max)
        kc.inc(ins, 'sv')

    @block.scalar
    def _(s):
        kc.w(s, 'sv')
        ins = s.activation(out=st[:, S4:S4 + nt], in_=st[:, S3:S3 + nt],
                           func=AF.Ln, bias=kc.t['epsc'][:, 0:1])
        kc.inc(ins, 'ss')
        kc.w(s, 'ss')
        ins = s.activation(out=st[:, S4:S4 + nt], in_=st[:, S4:S4 + nt],
                           func=AF.Exp, scale=-0.5)
        kc.inc(ins, 'ss')
    return S2, S4


def _masks_attn_a(core):
    j = np.arange(192)[None, :]
    r = np.arange(128)[:, None]
    m = np.zeros((2, 128, 192), dtype=np.float32)
    for qt in range(2):
        ok = (j >= r + 1) & (j <= r + 64)
        if core == 0 and qt == 0:
            ok = ok & (j >= 64)
        m[qt] = np.where(ok, 0.0, MBIG)
    return m.astype(BFNP)


MBIG = 30000.0


def _tri_masks():
    r = np.arange(128)
    causB = np.where(r[None, :] > r[:, None], MBIG, 0.0).astype(np.float32)
    antiB = np.where(r[None, :] < r[:, None], MBIG, 0.0).astype(np.float32)
    return causB.astype(BFNP), antiB.astype(BFNP)


def _mqk():
    A = 128.0
    colp = np.arange(2048)
    T, rem = colp // 128, colp % 128
    s, pp = rem // 16, rem % 16
    mq = np.zeros((17, 2048), dtype=np.float32)
    for i in range(16):
        mq[i] = (pp == i) * A
    mq[16] = A
    mk = -mq.copy()
    mk[16] = A
    return mq.astype(BFNP), mk.astype(BFNP)


def host_prep(inputs):
    x0 = np.asarray(inputs['L0_attn_a_x'])
    T = lambda a: np.ascontiguousarray(np.asarray(a).T)
    A = lambda a: np.ascontiguousarray(np.asarray(a, dtype=np.float32))
    com = {
        'wqaT': T(inputs['L0_attn_a_Wq']), 'wkaT': T(inputs['L0_attn_a_Wk']),
        'wvaT': T(inputs['L0_attn_a_Wv']), 'woaT': T(inputs['L0_attn_a_Wo']),
        'winU': np.ascontiguousarray(np.asarray(inputs['L0_ffn_in_proj_w'], dtype=np.float32)
                                     .reshape(24, 128, 1024).transpose(0, 2, 1)),
        'binc': np.ascontiguousarray(A(inputs['L0_ffn_in_proj_b']).reshape(24, 128).T),
        'woutT': T(inputs['L0_ffn_out_w']),
        'outb_row': A(inputs['L0_ffn_out_b']).reshape(1, 1024),
        'gwT': T(inputs['L0_ffn_gate_w']),
        'gb_row': A(inputs['L0_ffn_gate_b']).reshape(1, 8),
        'w1U': np.ascontiguousarray(np.asarray(inputs['L0_ffn_w1']).transpose(0, 2, 1)
                                    .reshape(8, 1024, 32, 128).transpose(0, 2, 1, 3)),
        'w2T': np.ascontiguousarray(np.asarray(inputs['L0_ffn_w2']).transpose(0, 2, 1)),
        'b1c': np.ascontiguousarray(
            A(inputs['L0_ffn_b1']).reshape(8, 32, 128).transpose(2, 0, 1).reshape(128, 256)),
        'b2a': A(inputs['L0_ffn_b2']),
        'g1c': np.ascontiguousarray(A(inputs['L0_ffn_ln1_g']).reshape(8, 128).T),
        'b1lc': np.ascontiguousarray(A(inputs['L0_ffn_ln1_b']).reshape(8, 128).T),
        'g2B': np.ascontiguousarray(np.tile(A(inputs['L0_ffn_ln2_g'])[None, :], (128, 1))),
        'b2B': np.ascontiguousarray(np.tile(A(inputs['L0_ffn_ln2_b'])[None, :], (128, 1))),
        'wkbT2': np.ascontiguousarray(np.concatenate([T(inputs['L0_attn_b_Wk'])] * 2, axis=1)),
        'wvbT': T(inputs['L0_attn_b_Wv']),
        'wfT': T(inputs['L0_attn_b_Wf']),
        'bf_row': A(inputs['L0_attn_b_bf']).reshape(1, 1024),
        'idn': np.eye(128, dtype=BFNP),
    }
    com['mq'], com['mk'] = _mqk()
    com['mcaus'], com['manti'] = _tri_masks()
    wqbT = T(inputs['L0_attn_b_Wq'])
    in_maps = []
    for c in range(8):
        m = dict(com)
        xs = np.zeros((8, 320, 1024), dtype=np.float32)
        xs[:, 64:] = x0[:, 256 * c:256 * (c + 1)]
        if c > 0:
            xs[:, :64] = x0[:, 256 * c - 64:256 * c]
        m['xs'] = xs
        m['mska'] = _masks_attn_a(c)
        m['wqbT'] = np.ascontiguousarray(wqbT[:, 128 * c:128 * (c + 1)])
        in_maps.append(m)
    return in_maps


VT1 = 163840


def _vt(kc, s, k):
    if k < 2:
        base = (VT1 + 2048 * (2 * s + k)) // 2
        return kc.arena[0:128, base:base + 1024]
    base = (VT1 + 32768 + 2048 * (s // 2)) // 2
    r0 = 64 * (s % 2)
    return kc.arena[r0:r0 + 64, base:base + 1024]


def run_s1(kc, h):
    block = kc.block
    kc.reg('ufm', 32768, 8 * 2560, BF16)
    kc.reg('xh', 73728, NHALO * 1024, BF16)
    kc.reg('utmpA', 81920, 2 * 1024, BF16)
    kc.reg('sqdA', 86016, 1024, F32)
    psA, psB = kc.t['psA'], kc.t['psB']

    @block.gpsimd
    def _(g):
        for t in range(NHALO):
            for k in range(2):
                kc.inc(g.dma_start(out=kc.t['xh'][64 * k:64 * (k + 1), 1024 * t:1024 * (t + 1)],
                                   in_=h['xs'].ap()[2 * t + k, 0:64, :]), 'dq', 16)

    @block.vector
    def _(v):
        kc.w(v, 'dq')
        kc.w(v, 'ds')

    @block.scalar
    def _(s):
        kc.w(s, 'dq')
    M1, R1 = emit_ln_stats(kc, 'xsb', list(range(NT)), 0, 'sqdA')
    M2, R2 = emit_ln_stats(kc, 'xh', list(range(NHALO)), 100, 'sqdA')

    evs = []
    for it in range(NT + NHALO):
        halo = it >= NT
        t = it - NT if halo else it

        @block.vector
        def _(v):
            kc.w(v, 'ss')
            if it >= 2:
                kc.w(v, 'st', evs[it - 2][0])
            src = kc.t['xh'] if halo else kc.t['xsb']
            mcol = (M2 + t) if halo else (M1 + t)
            rcol = (R2 + t) if halo else (R1 + t)
            ins = v.tensor_scalar(out=kc.t['utmpA'][:, 1024 * (it % 2):1024 * (it % 2 + 1)],
                                  in0=src[:, 1024 * t:1024 * (t + 1)],
                                  scalar1=kc.t['stats'][:, mcol:mcol + 1],
                                  scalar2=kc.t['stats'][:, rcol:rcol + 1],
                                  op0=ALU.subtract, op1=ALU.mult)
            kc.inc(ins, 'sv')

        @block.tensor
        def _(tn):
            kc.w(tn, 'sv')
            if it >= 2:
                kc.w(tn, 'sv', evs[it - 2][1])
            for f in range(8):
                ins = tn.matmul(out=psA[:, 1024 * (it % 2) + 128 * f:1024 * (it % 2) + 128 * (f + 1)],
                                lhsT=kc.t['utmpA'][:, 1024 * (it % 2) + 128 * f:
                                                  1024 * (it % 2) + 128 * (f + 1)],
                                rhs=kc.t['idnS'][:, :], start=True, stop=True)
            kc.inc(ins, 'st')
        st_t = kc.n['st']

        @block.vector
        def _(v):
            kc.w(v, 'st')
            if not halo:
                s, hv = t // 2, t % 2
                a0 = 320 * s + 64 + 128 * hv
                ov = kc.t['ufm'][:, :].rearrange("p (f a) -> p f a", a=2560)[:, :, a0:a0 + 128]
                iv = psA[:, 1024 * (it % 2):1024 * (it % 2 + 1)].rearrange("p (f a) -> p f a", a=128)
            else:
                ov = kc.t['ufm'][:, :].rearrange("p (f s i) -> p f s i", s=8, i=320)[
                    :, :, 2 * t:2 * t + 2, 0:64]
                iv = psA[:, 1024 * (it % 2):1024 * (it % 2 + 1)].rearrange(
                    "p (f k i) -> p f k i", k=2, i=64)
            ins = v.tensor_copy(out=ov, in_=iv)
            kc.inc(ins, 'sv')
        evs.append((st_t, kc.n['sv']))
    ufm_done = kc.n['sv']
    kc.reg('qfull', 73728, 8 * 2048, BF16)
    kc.reg('wbuf', 106496, 8 * 1024, BF16)

    @block.gpsimd
    def _(g):
        kc.w(g, 'st')
        kc.inc(g.dma_start(out=kc.t['wbuf'][:, :].rearrange("p (c f) -> p c f", f=1024),
                           in_=h['wqaT'].ap().rearrange("(c p) f -> p c f", p=128)), 'dq', 16)
    ld_wq = kc.n['dq']

    ev_q = []
    for oc in range(8):
        for grp in range(4):
            idx = oc * 4 + grp
            reg = idx % 4

            @block.tensor
            def _(tn):
                if idx == 0:
                    kc.w(tn, 'dq', ld_wq)
                    kc.w(tn, 'sv', ufm_done)
                if idx >= 4:
                    kc.w(tn, 'ss', ev_q[idx - 4])
                ua = kc.t['ufm'][:, :].rearrange("p (f s i) -> p f s i", f=8, i=320)
                rhs = ua[:, :, 2 * grp:2 * grp + 2, 64:320]
                ov = psB[:, 512 * reg:512 * (reg + 1)].rearrange("p (s i) -> p s i", s=2)
                for c in range(8):
                    ins = tn.matmul(out=ov, lhsT=kc.t['wbuf'][:, 1024 * c + 128 * oc:
                                                              1024 * c + 128 * (oc + 1)],
                                    rhs=rhs[:, c, :, :], start=(c == 0), stop=(c == 7))
                kc.inc(ins, 'st')

            @block.vector
            def _(v):
                kc.w(v, 'st')
                ins = v.tensor_scalar(out=kc.t['qfull'][:, 2048 * oc + 512 * grp:2048 * oc + 512 * (grp + 1)],
                                      in0=psB[:, 512 * reg:512 * (reg + 1)],
                                      scalar1=-0.125, scalar2=None, op0=ALU.mult)
                kc.inc(ins, 'ss')
            ev_q.append(kc.n['ss'])

    @block.gpsimd
    def _(g):
        kc.w(g, 'st')
        kc.inc(g.dma_start(out=kc.t['wbuf'][:, :].rearrange("p (c f) -> p c f", f=1024),
                           in_=h['wkaT'].ap().rearrange("(c p) f -> p c f", p=128)), 'dq', 16)
    ld_wk = kc.n['dq']
    kc.reg('kfm', 122880, 8 * 2560, BF16)

    ev_k = []
    for oc in range(8):
        for grp in range(5):
            idx = oc * 5 + grp
            reg = idx % 4

            @block.tensor
            def _(tn):
                if idx == 0:
                    kc.w(tn, 'dq', ld_wk)
                    kc.w(tn, 'ss', ev_q[-1])
                if idx >= 4:
                    kc.w(tn, 'sv', ev_k[idx - 4])
                for c in range(8):
                    ins = tn.matmul(out=psB[:, 512 * reg:512 * (reg + 1)],
                                    lhsT=kc.t['wbuf'][:, 1024 * c + 128 * oc:1024 * c + 128 * (oc + 1)],
                                    rhs=kc.t['ufm'][:, 2560 * c + 512 * grp:2560 * c + 512 * (grp + 1)],
                                    start=(c == 0), stop=(c == 7))
                kc.inc(ins, 'st')

            @block.vector
            def _(v):
                kc.w(v, 'st')
                ins = v.tensor_copy(out=kc.t['kfm'][:, 2560 * oc + 512 * grp:2560 * oc + 512 * (grp + 1)],
                                    in_=psB[:, 512 * reg:512 * (reg + 1)])
                kc.inc(ins, 'sv')
            ev_k.append(kc.n['sv'])

    @block.gpsimd
    def _(g):
        kc.w(g, 'st')
        kc.inc(g.dma_start(out=kc.t['wbuf'][:, :].rearrange("p (c f) -> p c f", f=1024),
                           in_=h['wvaT'].ap().rearrange("(c p) f -> p c f", p=128)), 'dq', 16)
    ld_wv = kc.n['dq']

    ev_v = []
    for s in range(8):
        for k in range(3):
            idx = 3 * s + k
            reg = idx % 2

            @block.tensor
            def _(tn):
                if idx == 0:
                    kc.w(tn, 'dq', ld_wv)
                    kc.w(tn, 'sv', ev_k[-1])
                elif idx == 1:
                    pass
                else:
                    kc.w(tn, 'sv', ev_v[idx - 2])
                rows = 128 if k < 2 else 64
                r0 = 64 * (s % 2) if k == 2 else 0
                tp = (0, r0) if k == 2 else None
                ua = kc.t['ufm'][:, :].rearrange("p (f a) -> p f a", a=2560)
                for half in range(2):
                    for c in range(8):
                        ins = tn.matmul(out=psB[r0:r0 + rows, 1024 * reg + 512 * half:
                                                1024 * reg + 512 * (half + 1)],
                                        lhsT=ua[:, c, 320 * s + 128 * k:320 * s + 128 * k + rows],
                                        rhs=kc.t['wbuf'][:, 1024 * c + 512 * half:
                                                         1024 * c + 512 * (half + 1)],
                                        start=(c == 0), stop=(c == 7), tile_position=tp)
                kc.inc(ins, 'st')

            @block.vector
            def _(v):
                kc.w(v, 'st')
                rows = 128 if k < 2 else 64
                r0 = 64 * (s % 2) if k == 2 else 0
                ins = v.tensor_copy(out=_vt(kc, s, k),
                                    in_=psB[r0:r0 + rows, 1024 * reg:1024 * (reg + 1)])
                kc.inc(ins, 'sv')
            ev_v.append(kc.n['sv'])

    @block.gpsimd
    def _(g):
        kc.w(g, 'st')
        kc.inc(g.dma_start(out=kc.t['wbuf'][:, :].rearrange("p (c f) -> p c f", f=1024),
                           in_=h['woaT'].ap().rearrange("(c p) f -> p c f", p=128)), 'dq', 16)
    ld_wo = kc.n['dq']
    kc.reg('mskaS', 32768, 2 * 192, BF16)
    kc.reg('pbuf', 33792, 2 * 256, BF16)
    kc.reg('ptb', 34816, 2 * 256, BF16)
    kc.reg('diag', 35840, 2 * 128, BF16)
    kc.reg('ctxb', 36864, 1024, BF16)
    kc.reg('acol', 39936, 64, F32)

    @block.sync
    def _(sy):
        kc.w(sy, 'st')
        kc.inc(sy.dma_start(out=kc.t['mskaS'][:, 0:192], in_=h['mska'].ap()[0, :, :]), 'ds', 16)
        kc.inc(sy.dma_start(out=kc.t['mskaS'][:, 192:384], in_=h['mska'].ap()[1, :, :]), 'ds', 16)

    SC = [1024, 1536]
    PT = [1216, 1728]
    for s in range(8):
        for qt in range(2):
            inst = 2 * s + qt
            mmpt = {}
            avmm = {}
            evpt = {}
            for hd in range(16):
                reg = hd % 2

                @block.tensor
                def _(tn):
                    if inst == 0 and hd == 0:
                        kc.w(tn, 'ss')
                    if hd >= 2:
                        kc.w(tn, 'ss', evpt[hd - 2])
                    elif inst > 0:
                        kc.w(tn, 'ss')
                    oc, rr = hd // 2, 64 * (hd % 2)
                    a0 = 320 * s + 128 * qt
                    q0 = 2048 * oc + 256 * s + 128 * qt
                    tn.matmul(out=psA[:, SC[reg]:SC[reg] + 192],
                              lhsT=kc.t['qfull'][rr:rr + 64, q0:q0 + 128],
                              rhs=kc.t['kfm'][rr:rr + 64, 2560 * oc + a0:2560 * oc + a0 + 192],
                              start=True, stop=False)
                    ins = tn.matmul(out=psA[:, SC[reg]:SC[reg] + 192],
                                    lhsT=kc.t['idnS'][:, :],
                                    rhs=kc.t['mskaS'][:, 192 * qt:192 * (qt + 1)],
                                    start=False, stop=True)
                    kc.inc(ins, 'st')

                @block.vector
                def _(v):
                    kc.w(v, 'st')
                    ins = v.tensor_reduce(out=kc.t['acol'][:, hd:hd + 1],
                                          in_=psA[:, SC[reg]:SC[reg] + 192],
                                          axis=AX.X, op=ALU.min)
                    kc.inc(ins, 'sv')

                @block.scalar
                def _(sc):
                    kc.w(sc, 'sv')
                    if hd >= 2:
                        kc.w(sc, 'st', mmpt[hd - 2])
                    ins = sc.activation(out=kc.t['pbuf'][:, 256 * reg:256 * reg + 192],
                                        in_=psA[:, SC[reg]:SC[reg] + 192],
                                        func=AF.Exp, scale=-1.0, bias=kc.t['acol'][:, hd:hd + 1])
                    kc.inc(ins, 'ss')

                @block.vector
                def _(v):
                    kc.w(v, 'ss')
                    ins = v.reduce_sum(out=kc.t['acol'][:, 16 + hd:17 + hd],
                                       in_=kc.t['pbuf'][:, 256 * reg:256 * reg + 192], axis=AX.X)
                    kc.inc(ins, 'sv')
                    kc.w(v, 'sv')
                    ins = v.reciprocal(out=kc.t['acol'][:, 32 + hd:33 + hd],
                                       in_=kc.t['acol'][:, 16 + hd:17 + hd])
                    kc.inc(ins, 'sv')
                    kc.w(v, 'sv')
                    if hd >= 2:
                        kc.w(v, 'st', mmpt[hd - 2])
                    ins = v.tensor_scalar(out=kc.t['diag'][:, 128 * reg:128 * (reg + 1)],
                                          in0=kc.t['idnS'][:, :],
                                          scalar1=kc.t['acol'][:, 32 + hd:33 + hd],
                                          scalar2=None, op0=ALU.mult)
                    kc.inc(ins, 'sv')

                @block.tensor
                def _(tn):
                    kc.w(tn, 'sv')
                    rB = 64 * (s % 2) if qt == 1 else 0
                    tn.matmul(out=psA[:, PT[reg]:PT[reg] + 128],
                              lhsT=kc.t['pbuf'][:, 256 * reg:256 * reg + 128],
                              rhs=kc.t['diag'][:, 128 * reg:128 * (reg + 1)],
                              start=True, stop=True)
                    ins = tn.matmul(out=psA[rB:rB + 64, PT[reg] + 128:PT[reg] + 256],
                                    lhsT=kc.t['pbuf'][:, 256 * reg + 128:256 * reg + 192],
                                    rhs=kc.t['diag'][:, 128 * reg:128 * (reg + 1)],
                                    start=True, stop=True, tile_position=(0, rB))
                    mmpt[hd] = kc.inc(ins, 'st')

                @block.scalar
                def _(sc):
                    kc.w(sc, 'st', mmpt[hd])
                    if hd >= 2:
                        kc.w(sc, 'st', avmm[hd - 2])
                    rB = 64 * (s % 2) if qt == 1 else 0
                    sc.activation(out=kc.t['ptb'][:, 256 * reg:256 * reg + 128],
                                  in_=psA[:, PT[reg]:PT[reg] + 128], func=AF.Relu)
                    ins = sc.activation(out=kc.t['ptb'][rB:rB + 64, 256 * reg + 128:256 * (reg + 1)],
                                        in_=psA[rB:rB + 64, PT[reg] + 128:PT[reg] + 256],
                                        func=AF.Relu)
                    evpt[hd] = kc.inc(ins, 'ss')

                @block.tensor
                def _(tn):
                    kc.w(tn, 'ss', evpt[hd])
                    oc, rr = hd // 2, 64 * (hd % 2)
                    vA = _vt(kc, s, qt)
                    vB = _vt(kc, s, qt + 1)
                    tn.matmul(out=psB[rr:rr + 64, 128 * oc:128 * (oc + 1)],
                              lhsT=vA[:, 64 * hd:64 * (hd + 1)],
                              rhs=kc.t['ptb'][:, 256 * reg:256 * reg + 128],
                              start=True, stop=False)
                    rB = 64 * (s % 2) if qt == 1 else 0
                    lhsTB = vB[0:64, 64 * hd:64 * (hd + 1)] if qt == 0 else vB[:, 64 * hd:64 * (hd + 1)]
                    ins = tn.matmul(out=psB[rr:rr + 64, 128 * oc:128 * (oc + 1)],
                                    lhsT=lhsTB,
                                    rhs=kc.t['ptb'][rB:rB + 64, 256 * reg + 128:256 * (reg + 1)],
                                    start=False, stop=True, tile_position=(rB, rr))
                    avmm[hd] = kc.inc(ins, 'st')

            @block.vector
            def _(v):
                kc.w(v, 'st')
                ins = v.tensor_copy(out=kc.t['ctxb'][:, :], in_=psB[:, 0:1024])
                kc.inc(ins, 'sv')

            @block.tensor
            def _(tn):
                kc.w(tn, 'sv')
                if inst == 0:
                    kc.w(tn, 'dq', ld_wo)
                for c in range(8):
                    for half in range(2):
                        ins = tn.matmul(out=psB[:, 1024 + 512 * half:1024 + 512 * (half + 1)],
                                        lhsT=kc.t['ctxb'][:, 128 * c:128 * (c + 1)],
                                        rhs=kc.t['wbuf'][:, 1024 * c + 512 * half:
                                                         1024 * c + 512 * (half + 1)],
                                        start=(c == 0), stop=(c == 7))
                kc.inc(ins, 'st')

            @block.vector
            def _(v):
                kc.w(v, 'st')
                t = 2 * s + qt
                ins = v.scalar_tensor_tensor(out=kc.t['xsb'][:, 1024 * t:1024 * (t + 1)],
                                             in0=psB[:, 1024:2048], scalar=1.0,
                                             in1=kc.t['xsb'][:, 1024 * t:1024 * (t + 1)],
                                             op0=ALU.mult, op1=ALU.add)
                kc.inc(ins, 'sv')


def _dbg_dump(kc, name, cols=16384):
    block = kc.block

    @block.vector
    def _(v):
        kc.w(v, 'st'); kc.w(v, 'ss'); kc.w(v, 'sg'); kc.w(v, 'dq'); kc.w(v, 'ds')
        ins = v.tensor_copy(out=kc.t['xsb'][:, 0:cols], in_=kc.t[name][:, 0:cols])
        kc.inc(ins, 'sv')


def run_ffn(kc, h, tag):
    block = kc.block
    dbg = getattr(kc, 'dbg', None)
    psA, psB = kc.t['psA'], kc.t['psB']
    dram = kc.dram
    UTM, UFM2, SQ2, UT2 = 32768, 65536, 98304, 102400
    WIN, QKV, TAT, CTXF = 106496, 118784, 135168, 147456
    XFM = 65536
    MQ, MK = 180224, 184320
    W1U, W2U, TMPY = 180224, 184320, 188416
    M0 = 192512
    kc.reg('utm', UTM, 16 * 1024, BF16)
    kc.reg('ufm2', UFM2, 8 * 2048, BF16)
    kc.reg('sqd2', SQ2, 1024, F32)
    kc.reg('winb', WIN, 2 * 3 * 1024, BF16)
    kc.reg('qj', QKV, 2048, BF16)
    kc.reg('kj', QKV + 4096, 2048, BF16)
    kc.reg('vj', QKV + 8192, 2048, BF16)
    kc.reg('vtj', QKV + 12288, 2048, BF16)
    kc.reg('Ajh', TAT, 1024, BF16)
    kc.reg('Atb', TAT + 2048, 1024, BF16)
    kc.reg('diags', TAT + 4096, 8 * 128, BF16)
    kc.reg('ctxf', CTXF, 8 * 2048, BF16)
    kc.reg('xfm', XFM, 8 * 2048, BF16)
    kc.reg('mqs', MQ, 2048, BF16, rows=17)
    kc.reg('mks', MK, 2048, BF16, rows=17)
    kc.reg('woutb', WIN, 8 * 1024, BF16)
    kc.reg('g2B', M0, 1024, BF16)
    kc.reg('b2B', M0 + 2048, 1024, BF16)
    kc.reg('b2a', M0 + 4096, 1024, BF16, rows=8)
    kc.reg('b1cs', M0 + 6144, 256, F32)
    kc.reg('bincs', M0 + 7168, 24, F32)
    kc.reg('binq', M0 + 7296, 8, F32)
    kc.reg('g1cs', M0 + 7360, 8, F32)
    kc.reg('b1ls', M0 + 7424, 8, F32)
    kc.reg('gwTs', M0 + 7488, 64, BF16)
    kc.reg('glog', M0 + 7680, 128, F32)
    kc.reg('comb', M0 + 8192, 128, F32)
    kc.reg('combT', M0 + 8704, 128, BF16, rows=8)
    kc.reg('outbr', M0 + 8960, 1024, BF16, rows=1)
    kc.reg('gbr', M0 + 11008, 8, BF16, rows=1)
    kc.reg('combB', M0 + 11072, 128, BF16)
    kc.reg('w2f', 106496, 32 * 1024, BF16)
    kc.reg('w1u', 172032, 2 * 2048, BF16)
    kc.reg('tmpy', 188416, 1024, F32)
    kc.reg('mtile', 106496, 2 * 1024, F32)
    kc.reg('t2all', 114688, 16 * 1024, F32)

    def hview(j):
        if j < 8:
            off = 4096 * j
        elif j < 16:
            off = 32768 + 4096 * (j - 8)
        elif j < 26:
            off = 106496 + 4096 * (j - 16)
        else:
            off = 147456 + 4096 * (j - 26)
        return kc.arena[0:128, off // 2:off // 2 + 2048]

    @block.sync
    def _(sy):
        kc.w(sy, 'st')
        kc.w(sy, 'sv')
        kc.w(sy, 'ss')
        for name, hh2 in [('mqs', 'mq'), ('mks', 'mk'), ('b1cs', 'b1c'), ('bincs', 'binc'),
                          ('g1cs', 'g1c'), ('b1ls', 'b1lc')]:
            kc.inc(sy.dma_start(out=kc.t[name][:, :], in_=h[hh2].ap()[:, :]), 'ds', 16)
    ld_c1 = kc.n['ds']

    @block.gpsimd
    def _(g):
        kc.w(g, 'st')
        kc.w(g, 'sv')
        for name, hh2 in [('g2B', 'g2B'), ('b2B', 'b2B'), ('b2a', 'b2a'),
                          ('outbr', 'outb_row'), ('gbr', 'gb_row')]:
            kc.inc(g.dma_start(out=kc.t[name][:, :], in_=h[hh2].ap()[:, :]), 'dq', 16)
        kc.inc(g.dma_start(out=kc.t['gwTs'][:, :].rearrange("p (c e) -> p c e", e=8),
                           in_=h['gwT'].ap().rearrange("(c p) e -> p c e", p=128)), 'dq', 16)
        for oc in (0, 8, 16):
            kc.inc(g.dma_start(
                out=kc.t['winb'][:, 1024 * (oc // 8):1024 * (oc // 8 + 1)]
                    .rearrange("p (c f) -> p c f", f=128),
                in_=h['winU'].ap()[oc, :, :].rearrange("(c p) f -> p c f", p=128)), 'dq', 16)
    ld_u0 = kc.n['dq']

    @block.vector
    def _(v):
        kc.w(v, 'ds', ld_c1)
        ins = v.tensor_scalar(out=kc.t['binq'][:, :], in0=kc.t['bincs'][:, 0:8],
                              scalar1=-0.125, scalar2=None, op0=ALU.mult)
        kc.inc(ins, 'sv')

    kc.scp = kc.nc.enter_named_scope('ffn_ln', False)[0]; kc.scpn = 'ffn_ln'
    M1, R1 = emit_ln_stats(kc, 'xsb', list(range(NT)), 0, 'sqd2')
    evs = []
    for t in range(NT):
        @block.vector
        def _(v):
            kc.w(v, 'ss')
            ins = v.tensor_scalar(out=kc.t['utm'][:, 1024 * t:1024 * (t + 1)],
                                  in0=kc.t['xsb'][:, 1024 * t:1024 * (t + 1)],
                                  scalar1=kc.t['stats'][:, M1 + t:M1 + t + 1],
                                  scalar2=kc.t['stats'][:, R1 + t:R1 + t + 1],
                                  op0=ALU.subtract, op1=ALU.mult)
            kc.inc(ins, 'sv')

        @block.tensor
        def _(tn):
            kc.w(tn, 'sv')
            if t >= 2:
                kc.w(tn, 'sv', evs[t - 2])
            for f in range(8):
                ins = tn.matmul(out=psA[:, 1024 * (t % 2) + 128 * f:1024 * (t % 2) + 128 * (f + 1)],
                                lhsT=kc.t['utm'][:, 1024 * t + 128 * f:1024 * t + 128 * (f + 1)],
                                rhs=kc.t['idnS'][:, :], start=True, stop=True)
            kc.inc(ins, 'st')

        @block.vector
        def _(v):
            kc.w(v, 'st')
            ov = kc.t['ufm2'][:, :].rearrange("p (f a) -> p f a", a=2048)[:, :, 128 * t:128 * (t + 1)]
            iv = psA[:, 1024 * (t % 2):1024 * (t % 2 + 1)].rearrange("p (f a) -> p f a", a=128)
            ins = v.tensor_copy(out=ov, in_=iv)
            kc.inc(ins, 'sv')
        evs.append(kc.n['sv'])
    ufm2_done = kc.n['sv']
    kc.nc.leave_named_scope(kc.scpn, kc.scp, False)
    kc.scp = kc.nc.enter_named_scope('ffn_inp_attn', False)[0]; kc.scpn = 'ffn_inp_attn'
    if dbg == 'src':
        return _dbg_dump(kc, 'utm')

    for j in range(8):
        if j < 7:
            @block.gpsimd
            def _(g):
                kc.w(g, 'st')
                for pl, oc in enumerate((j + 1, 9 + j, 17 + j)):
                    kc.inc(g.dma_start(
                        out=kc.t['winb'][:, 3072 * ((j + 1) % 2) + 1024 * pl:
                                         3072 * ((j + 1) % 2) + 1024 * (pl + 1)]
                            .rearrange("p (c f) -> p c f", f=128),
                        in_=h['winU'].ap()[oc, :, :].rearrange("(c p) f -> p c f", p=128)), 'dq', 16)

        u2 = kc.t['ufm2']
        for which, dst, pX, scale in [(0, 'qj', psA, -0.125), (1, 'kj', psB, 1.0), (2, 'vj', psA, 1.0)]:
            @block.tensor
            def _(tn):
                if j == 0 and which == 0:
                    kc.w(tn, 'dq', ld_u0)
                    kc.w(tn, 'sv', ufm2_done)
                elif which == 0 and j > 0:
                    kc.w(tn, 'dq')
                kc.w(tn, 'ss')
                plane = 3072 * (j % 2) + 1024 * which
                for c in range(8):
                    for g4 in range(4):
                        ins = tn.matmul(out=pX[:, 512 * g4:512 * (g4 + 1)],
                                        lhsT=kc.t['winb'][:, plane + 128 * c:plane + 128 * (c + 1)],
                                        rhs=u2[:, 2048 * c + 512 * g4:2048 * c + 512 * (g4 + 1)],
                                        start=(c == 0), stop=(c == 7))
                kc.inc(ins, 'st')

            @block.vector
            def _(v):
                kc.w(v, 'st')
                kc.w(v, 'ss')
                bias_ap = kc.t['binq'][:, j:j + 1] if which == 0 else \
                    kc.t['bincs'][:, 8 * which + j:8 * which + j + 1]
                iv = pX[:, 0:2048].rearrange("p (s T c) -> p s T c", s=8, c=16)
                ov = kc.t[dst][:, :].rearrange("p (T s c) -> p s T c", T=16, c=16)
                ins = v.tensor_scalar(out=ov, in0=iv, scalar1=bias_ap, scalar2=scale,
                                      op0=ALU.add, op1=ALU.mult)
                kc.inc(ins, 'ss')

        @block.tensor
        def _(tn):
            kc.w(tn, 'ss')
            for T in range(16):
                ins = tn.matmul(out=psB[:, 128 * T:128 * (T + 1)],
                                lhsT=kc.t['vj'][:, 128 * T:128 * (T + 1)],
                                rhs=kc.t['idnS'][:, :], start=True, stop=True)
            kc.inc(ins, 'st')

        @block.vector
        def _(v):
            kc.w(v, 'st')
            ins = v.tensor_copy(out=kc.t['vtj'][:, :], in_=psB[:, 0:2048])
            kc.inc(ins, 'sv')

        for H2 in range(2):
            for hh in range(2):
                rr = 64 * hh

                @block.tensor
                def _(tn):
                    kc.w(tn, 'sv')
                    kc.w(tn, 'ss')
                    for Tl in range(8):
                        T = 8 * H2 + Tl
                        tn.matmul(out=psA[:, 128 * Tl:128 * (Tl + 1)],
                                  lhsT=kc.t['mqs'][:, 128 * T:128 * (T + 1)],
                                  rhs=kc.t['mks'][:, 128 * T:128 * (T + 1)], start=True, stop=False)
                        ins = tn.matmul(out=psA[:, 128 * Tl:128 * (Tl + 1)],
                                        lhsT=kc.t['qj'][rr:rr + 64, 128 * T:128 * (T + 1)],
                                        rhs=kc.t['kj'][rr:rr + 64, 128 * T:128 * (T + 1)],
                                        start=False, stop=True)
                    kc.inc(ins, 'st')

                @block.vector
                def _(v):
                    kc.w(v, 'st')
                    ins = v.tensor_reduce(out=kc.t['stats'][:, 120:121], in_=psA[:, 0:1024],
                                          axis=AX.X, op=ALU.min)
                    kc.inc(ins, 'sv')

                @block.scalar
                def _(sc):
                    kc.w(sc, 'sv')
                    ins = sc.activation(out=kc.t['Ajh'][:, 0:1024], in_=psA[:, 0:1024],
                                        func=AF.Exp, scale=-1.0, bias=kc.t['stats'][:, 120:121])
                    kc.inc(ins, 'ss')

                @block.vector
                def _(v):
                    kc.w(v, 'ss')
                    ins = v.reduce_sum(out=kc.t['stats'][:, 104:112],
                                       in_=kc.t['Ajh'][:, 0:1024].rearrange("p (T c) -> p T c", c=128),
                                       axis=AX.X)
                    kc.inc(ins, 'sv')
                    kc.w(v, 'sv')
                    ins = v.reciprocal(out=kc.t['stats'][:, 112:120], in_=kc.t['stats'][:, 104:112])
                    kc.inc(ins, 'sv')
                    kc.w(v, 'sv')
                    for Tl in range(8):
                        ins = v.tensor_scalar(out=kc.t['diags'][:, 128 * Tl:128 * (Tl + 1)],
                                              in0=kc.t['idnS'][:, :],
                                              scalar1=kc.t['stats'][:, 112 + Tl:113 + Tl],
                                              scalar2=None, op0=ALU.mult)
                    kc.inc(ins, 'sv')

                @block.tensor
                def _(tn):
                    kc.w(tn, 'sv')
                    for Tl in range(8):
                        ins = tn.matmul(out=psB[:, 128 * Tl:128 * (Tl + 1)],
                                        lhsT=kc.t['Ajh'][:, 128 * Tl:128 * (Tl + 1)],
                                        rhs=kc.t['diags'][:, 128 * Tl:128 * (Tl + 1)],
                                        start=True, stop=True)
                    kc.inc(ins, 'st')

                @block.scalar
                def _(sc):
                    kc.w(sc, 'st')
                    ins = sc.activation(out=kc.t['Atb'][:, 0:1024], in_=psB[:, 0:1024], func=AF.Relu)
                    kc.inc(ins, 'ss')

                @block.tensor
                def _(tn):
                    kc.w(tn, 'ss')
                    for Tl in range(8):
                        T = 8 * H2 + Tl
                        ins = tn.matmul(out=psB[rr:rr + 64, 1024 + 128 * Tl:1024 + 128 * (Tl + 1)],
                                        lhsT=kc.t['vtj'][:, 128 * T + 64 * hh:128 * T + 64 * (hh + 1)],
                                        rhs=kc.t['Atb'][:, 128 * Tl:128 * (Tl + 1)],
                                        start=True, stop=True, tile_position=(0, rr))
                    kc.inc(ins, 'st')

            @block.vector
            def _(v):
                kc.w(v, 'st')
                iv = psB[:, 1024:2048].rearrange("p (T s c) -> p T s c", T=8, c=16)
                ov = kc.t['ctxf'][:, 2048 * j:2048 * (j + 1)] \
                    .rearrange("p (s h T c) -> p h T s c", s=8, h=2, c=16)[:, H2]
                ins = v.tensor_copy(out=ov, in_=iv)
                kc.inc(ins, 'sv')
    ctx_done = kc.n['sv']
    kc.nc.leave_named_scope(kc.scpn, kc.scp, False)
    kc.scp = kc.nc.enter_named_scope('ffn_outproj_ln1_gate', False)[0]; kc.scpn = 'ffn_outproj_ln1_gate'
    if dbg == 'ctx':
        return _dbg_dump(kc, 'ctxf')
    if dbg == 'qkv':
        return _dbg_dump(kc, 'qj', 2048)

    @block.gpsimd
    def _(g):
        kc.w(g, 'st')
        kc.inc(g.dma_start(out=kc.t['woutb'][:, :].rearrange("p (c f) -> p c f", f=1024),
                           in_=h['woutT'].ap().rearrange("(c p) f -> p c f", p=128)), 'dq', 16)
    ld_wo2 = kc.n['dq']

    ev_o = []
    for t in range(NT):
        @block.tensor
        def _(tn):
            if t == 0:
                kc.w(tn, 'dq', ld_wo2)
                kc.w(tn, 'sv', ctx_done)
            if t >= 2:
                kc.w(tn, 'sv', ev_o[t - 2])
            reg = t % 2
            for c in range(8):
                for half in range(2):
                    tn.matmul(out=psA[:, 1024 * reg + 512 * half:1024 * reg + 512 * (half + 1)],
                              lhsT=kc.t['ctxf'][:, 2048 * c + 128 * t:2048 * c + 128 * (t + 1)],
                              rhs=kc.t['woutb'][:, 1024 * c + 512 * half:1024 * c + 512 * (half + 1)],
                              start=(c == 0), stop=False)
            for half in range(2):
                ins = tn.matmul(out=psA[:, 1024 * reg + 512 * half:1024 * reg + 512 * (half + 1)],
                                lhsT=kc.t['ones_row'][:, :],
                                rhs=kc.t['outbr'][:, 512 * half:512 * (half + 1)],
                                start=False, stop=True)
            kc.inc(ins, 'st')

        @block.vector
        def _(v):
            kc.w(v, 'st')
            reg = t % 2
            ins = v.scalar_tensor_tensor(out=kc.t['utm'][:, 1024 * t:1024 * (t + 1)],
                                         in0=psA[:, 1024 * reg:1024 * (reg + 1)], scalar=1.0,
                                         in1=kc.t['utm'][:, 1024 * t:1024 * (t + 1)],
                                         op0=ALU.mult, op1=ALU.add)
            kc.inc(ins, 'sv')
        ev_o.append(kc.n['sv'])

    if dbg == 's1':
        return _dbg_dump(kc, 'utm')
    M2c, R2c = emit_ln_stats(kc, 'utm', list(range(NT)), 0, 'sqd2')
    kc.reg('utmp2', UT2, 2048, BF16)
    ev_x = []
    for t in range(NT):
        @block.vector
        def _(v):
            kc.w(v, 'ss')
            if t >= 2:
                kc.w(v, 'st', ev_x[t - 2][0])
            ins = v.tensor_scalar(out=kc.t['utmp2'][:, 1024 * (t % 2):1024 * (t % 2 + 1)],
                                  in0=kc.t['utm'][:, 1024 * t:1024 * (t + 1)],
                                  scalar1=kc.t['stats'][:, M2c + t:M2c + t + 1],
                                  scalar2=kc.t['stats'][:, R2c + t:R2c + t + 1],
                                  op0=ALU.subtract, op1=ALU.mult)
            kc.inc(ins, 'sv')

        @block.tensor
        def _(tn):
            kc.w(tn, 'sv')
            if t >= 2:
                kc.w(tn, 'sv', ev_x[t - 2][1])
            for f in range(8):
                ins = tn.matmul(out=psB[:, 1024 * (t % 2) + 128 * f:1024 * (t % 2) + 128 * (f + 1)],
                                lhsT=kc.t['utmp2'][:, 1024 * (t % 2) + 128 * f:
                                                   1024 * (t % 2) + 128 * (f + 1)],
                                rhs=kc.t['idnS'][:, :], start=True, stop=True)
            kc.inc(ins, 'st')
        st_t = kc.n['st']

        @block.vector
        def _(v):
            kc.w(v, 'st')
            for f in range(8):
                ins = v.tensor_scalar(
                    out=kc.t['xfm'][:, :].rearrange("p (f a) -> p f a", a=2048)[
                        :, f, 128 * t:128 * (t + 1)],
                    in0=psB[:, 1024 * (t % 2) + 128 * f:1024 * (t % 2) + 128 * (f + 1)],
                    scalar1=kc.t['g1cs'][:, f:f + 1], scalar2=kc.t['b1ls'][:, f:f + 1],
                    op0=ALU.mult, op1=ALU.add)
            kc.inc(ins, 'sv')
        ev_x.append((st_t, kc.n['sv']))
    xfm_done = kc.n['sv']

    if dbg == 'xin':
        return _dbg_dump(kc, 'xfm')
    for t in range(NT):
        @block.tensor
        def _(tn):
            if t == 0:
                kc.w(tn, 'sv', xfm_done)
                kc.w(tn, 'dq')
            if t >= 4:
                kc.w(tn, 'sv')
            for c in range(8):
                ins = tn.matmul(out=psA[:, 8 * t:8 * (t + 1)],
                                lhsT=kc.t['xfm'][:, 2048 * c + 128 * t:2048 * c + 128 * (t + 1)],
                                rhs=kc.t['gwTs'][:, 8 * c:8 * (c + 1)],
                                start=(c == 0), stop=False)
            ins = tn.matmul(out=psA[:, 8 * t:8 * (t + 1)],
                            lhsT=kc.t['ones_row'][:, :], rhs=kc.t['gbr'][:, :],
                            start=False, stop=True)
            kc.inc(ins, 'st')

    @block.vector
    def _(v):
        kc.w(v, 'st')
        ins = v.tensor_copy(out=kc.t['glog'][:, :], in_=psA[:, 0:128])
        kc.inc(ins, 'sv')
        kc.w(v, 'sv')
        gl = kc.t['glog'][:, :].rearrange("p (T e) -> p T e", e=8)
        ins = v.reduce_max(out=kc.t['stats'][:, 0:16], in_=gl, axis=AX.X)
        kc.inc(ins, 'sv')
        kc.w(v, 'sv')
        m1b = kc.t['stats'][:, 0:16].unsqueeze(2).broadcast_to((128, 16, 8))
        ins = v.tensor_tensor(out=kc.t['comb'][:, :].rearrange("p (T e) -> p T e", e=8),
                              in0=gl, in1=m1b, op=ALU.is_equal)
        kc.inc(ins, 'sv')
        kc.w(v, 'sv')
        ins = v.scalar_tensor_tensor(out=kc.t['sqd2'][:, 0:128].rearrange("p (T e) -> p T e", e=8),
                                     in0=kc.t['comb'][:, :].rearrange("p (T e) -> p T e", e=8),
                                     scalar=-30000.0, in1=gl, op0=ALU.mult, op1=ALU.add)
        kc.inc(ins, 'sv')
        kc.w(v, 'sv')
        ins = v.reduce_max(out=kc.t['stats'][:, 16:32],
                           in_=kc.t['sqd2'][:, 0:128].rearrange("p (T e) -> p T e", e=8),
                           axis=AX.X)
        kc.inc(ins, 'sv')
        kc.w(v, 'sv')
        m1b2 = kc.t['stats'][:, 0:16].unsqueeze(2).broadcast_to((128, 16, 8))
        ins = v.tensor_tensor(out=kc.t['sqd2'][:, 0:128].rearrange("p (T e) -> p T e", e=8),
                              in0=gl, in1=m1b2, op=ALU.subtract)
        kc.inc(ins, 'sv')

    @block.scalar
    def _(sc):
        kc.w(sc, 'sv')
        ins = sc.activation(out=kc.t['sqd2'][:, 0:128], in_=kc.t['sqd2'][:, 0:128], func=AF.Exp)
        kc.inc(ins, 'ss')

    @block.vector
    def _(v):
        kc.w(v, 'ss')
        gl = kc.t['glog'][:, :].rearrange("p (T e) -> p T e", e=8)
        m2b = kc.t['stats'][:, 16:32].unsqueeze(2).broadcast_to((128, 16, 8))
        ins = v.tensor_tensor(out=kc.t['comb'][:, :].rearrange("p (T e) -> p T e", e=8),
                              in0=gl, in1=m2b, op=ALU.is_ge)
        kc.inc(ins, 'sv')
        kc.w(v, 'sv')
        ins = v.tensor_tensor(out=kc.t['comb'][:, :], in0=kc.t['comb'][:, :],
                              in1=kc.t['sqd2'][:, 0:128], op=ALU.mult)
        kc.inc(ins, 'sv')
        kc.w(v, 'sv')
        ins = v.reduce_sum(out=kc.t['stats'][:, 32:48],
                           in_=kc.t['comb'][:, :].rearrange("p (T e) -> p T e", e=8), axis=AX.X)
        kc.inc(ins, 'sv')
        kc.w(v, 'sv')
        ins = v.reciprocal(out=kc.t['stats'][:, 48:64], in_=kc.t['stats'][:, 32:48])
        kc.inc(ins, 'sv')
        kc.w(v, 'sv')
        rb = kc.t['stats'][:, 48:64].unsqueeze(2).broadcast_to((128, 16, 8))
        ins = v.tensor_tensor(out=kc.t['comb'][:, :].rearrange("p (T e) -> p T e", e=8),
                              in0=kc.t['comb'][:, :].rearrange("p (T e) -> p T e", e=8),
                              in1=rb, op=ALU.mult)
        kc.inc(ins, 'sv')
    comb_done = kc.n['sv']

    if dbg == 'comb':
        return _dbg_dump(kc, 'comb', 128)
    @block.vector
    def _(v):
        kc.w(v, 'sv', comb_done)
        ins = v.tensor_copy(out=kc.t['combB'][:, :], in_=kc.t['comb'][:, :])
        kc.inc(ins, 'sv')
    combB_done = kc.n['sv']

    ev_b2 = []
    tmpy_dq = []
    for t in range(NT):
        @block.tensor
        def _(tn):
            kc.w(tn, 'sv', combB_done)
            if t >= 2:
                kc.w(tn, 'ss', ev_b2[t - 2])
            ins = tn.matmul(out=psB[0:8, 128 * (t % 2):128 * (t % 2) + 128],
                            lhsT=kc.t['combB'][:, 8 * t:8 * (t + 1)],
                            rhs=kc.t['idnS'][:, :], start=True, stop=True)
            kc.inc(ins, 'st')

        @block.vector
        def _(v):
            kc.w(v, 'st')
            ins = v.tensor_copy(out=kc.t['combT'][:, :],
                                in_=psB[0:8, 128 * (t % 2):128 * (t % 2) + 128])
            kc.inc(ins, 'sv')

        @block.tensor
        def _(tn):
            kc.w(tn, 'sv')
            for half in range(2):
                ins = tn.matmul(out=psA[:, 1024 * (t % 2) + 512 * half:
                                        1024 * (t % 2) + 512 * (half + 1)],
                                lhsT=kc.t['combT'][:, :],
                                rhs=kc.t['b2a'][:, 512 * half:512 * (half + 1)],
                                start=True, stop=True)
            kc.inc(ins, 'st')

        @block.vector
        def _(v):
            kc.w(v, 'st')
            if t >= 1:
                kc.w(v, 'dq', tmpy_dq[t - 1])
            ins = v.tensor_copy(out=kc.t['tmpy'][:, 0:1024],
                                in_=psA[:, 1024 * (t % 2):1024 * (t % 2 + 1)])
            ev_b2.append(kc.inc(ins, 'ss'))

        @block.gpsimd
        def _(g):
            kc.w(g, 'ss')
            kc.inc(g.dma_start(out=dram['moeacc'].ap()[:, 1024 * t:1024 * (t + 1)],
                               in_=kc.t['tmpy'][:, 0:1024]), 'dq', 16)
        tmpy_dq.append(kc.n['dq'])

    kc.nc.leave_named_scope(kc.scpn, kc.scp, False)
    kc.scp = kc.nc.enter_named_scope('ffn_moe', False)[0]; kc.scpn = 'ffn_moe'
    @block.sync
    def _(sy):
        kc.w(sy, 'sv')
        kc.inc(sy.dma_start(out=dram['spill'].ap()[:, :], in_=kc.t['xsb'][:, :]), 'ds', 16)
    spill_done = kc.n['ds']

    def hview(j):
        return kc.arena[0:128, (2048 * j) // 2:(2048 * j) // 2 + 1024]

    w1_consumed = {}
    w1_ld = {}
    hev = {}
    yev = {}
    w2_ld = {}
    y_done = {}
    last_tmpy_dq = [tmpy_dq[-1]]
    PSH = [psA, psB]

    def hreg(j):
        t_ = PSH[(j % 4) // 2]
        off = 1024 * (j % 2)
        return t_[:, off:off + 1024]

    for e in range(8):
        for th in range(2):
            for u in range(16):
                @block.gpsimd
                def _(g):
                    prev = (e, th, u - 2) if u >= 2 else \
                        ((e, th - 1, 14 + u) if th > 0 else
                         ((e - 1, 1, 14 + u) if e > 0 else None))
                    if prev is not None and prev in w1_consumed:
                        kc.w(g, 'st', w1_consumed[prev])
                    elif e == 0 and th == 0 and u == 0:
                        kc.w(g, 'ds', spill_done)
                        kc.w(g, 'sv')
                        kc.w(g, 'st')
                    kc.inc(g.dma_start(
                        out=kc.t['w1u'][:, 2048 * (u % 2):2048 * (u % 2 + 1)]
                            .rearrange("p (j x) -> p j x", j=2),
                        in_=h['w1U'].ap()[e, 2 * u:2 * u + 2, :, :]
                            .rearrange("j (c p) f -> p j c f", p=128)), 'dq', 16)
                    w1_ld[(e, th, u)] = kc.n['dq']
                    if th == 0:
                        if e == 0 and u == 0:
                            kc.w(g, 'st')
                        elif u == 0:
                            kc.w(g, 'st', y_done[e - 1])
                        kc.inc(g.dma_start(
                            out=kc.t['w2f'][:, 2048 * u:2048 * (u + 1)]
                                .rearrange("p (c f) -> p c f", f=1024),
                            in_=h['w2T'].ap()[e, 256 * u:256 * (u + 1), :]
                                .rearrange("(c p) f -> p c f", p=128)), 'dq', 16)
                        w2_ld[(e, u)] = kc.n['dq']

                for jj in range(2):
                    j = 2 * u + jj

                    @block.tensor
                    def _(tn):
                        kc.w(tn, 'dq', w1_ld[(e, th, u)])
                        if e == 0 and th == 0 and j == 0:
                            kc.w(tn, 'ds', spill_done)
                            kc.w(tn, 'sv')
                            kc.w(tn, 'ss')
                        prev4 = (th, j - 4) if j >= 4 else ((th - 1, 28 + j) if th > 0 else None)
                        if prev4 is not None and prev4 in hev:
                            kc.w(tn, 'ss', hev[prev4])
                        pj = hreg(j)
                        for c in range(8):
                            for g4 in range(2):
                                ins = tn.matmul(
                                    out=pj[:, 512 * g4:512 * (g4 + 1)],
                                    lhsT=kc.t['w1u'][:, 2048 * (u % 2) + 1024 * jj + 128 * c:
                                                     2048 * (u % 2) + 1024 * jj + 128 * (c + 1)],
                                    rhs=kc.t['xfm'][:, 2048 * c + 1024 * th + 512 * g4:
                                                    2048 * c + 1024 * th + 512 * (g4 + 1)],
                                    start=(c == 0), stop=(c == 7))
                        if jj == 1:
                            w1_consumed[(e, th, u)] = kc.inc(ins, 'st')
                        else:
                            kc.inc(ins, 'st')

                    @block.scalar
                    def _(sc):
                        kc.w(sc, 'st')
                        ins = sc.activation(out=hview(j), in_=hreg(j), func=AF.Relu,
                                            bias=kc.t['b1cs'][:, 32 * e + j:32 * e + j + 1])
                        hev[(th, j)] = kc.inc(ins, 'ss')

            for tl in range(8):
                @block.tensor
                def _(tn):
                    if tl == 0:
                        kc.w(tn, 'ss', hev[(th, 31)])
                        kc.w(tn, 'dq', w2_ld[(e, 15)])
                    slot = tl % 4
                    pt_ = PSH[slot // 2][:, 1024 * (slot % 2):1024 * (slot % 2) + 1024]
                    pidx = (e, th, tl - 4) if tl >= 4 else \
                        ((e, th - 1, 4 + tl) if th > 0 else ((e - 1, 1, 4 + tl) if e > 0 else None))
                    if pidx is not None and pidx in yev:
                        kc.w(tn, 'ss', yev[pidx])
                    for j2 in range(32):
                        for half in range(2):
                            ins = tn.matmul(
                                out=pt_[:, 512 * half:512 * (half + 1)],
                                lhsT=hview(j2)[:, 128 * tl:128 * (tl + 1)],
                                rhs=kc.t['w2f'][:, 1024 * j2 + 512 * half:
                                                1024 * j2 + 512 * (half + 1)],
                                start=(j2 == 0), stop=(j2 == 31))
                    kc.inc(ins, 'st')
                if th == 1 and tl == 7:
                    y_done[e] = kc.n['st']

                @block.vector
                def _(v):
                    kc.w(v, 'st')
                    kc.w(v, 'dq', last_tmpy_dq[0])
                    slot = tl % 4
                    pt_ = PSH[slot // 2][:, 1024 * (slot % 2):1024 * (slot % 2) + 1024]
                    t = 8 * th + tl
                    ins = v.tensor_scalar(out=kc.t['tmpy'][:, 0:1024],
                                          in0=pt_, scalar1=kc.t['comb'][:, 8 * t + e:8 * t + e + 1],
                                          scalar2=None, op0=ALU.mult)
                    yev[(e, th, tl)] = kc.inc(ins, 'ss')

                @block.gpsimd
                def _(g):
                    kc.w(g, 'ss', yev[(e, th, tl)])
                    t = 8 * th + tl
                    kc.inc(g.dma_start(out=dram['moeacc'].ap()[:, 1024 * t:1024 * (t + 1)],
                                       in_=kc.t['tmpy'][:, 0:1024],
                                       accum_op=ALU.add), 'dq', 16)
                last_tmpy_dq[0] = kc.n['dq']
    moe_done_dq = kc.n['dq']

    @block.sync
    def _(sy):
        kc.w(sy, 'st')
        kc.inc(sy.dma_start(out=kc.t['xsb'][:, :], in_=dram['spill'].ap()[:, :]), 'ds', 16)
    unspill_done = kc.n['ds']

    ev_f = []
    for t in range(NT):
        @block.gpsimd
        def _(g):
            if t == 0:
                kc.w(g, 'dq', moe_done_dq)
            if t >= 2:
                kc.w(g, 'sv')
            kc.inc(g.dma_start(out=kc.t['mtile'][:, 1024 * (t % 2):1024 * (t % 2 + 1)],
                               in_=dram['moeacc'].ap()[:, 1024 * t:1024 * (t + 1)]), 'dq', 16)

        @block.tensor
        def _(tn):
            if t == 0:
                kc.w(tn, 'ss')
            if t >= 2:
                kc.w(tn, 'sv', ev_f[t - 2])
            for f in range(8):
                ins = tn.matmul(out=psA[:, 1024 * (t % 2) + 128 * f:1024 * (t % 2) + 128 * (f + 1)],
                                lhsT=kc.t['xfm'][:, 2048 * f + 128 * t:2048 * f + 128 * (t + 1)],
                                rhs=kc.t['idnS'][:, :], start=True, stop=True)
            kc.inc(ins, 'st')

        @block.vector
        def _(v):
            kc.w(v, 'st')
            kc.w(v, 'dq')
            ins = v.scalar_tensor_tensor(out=kc.t['t2all'][:, 1024 * t:1024 * (t + 1)],
                                         in0=psA[:, 1024 * (t % 2):1024 * (t % 2 + 1)], scalar=1.0,
                                         in1=kc.t['mtile'][:, 1024 * (t % 2):1024 * (t % 2 + 1)],
                                         op0=ALU.mult, op1=ALU.add)
            kc.inc(ins, 'sv')
        ev_f.append(kc.n['sv'])

    if dbg == 'moe':
        return _dbg_dump(kc, 't2all')
    M3, R3 = emit_ln_stats(kc, 't2all', list(range(NT)), 0, 'sqd2')
    for t in range(NT):
        @block.vector
        def _(v):
            kc.w(v, 'ss')
            if t == 0:
                kc.w(v, 'ds', unspill_done)
            ins = v.tensor_scalar(out=kc.t['utmp2'][:, 1024 * (t % 2):1024 * (t % 2 + 1)],
                                  in0=kc.t['t2all'][:, 1024 * t:1024 * (t + 1)],
                                  scalar1=kc.t['stats'][:, M3 + t:M3 + t + 1],
                                  scalar2=kc.t['stats'][:, R3 + t:R3 + t + 1],
                                  op0=ALU.subtract, op1=ALU.mult)
            kc.inc(ins, 'sv')
            kc.w(v, 'sv')
            ins = v.tensor_tensor(out=kc.t['utmp2'][:, 1024 * (t % 2):1024 * (t % 2 + 1)],
                                  in0=kc.t['utmp2'][:, 1024 * (t % 2):1024 * (t % 2 + 1)],
                                  in1=kc.t['g2B'][:, :], op=ALU.mult)
            kc.inc(ins, 'sv')
            kc.w(v, 'sv')
            ins = v.tensor_tensor(out=kc.t['utmp2'][:, 1024 * (t % 2):1024 * (t % 2 + 1)],
                                  in0=kc.t['utmp2'][:, 1024 * (t % 2):1024 * (t % 2 + 1)],
                                  in1=kc.t['b2B'][:, :], op=ALU.add)
            kc.inc(ins, 'sv')
            kc.w(v, 'sv')
            ins = v.tensor_tensor(out=kc.t['xsb'][:, 1024 * t:1024 * (t + 1)],
                                  in0=kc.t['xsb'][:, 1024 * t:1024 * (t + 1)],
                                  in1=kc.t['utmp2'][:, 1024 * (t % 2):1024 * (t % 2 + 1)],
                                  op=ALU.add)
            kc.inc(ins, 'sv')
    kc.nc.leave_named_scope(kc.scpn, kc.scp, False)


def run_s3(kc, h, dram):
    block = kc.block
    psA, psB = kc.t['psA'], kc.t['psB']
    kc.reg('u2f', 49152, 8 * 2048, BF16)
    kc.reg('scr3', 197632, 2 * 1024, BF16)
    kc.reg('sqd3', 201728, 1024, F32)
    kc.reg('stg', 32768, 2 * 8 * 512, BF16)
    kc.reg('qfm', 49152, 16384, BF16)
    kc.reg('kfm3', 81920, 16384, BF16)
    kc.reg('vtm3', 114688, 128 * 64, BF16)
    kc.reg('pb3', 131072, 2 * 1280, BF16)
    kc.reg('pt3', 136192, 2 * 1280, BF16)
    kc.reg('dg3', 141312, 2 * 128, BF16)
    kc.reg('ctxall', 141824, 16384, BF16)
    kc.reg('wfb', 174592, 8 * 1024, BF16)
    kc.reg('mcauB', 190976, 128, BF16)
    kc.reg('mantB', 191232, 128, BF16)
    kc.reg('bfr', 191488, 1024, BF16, rows=1)
    kc.reg('wqs', 193536, 8 * 128, BF16)
    kc.reg('wks', 195584, 8 * 128, BF16)
    kc.reg('wvs', 205824, 8 * 64, BF16)
    kc.reg('ctxf3', 49152, 8 * 2048, BF16)

    @block.sync
    def _(sy):
        kc.w(sy, 'st'); kc.w(sy, 'sv'); kc.w(sy, 'ss')
        kc.inc(sy.dma_start(out=kc.t['mcauB'][:, :], in_=h['mcaus'].ap()[:, :]), 'ds', 16)
        kc.inc(sy.dma_start(out=kc.t['mantB'][:, :], in_=h['manti'].ap()[:, :]), 'ds', 16)

    @block.gpsimd
    def _(g):
        kc.w(g, 'st'); kc.w(g, 'sv'); kc.w(g, 'ss')
        kc.inc(g.dma_start(out=kc.t['bfr'][:, :], in_=h['bf_row'].ap()[:, :]), 'dq', 16)
        kc.inc(g.dma_start(out=kc.t['wqs'][:, :].rearrange("p (c f) -> p c f", f=128),
                           in_=h['wqbT'].ap().rearrange("(c p) f -> p c f", p=128)), 'dq', 16)
        kc.inc(g.dma_start(out=kc.t['wks'][:, :].rearrange("p (c f) -> p c f", f=128),
                           in_=h['wkbT2'].ap().rearrange("(c p) f -> p c f", p=128)), 'dq', 16)
        kc.inc(g.dma_start(out=kc.t['wvs'][:, :].rearrange("p (c f) -> p c f", f=64),
                           in_=h['wvbT'].ap().rearrange("(c p) f -> p c f", p=128)), 'dq', 16)
        kc.inc(g.dma_start(out=kc.t['wfb'][:, :].rearrange("p (c f) -> p c f", f=1024),
                           in_=h['wfT'].ap().rearrange("(c p) f -> p c f", p=128)), 'dq', 16)
    ld_s3 = kc.n['dq']

    kc.scp = kc.nc.enter_named_scope('s3_ln_ag', False)[0]; kc.scpn = 's3_ln_ag'
    M1, R1 = emit_ln_stats(kc, 'xsb', list(range(NT)), 0, 'sqd3')
    evs = []
    for t in range(NT):
        @block.vector
        def _(v):
            kc.w(v, 'ss')
            ins = v.tensor_scalar(out=kc.t['scr3'][:, 1024 * (t % 2):1024 * (t % 2 + 1)],
                                  in0=kc.t['xsb'][:, 1024 * t:1024 * (t + 1)],
                                  scalar1=kc.t['stats'][:, M1 + t:M1 + t + 1],
                                  scalar2=kc.t['stats'][:, R1 + t:R1 + t + 1],
                                  op0=ALU.subtract, op1=ALU.mult)
            kc.inc(ins, 'sv')

        @block.tensor
        def _(tn):
            kc.w(tn, 'sv')
            if t >= 2:
                kc.w(tn, 'sv', evs[t - 2])
            for f in range(8):
                ins = tn.matmul(out=psA[:, 1024 * (t % 2) + 128 * f:1024 * (t % 2) + 128 * (f + 1)],
                                lhsT=kc.t['scr3'][:, 1024 * (t % 2) + 128 * f:
                                                  1024 * (t % 2) + 128 * (f + 1)],
                                rhs=kc.t['idnS'][:, :], start=True, stop=True)
            kc.inc(ins, 'st')

        @block.vector
        def _(v):
            kc.w(v, 'st')
            ov = kc.t['u2f'][:, :].rearrange("p (f a) -> p f a", a=2048)[:, :, 128 * t:128 * (t + 1)]
            iv = psA[:, 1024 * (t % 2):1024 * (t % 2 + 1)].rearrange("p (f a) -> p f a", a=128)
            ins = v.tensor_copy(out=ov, in_=iv)
            kc.inc(ins, 'sv')
        evs.append(kc.n['sv'])

    @block.sync
    def _(sy):
        kc.w(sy, 'sv')
        for f in range(8):
            kc.inc(sy.dma_start(out=dram['u2c'].ap()[128 * f:128 * (f + 1), :],
                                in_=kc.t['u2f'][:, 2048 * f:2048 * (f + 1)]), 'ds', 16)
    contrib_done = kc.n['ds']

    @block.gpsimd
    def _(g):
        kc.w(g, 'ds', contrib_done)
        ins = g.collective_compute("AllGather", ALU.bypass,
                                   replica_groups=[list(range(8))],
                                   ins=[dram['u2c'].ap().opt()],
                                   outs=[dram['uall'].ap().opt()])
        kc.inc(ins, 'cc', 1)
    ag_done = kc.n['cc']

    kc.nc.leave_named_scope(kc.scpn, kc.scp, False)
    kc.scp = kc.nc.enter_named_scope('s3_proj', False)[0]; kc.scpn = 's3_proj'
    ev_p = []
    for g512 in range(32):
        r, cg = g512 // 4, g512 % 4

        @block.sync
        def _(sy):
            if g512 == 0:
                kc.w(sy, 'cc', ag_done)
            if g512 >= 2:
                kc.w(sy, 'st', ev_p[g512 - 2][0])
            for kcc in range(8):
                kc.inc(sy.dma_start(
                    out=kc.t['stg'][:, 4096 * (g512 % 2) + 512 * kcc:
                                    4096 * (g512 % 2) + 512 * (kcc + 1)],
                    in_=dram['uall'].ap()[1024 * r + 128 * kcc:1024 * r + 128 * (kcc + 1),
                                          512 * cg:512 * (cg + 1)]), 'ds', 16)

        @block.tensor
        def _(tn):
            kc.w(tn, 'ds')
            if g512 == 0:
                kc.w(tn, 'dq', ld_s3)
            if g512 >= 1:
                kc.w(tn, 'ss', ev_p[g512 - 1][1])
            base = 4096 * (g512 % 2)
            for kcc in range(8):
                tn.matmul(out=psB[:, 0:512],
                          lhsT=kc.t['wqs'][:, 128 * kcc:128 * (kcc + 1)],
                          rhs=kc.t['stg'][:, base + 512 * kcc:base + 512 * (kcc + 1)],
                          start=(kcc == 0), stop=(kcc == 7))
            for kcc in range(8):
                tn.matmul(out=psB[:, 512:1024],
                          lhsT=kc.t['wks'][:, 128 * kcc:128 * (kcc + 1)],
                          rhs=kc.t['stg'][:, base + 512 * kcc:base + 512 * (kcc + 1)],
                          start=(kcc == 0), stop=(kcc == 7))
            for ti in range(4):
                for kcc in range(8):
                    ins = tn.matmul(out=psB[:, 1024 + 64 * ti:1024 + 64 * (ti + 1)],
                                    lhsT=kc.t['stg'][:, base + 512 * kcc + 128 * ti:
                                                     base + 512 * kcc + 128 * (ti + 1)],
                                    rhs=kc.t['wvs'][:, 64 * kcc:64 * (kcc + 1)],
                                    start=(kcc == 0), stop=(kcc == 7))
            kc.inc(ins, 'st')
        ev_mm = kc.n['st']

        @block.vector
        def _(v):
            kc.w(v, 'st', ev_mm)
            v.tensor_scalar(out=kc.t['qfm'][:, 512 * g512:512 * (g512 + 1)],
                            in0=psB[:, 0:512], scalar1=-0.125, scalar2=None, op0=ALU.mult)
            v.tensor_copy(out=kc.t['kfm3'][:, 512 * g512:512 * (g512 + 1)],
                          in_=psB[:, 512:1024])
            ins = v.tensor_copy(out=kc.t['vtm3'][:, 256 * g512:256 * (g512 + 1)],
                                in_=psB[:, 1024:1280])
            kc.inc(ins, 'ss')
        ev_p.append((ev_mm, kc.n['ss']))
    proj_done = kc.n['ss']
    kc.nc.leave_named_scope(kc.scpn, kc.scp, False)
    kc.scp = kc.nc.enter_named_scope('s3_attn', False)[0]; kc.scpn = 's3_attn'
    if getattr(kc, 'dbg', None) == 'qb':
        return _dbg_dump(kc, 'qfm')
    if getattr(kc, 'dbg', None) == 'kb':
        return _dbg_dump(kc, 'kfm3')
    if getattr(kc, 'dbg', None) == 'vb':
        return _dbg_dump(kc, 'vtm3', 8192)

    insts = [(gq, s) for gq in range(16) for s in range(8)]
    SREG = [psA, psB]
    kc.reg('pb3b', 32768, 2 * 1280, BF16)
    kc.reg('dg3b', 38400, 2 * 128, BF16)
    kc.reg('pt3b', 40960, 2 * 1280, BF16)
    PBV = lambda n: kc.t['pb3'] if n % 2 == 0 else kc.t['pb3b']
    DGV = lambda n: kc.t['dg3'] if n % 2 == 0 else kc.t['dg3b']
    PTV = lambda n: kc.t['pt3'] if n % 2 == 0 else kc.t['pt3b']
    sc_mm = {}; min_ev = {}; exp_ev = {}; rc_ev = {}; dg_ev = {}
    ptA_mm = {}; pt_mm = {}; rA_ev = {}; rB_ev = {}; av_mm = {}; ctx_ev = {}
    for hd in range(2):
        for nn in (-2, -1):
            pt_mm[(nn, hd)] = kc.n['st']
            av_mm[(nn, hd)] = kc.n['st']
            rA_ev[(nn, hd)] = kc.n['ss']
            rB_ev[(nn, hd)] = kc.n['sv']
            dg_ev[(nn, hd)] = kc.n['sg']
    ctx_ev[-1] = kc.n['sv']

    def _kinfo(gq, s, nk, b):
        kc128 = gq - nk + 1 + b
        kcol = 2048 * (kc128 // 2) + 256 * s + 128 * (kc128 % 2)
        vtile = 16 * (kc128 // 2) + 2 * s + (kc128 % 2)
        return kcol, vtile

    def _emit_scores(n):
        gq, s = insts[n]
        nk = min(gq + 1, 9)
        qcol = 2048 * (gq // 2) + 256 * s + 128 * (gq % 2)
        for hd in range(2):
            @block.tensor
            def _(tn, gq=gq, s=s, nk=nk, qcol=qcol, hd=hd, n=n):
                if n == 0:
                    kc.w(tn, 'ss', proj_done)
                else:
                    kc.w(tn, 'ss', exp_ev[(n - 1, hd)])
                for b in range(nk):
                    kcol, _vt3 = _kinfo(gq, s, nk, b)
                    masked = (b == nk - 1) or (nk == 9 and b == 0)
                    ins = tn.matmul(out=SREG[hd][:, 128 * b:128 * (b + 1)],
                                    lhsT=kc.t['qfm'][64 * hd:64 * (hd + 1), qcol:qcol + 128],
                                    rhs=kc.t['kfm3'][64 * hd:64 * (hd + 1), kcol:kcol + 128],
                                    start=True, stop=not masked)
                    if masked:
                        mk = 'mcauB' if b == nk - 1 else 'mantB'
                        ins = tn.matmul(out=SREG[hd][:, 128 * b:128 * (b + 1)],
                                        lhsT=kc.t['idnS'][:, :], rhs=kc.t[mk][:, :],
                                        start=False, stop=True)
                sc_mm[(n, hd)] = kc.inc(ins, 'st')

    _emit_scores(0)
    N_INST = len(insts)
    for n in range(N_INST):
        gq, s = insts[n]
        nk = min(gq + 1, 9)
        nA = min(nk, 5)
        nB = nk - nA
        qcol = 2048 * (gq // 2) + 256 * s + 128 * (gq % 2)

        @block.vector
        def _(v, n=n):
            if n > 0:
                kc.w(v, 'st', av_mm[(n - 1, 1)])
                gq1, s1_ = insts[n - 1]
                qc1 = 2048 * (gq1 // 2) + 256 * s1_ + 128 * (gq1 % 2)
                ins = v.tensor_copy(out=kc.t['ctxall'][:, qc1:qc1 + 128],
                                    in_=psA[:, 1920:2048])
                ctx_ev[n - 1] = kc.inc(ins, 'sv')
            for hd in range(2):
                kc.w(v, 'st', sc_mm[(n, hd)])
                ins = v.tensor_reduce(out=kc.t['stats'][:, 120 + hd:121 + hd],
                                      in_=SREG[hd][:, 0:128 * nk], axis=AX.X, op=ALU.min)
                min_ev[(n, hd)] = kc.inc(ins, 'sv')

        @block.scalar
        def _(sc, n=n, nk=nk):
            for hd in range(2):
                kc.w(sc, 'sv', min_ev[(n, hd)])
                kc.w(sc, 'st', pt_mm[(n - 2, hd)])
                ins = sc.activation(out=PBV(n)[:, 1280 * hd:1280 * hd + 128 * nk],
                                    in_=SREG[hd][:, 0:128 * nk], func=AF.Exp, scale=-1.0,
                                    bias=kc.t['stats'][:, 120 + hd:121 + hd],
                                    accum_out=kc.t['stats'][:, 122 + hd:123 + hd])
                exp_ev[(n, hd)] = kc.inc(ins, 'ss')

        @block.vector
        def _(v, n=n):
            for hd in range(2):
                kc.w(v, 'ss', exp_ev[(n, hd)])
                kc.w(v, 'sg', dg_ev[(n - 1, hd)])
                ins = v.reciprocal(out=kc.t['stats'][:, 124 + hd:125 + hd],
                                   in_=kc.t['stats'][:, 122 + hd:123 + hd])
                rc_ev[(n, hd)] = kc.inc(ins, 'sv')

        @block.gpsimd
        def _(g, n=n):
            for hd in range(2):
                kc.w(g, 'sv', rc_ev[(n, hd)])
                kc.w(g, 'st', pt_mm[(n - 2, hd)])
                ins = g.tensor_scalar(out=DGV(n)[:, 128 * hd:128 * (hd + 1)],
                                      in0=kc.t['idnS'][:, :],
                                      scalar1=kc.t['stats'][:, 124 + hd:125 + hd],
                                      scalar2=None, op0=ALU.mult)
                dg_ev[(n, hd)] = kc.inc(ins, 'sg')

        for hd in range(2):
            @block.tensor
            def _(tn, n=n, hd=hd, nA=nA):
                kc.w(tn, 'sg', dg_ev[(n, hd)])
                kc.w(tn, 'ss', rA_ev[(n - 1, hd)])
                kc.w(tn, 'sv', rB_ev[(n - 1, hd)])
                for b in range(nA):
                    ins = tn.matmul(out=SREG[hd][:, 1152 + 128 * b:1152 + 128 * (b + 1)],
                                    lhsT=PBV(n)[:, 1280 * hd + 128 * b:
                                                1280 * hd + 128 * (b + 1)],
                                    rhs=DGV(n)[:, 128 * hd:128 * (hd + 1)],
                                    start=True, stop=True)
                ptA_mm[(n, hd)] = kc.inc(ins, 'st')
        if n + 1 < N_INST:
            _emit_scores(n + 1)

        @block.scalar
        def _(sc, n=n, nA=nA):
            for hd in range(2):
                kc.w(sc, 'st', ptA_mm[(n, hd)])
                kc.w(sc, 'st', av_mm[(n - 2, hd)])
                ins = sc.activation(out=PTV(n)[:, 1280 * hd:1280 * hd + 128 * nA],
                                    in_=SREG[hd][:, 1152:1152 + 128 * nA], func=AF.Relu)
                rA_ev[(n, hd)] = kc.inc(ins, 'ss')

        if nB > 0:
            for hd in range(2):
                @block.tensor
                def _(tn, n=n, hd=hd, nA=nA, nB=nB):
                    kc.w(tn, 'ss', rA_ev[(n, hd)])
                    for b in range(nB):
                        ins = tn.matmul(out=SREG[hd][:, 1152 + 128 * b:1152 + 128 * (b + 1)],
                                        lhsT=PBV(n)[:, 1280 * hd + 128 * (nA + b):
                                                    1280 * hd + 128 * (nA + b + 1)],
                                        rhs=DGV(n)[:, 128 * hd:128 * (hd + 1)],
                                        start=True, stop=True)
                    pt_mm[(n, hd)] = kc.inc(ins, 'st')

            @block.vector
            def _(v, n=n, nA=nA, nB=nB):
                for hd in range(2):
                    kc.w(v, 'st', pt_mm[(n, hd)])
                    kc.w(v, 'st', av_mm[(n - 2, hd)])
                    ins = v.tensor_copy(out=PTV(n)[:, 1280 * hd + 128 * nA:
                                                   1280 * hd + 128 * (nA + nB)],
                                        in_=SREG[hd][:, 1152:1152 + 128 * nB])
                    rB_ev[(n, hd)] = kc.inc(ins, 'sv')
        else:
            for hd in range(2):
                pt_mm[(n, hd)] = ptA_mm[(n, hd)]
                rB_ev[(n, hd)] = rB_ev[(n - 1, hd)]

        for hd in range(2):
            @block.tensor
            def _(tn, gq=gq, s=s, n=n, hd=hd, nk=nk, nA=nA, nB=nB):
                kc.w(tn, 'ss', rA_ev[(n, hd)])
                kc.w(tn, 'sv', ctx_ev[n - 1])
                if nB > 0:
                    kc.w(tn, 'sv', rB_ev[(n, hd)])
                for b in range(nk):
                    _kc, vtile = _kinfo(gq, s, nk, b)
                    ins = tn.matmul(out=psA[64 * hd:64 * (hd + 1), 1920:2048],
                                    lhsT=kc.t['vtm3'][:, 64 * vtile:64 * (vtile + 1)],
                                    rhs=PTV(n)[:, 1280 * hd + 128 * b:
                                               1280 * hd + 128 * (b + 1)],
                                    start=(b == 0), stop=(b == nk - 1),
                                    tile_position=(0, 64 * hd))
                av_mm[(n, hd)] = kc.inc(ins, 'st')

    @block.vector
    def _(v):
        kc.w(v, 'st', av_mm[(N_INST - 1, 1)])
        gq1, s1_ = insts[N_INST - 1]
        qc1 = 2048 * (gq1 // 2) + 256 * s1_ + 128 * (gq1 % 2)
        ins = v.tensor_copy(out=kc.t['ctxall'][:, qc1:qc1 + 128],
                            in_=psA[:, 1920:2048])
        kc.inc(ins, 'sv')
    att_done = kc.n['sv']
    kc.nc.leave_named_scope(kc.scpn, kc.scp, False)
    kc.scp = kc.nc.enter_named_scope('s3_a2a_wf', False)[0]; kc.scpn = 's3_a2a_wf'
    if getattr(kc, 'dbg', None) == 'att':
        return _dbg_dump(kc, 'ctxall')

    @block.sync
    def _(sy):
        kc.w(sy, 'sv', att_done)
        for f in range(8):
            kc.inc(sy.dma_start(out=dram['cta'].ap()[128 * f:128 * (f + 1), :],
                                in_=kc.t['ctxall'][:, 2048 * f:2048 * (f + 1)]), 'ds', 16)
    cta_done = kc.n['ds']

    @block.gpsimd
    def _(g):
        kc.w(g, 'ds', cta_done)
        with allow_ici_all_to_all():
            ins = g.collective_compute("AllToAll", ALU.bypass,
                                       replica_groups=[list(range(8))],
                                       ins=[dram['cta'].ap().opt()],
                                       outs=[dram['ctb'].ap().opt()])
        kc.inc(ins, 'cc2', 1)
    a2a_done = kc.n['cc2']

    @block.sync
    def _(sy):
        kc.w(sy, 'cc2', a2a_done)
        kc.w(sy, 'st')
        for f in range(8):
            kc.inc(sy.dma_start(out=kc.t['ctxf3'][:, 2048 * f:2048 * (f + 1)],
                                in_=dram['ctb'].ap()[128 * f:128 * (f + 1), :]), 'ds', 16)
    ctb_ld = kc.n['ds']

    ev_wf = []
    for t in range(NT):
        @block.tensor
        def _(tn):
            if t == 0:
                kc.w(tn, 'ds', ctb_ld)
            if t >= 2:
                kc.w(tn, 'sv', ev_wf[t - 2])
            reg = t % 2
            for c in range(8):
                tn.matmul(out=psB[:, 1024 * reg:1024 * reg + 512],
                          lhsT=kc.t['ctxf3'][:, 2048 * c + 128 * t:2048 * c + 128 * (t + 1)],
                          rhs=kc.t['wfb'][:, 1024 * c:1024 * c + 512],
                          start=(c == 0), stop=False)
                tn.matmul(out=psB[:, 1024 * reg + 512:1024 * reg + 1024],
                          lhsT=kc.t['ctxf3'][:, 2048 * c + 128 * t:2048 * c + 128 * (t + 1)],
                          rhs=kc.t['wfb'][:, 1024 * c + 512:1024 * (c + 1)],
                          start=(c == 0), stop=False)
            for half in range(2):
                ins = tn.matmul(out=psB[:, 1024 * reg + 512 * half:1024 * reg + 512 * (half + 1)],
                                lhsT=kc.t['ones_row'][:, :],
                                rhs=kc.t['bfr'][:, 512 * half:512 * (half + 1)],
                                start=False, stop=True)
            kc.inc(ins, 'st')

        @block.vector
        def _(v):
            kc.w(v, 'st')
            reg = t % 2
            ins = v.scalar_tensor_tensor(out=kc.t['xsb'][:, 1024 * t:1024 * (t + 1)],
                                         in0=psB[:, 1024 * reg:1024 * (reg + 1)], scalar=1.0,
                                         in1=kc.t['xsb'][:, 1024 * t:1024 * (t + 1)],
                                         op0=ALU.mult, op1=ALU.add)
            kc.inc(ins, 'sv')
        ev_wf.append(kc.n['sv'])
    kc.nc.leave_named_scope(kc.scpn, kc.scp, False)
def build(stages=4, dbg=None):
    nc = bass.Bass("TRN2", target_bir_lowering=False)
    es = ExitStack()
    h = {}
    for name, shape, dt in [
        ('xs', [8, 320, 1024], F32), ('wqaT', [1024, 1024], F32), ('wkaT', [1024, 1024], F32),
        ('wvaT', [1024, 1024], F32), ('woaT', [1024, 1024], F32),
        ('winU', [24, 1024, 128], F32), ('binc', [128, 24], F32), ('woutT', [1024, 1024], F32),
        ('outb_row', [1, 1024], F32), ('gwT', [1024, 8], F32), ('gb_row', [1, 8], F32),
        ('w1U', [8, 32, 1024, 128], F32), ('w2T', [8, 4096, 1024], F32),
        ('b1c', [128, 256], F32), ('b2a', [8, 1024], F32),
        ('g1c', [128, 8], F32), ('b1lc', [128, 8], F32),
        ('g2B', [128, 1024], F32), ('b2B', [128, 1024], F32),
        ('wqbT', [1024, 128], F32), ('wkbT2', [1024, 128], F32), ('wvbT', [1024, 64], F32),
        ('wfT', [1024, 1024], F32), ('bf_row', [1, 1024], F32),
        ('idn', [128, 128], BF16), ('mska', [2, 128, 192], BF16),
        ('mcaus', [128, 128], BF16), ('manti', [128, 128], BF16),
        ('mq', [17, 2048], BF16), ('mk', [17, 2048], BF16),
    ]:
        h[name] = nc.dram_tensor(name, shape, dt, kind="ExternalInput")
    y_h = nc.dram_tensor("y", [8, 256, 1024], F32, kind="ExternalOutput")
    dram = {
        'spill': nc.dram_tensor("spill", [128, NT * 1024], BF16),
        'u2c': nc.dram_tensor("u2c", [1024, 2048], BF16),
        'uall': nc.dram_tensor("uall", [8192, 2048], BF16, addr_space="Shared"),
        'cta': nc.dram_tensor("cta", [1024, 2048], BF16),
        'ctb': nc.dram_tensor("ctb", [1024, 2048], BF16),
        'moeacc': nc.dram_tensor("moeacc", [128, NT * 1024], F32),
    }

    with es:
        block = es.enter_context(nc.Block())
        kc = KC(nc, es, block)
        kc.h = h
        kc.dram = dram
        kc.dbg = dbg
        kc.mksem('dq', 'ds', 'sv', 'ss', 'st', 'sg', 'cc', 'cc2')

        kc.sb('idnS', [128, 128], BF16)
        kc.sb('epsc', [128, 1], F32)
        kc.sb('ones_row', [1, 128], BF16)
        kc.sb('stats', [128, 128], F32)
        kc.arena = kc.sb('arena', [128, 105728], BF16)
        kc.reg('xsb', 0, NT * 1024, BF16)
        kc.ps('psA', [128, 2048])
        kc.ps('psB', [128, 2048])

        @block.vector
        def _(v):
            v.memset(kc.t['epsc'][:, :], EPS)
            v.memset(kc.t['ones_row'][:, :], 1.0)

        @block.sync
        def _(sy):
            kc.inc(sy.dma_start(out=kc.t['idnS'][:, :], in_=h['idn'].ap()[:, :]), 'ds', 16)

        @block.gpsimd
        def _(g):
            for t in range(NT):
                s, hv = t // 2, t % 2
                kc.inc(g.dma_start(out=kc.t['xsb'][:, 1024 * t:1024 * (t + 1)],
                                   in_=h['xs'].ap()[s, 64 + 128 * hv:64 + 128 * (hv + 1), :]),
                       'dq', 16)

        with nc.named_scope('s1'):
            run_s1(kc, h)
        if stages >= 2:
            with nc.named_scope('ffn1'):
                run_ffn(kc, h, tag='f1')
        if stages >= 3:
            with nc.named_scope('s3'):
                run_s3(kc, h, dram)
        if stages >= 4:
            with nc.named_scope('ffn2'):
                run_ffn(kc, h, tag='f2')

        kc.reg('yout', 32768, 2 * 1024, F32)

        @block.vector
        def _(v):
            kc.w(v, 'sv'); kc.w(v, 'ss'); kc.w(v, 'st'); kc.w(v, 'sg')

        for t in range(NT):
            @block.vector
            def _(v, t=t):
                if t >= 2:
                    kc.w(v, 'ds')
                ins = v.tensor_copy(out=kc.t['yout'][:, 1024 * (t % 2):1024 * (t % 2 + 1)],
                                    in_=kc.t['xsb'][:, 1024 * t:1024 * (t + 1)])
                kc.inc(ins, 'sv')

            @block.sync
            def _(sy, t=t):
                kc.w(sy, 'sv')
                s, hv = t // 2, t % 2
                kc.inc(sy.dma_start(out=y_h.ap()[s, 128 * hv:128 * (hv + 1), :],
                                    in_=kc.t['yout'][:, 1024 * (t % 2):1024 * (t % 2 + 1)]),
                       'ds', 16)

        @block.sync
        def _(sy):
            kc.w(sy, 'ds')
            kc.w(sy, 'dq')
            kc.w(sy, 'sg')
    return nc


def kernel(**inputs):
    stages = getattr(kernel, 'STAGES', 4)
    in_maps = host_prep(inputs)
    nc = build(stages, getattr(kernel, 'DBG', None))
    if getattr(kernel, 'TRACE', False):
        res = run_bass_kernel_spmd(nc, in_maps, core_ids=list(range(8)), trace=True)
        kernel.last_scope_times = res.per_core_scope_times
        kernel.last_exec_ns = res.exec_time_ns
    else:
        res = run_bass_kernel_spmd(nc, in_maps, core_ids=list(range(8)))
    return np.concatenate([r["y"] for r in res.results], axis=1)


# version 09
# speedup vs baseline: 1.1534x; 1.0136x over previous
import numpy as np
import ml_dtypes
from contextlib import ExitStack

import concourse.bass as bass
import concourse.mybir as mybir
from concourse.bass_utils import run_bass_kernel_spmd
from concourse.aot_env import allow_ici_all_to_all

F32 = mybir.dt.float32
BF16 = mybir.dt.bfloat16
AF = mybir.ActivationFunctionType
ALU = mybir.AluOpType
AX = mybir.AxisListType
BFNP = ml_dtypes.bfloat16

NT = 16
NHALO = 4
D = 1024
EPS = 1e-5


class KC:
    def __init__(self, nc, es, block):
        self.nc, self.es, self.block = nc, es, block
        self.sem, self.n, self.t = {}, {}, {}

    def mksem(self, *names):
        for name in names:
            self.sem[name] = self.es.enter_context(self.nc.semaphore(name))
            self.n[name] = 0

    def sb(self, name, shape, dtype, stack=None):
        self.t[name] = (stack or self.es).enter_context(self.nc.sbuf_tensor(name, shape, dtype))
        return self.t[name]

    def reg(self, name, off_b, cols, dtype, rows=128):
        assert off_b % 64 == 0
        if dtype == BF16:
            v = self.arena[0:rows, off_b // 2: off_b // 2 + cols]
        else:
            v = self.arena[0:rows, off_b // 2: off_b // 2 + 2 * cols].bitcast(F32)
        self.t[name] = v
        return v

    def ps(self, name, shape, dtype=F32):
        self.t[name] = self.es.enter_context(self.nc.psum_tensor(name, shape, dtype))
        return self.t[name]

    def inc(self, instr, s, k=1):
        instr.then_inc(self.sem[s], k)
        self.n[s] += k
        return self.n[s]

    def inc2(self, instr, s1, s2):
        instr.then_inc(self.sem[s1], 1).then_inc(self.sem[s2], 1)
        self.n[s1] += 1
        self.n[s2] += 1
        return self.n[s2]

    def w(self, eng, s, v=None):
        eng.wait_ge(self.sem[s], self.n[s] if v is None else v)


def emit_ln_stats(kc, xbuf, tiles, scol, sqd='sqd'):
    block, st = kc.block, kc.t['stats']
    nt = len(tiles)
    S0, S1, S2, S3, S4 = (scol, scol + nt, scol + 2 * nt, scol + 3 * nt, scol + 4 * nt)

    @block.vector
    def _(v):
        kc.w(v, 'sv')
        for i, tl in enumerate(tiles):
            ins = v.reduce_sum(out=st[:, S0 + i:S0 + i + 1],
                               in_=kc.t[xbuf][:, 1024 * tl:1024 * (tl + 1)], axis=AX.X)
        kc.inc(ins, 'sv')

    @block.scalar
    def _(s):
        kc.w(s, 'sv')
        kc.w(s, 'ss')
        for i, tl in enumerate(tiles):
            ins = s.activation(out=kc.t[sqd][:, :], in_=kc.t[xbuf][:, 1024 * tl:1024 * (tl + 1)],
                               func=AF.Square, accum_out=st[:, S1 + i:S1 + i + 1])
        kc.inc(ins, 'ss')

    @block.vector
    def _(v):
        kc.w(v, 'sv')
        kc.w(v, 'ss')
        v.tensor_scalar(out=st[:, S2:S2 + nt], in0=st[:, S0:S0 + nt],
                        scalar1=1.0 / D, scalar2=None, op0=ALU.mult)
        ins = v.tensor_scalar(out=st[:, S3:S3 + nt], in0=st[:, S1:S1 + nt],
                              scalar1=1.0 / D, scalar2=None, op0=ALU.mult)
        kc.inc(ins, 'sv')
        kc.w(v, 'sv')
        ins = v.tensor_tensor(out=st[:, S4:S4 + nt], in0=st[:, S2:S2 + nt],
                              in1=st[:, S2:S2 + nt], op=ALU.mult)
        kc.inc(ins, 'sv')
        kc.w(v, 'sv')
        ins = v.tensor_tensor(out=st[:, S3:S3 + nt], in0=st[:, S3:S3 + nt],
                              in1=st[:, S4:S4 + nt], op=ALU.subtract)
        kc.inc(ins, 'sv')
        kc.w(v, 'sv')
        ins = v.tensor_scalar(out=st[:, S3:S3 + nt], in0=st[:, S3:S3 + nt],
                              scalar1=0.0, scalar2=None, op0=ALU.max)
        kc.inc(ins, 'sv')

    @block.scalar
    def _(s):
        kc.w(s, 'sv')
        ins = s.activation(out=st[:, S4:S4 + nt], in_=st[:, S3:S3 + nt],
                           func=AF.Ln, bias=kc.t['epsc'][:, 0:1])
        kc.inc(ins, 'ss')
        kc.w(s, 'ss')
        ins = s.activation(out=st[:, S4:S4 + nt], in_=st[:, S4:S4 + nt],
                           func=AF.Exp, scale=-0.5)
        kc.inc(ins, 'ss')
    return S2, S4


def _masks_attn_a(core):
    j = np.arange(192)[None, :]
    r = np.arange(128)[:, None]
    m = np.zeros((2, 128, 192), dtype=np.float32)
    for qt in range(2):
        ok = (j >= r + 1) & (j <= r + 64)
        if core == 0 and qt == 0:
            ok = ok & (j >= 64)
        m[qt] = np.where(ok, 0.0, MBIG)
    return m.astype(BFNP)


MBIG = 30000.0


def _tri_masks():
    r = np.arange(128)
    causB = np.where(r[None, :] > r[:, None], MBIG, 0.0).astype(np.float32)
    antiB = np.where(r[None, :] < r[:, None], MBIG, 0.0).astype(np.float32)
    return causB.astype(BFNP), antiB.astype(BFNP)


def _mqk():
    A = 128.0
    colp = np.arange(2048)
    T, rem = colp // 128, colp % 128
    s, pp = rem // 16, rem % 16
    mq = np.zeros((17, 2048), dtype=np.float32)
    for i in range(16):
        mq[i] = (pp == i) * A
    mq[16] = A
    mk = -mq.copy()
    mk[16] = A
    return mq.astype(BFNP), mk.astype(BFNP)


def host_prep(inputs):
    x0 = np.asarray(inputs['L0_attn_a_x'])
    T = lambda a: np.ascontiguousarray(np.asarray(a).T)
    A = lambda a: np.ascontiguousarray(np.asarray(a, dtype=np.float32))
    com = {
        'wqaT': T(inputs['L0_attn_a_Wq']), 'wkaT': T(inputs['L0_attn_a_Wk']),
        'wvaT': T(inputs['L0_attn_a_Wv']), 'woaT': T(inputs['L0_attn_a_Wo']),
        'winU': np.ascontiguousarray(np.asarray(inputs['L0_ffn_in_proj_w'], dtype=np.float32)
                                     .reshape(24, 128, 1024).transpose(0, 2, 1)),
        'binc': np.ascontiguousarray(A(inputs['L0_ffn_in_proj_b']).reshape(24, 128).T),
        'woutT': T(inputs['L0_ffn_out_w']),
        'outb_row': A(inputs['L0_ffn_out_b']).reshape(1, 1024),
        'gwT': T(inputs['L0_ffn_gate_w']),
        'gb_row': A(inputs['L0_ffn_gate_b']).reshape(1, 8),
        'w1U': np.ascontiguousarray(np.asarray(inputs['L0_ffn_w1']).transpose(0, 2, 1)
                                    .reshape(8, 1024, 32, 128).transpose(0, 2, 1, 3)),
        'w2T': np.ascontiguousarray(np.asarray(inputs['L0_ffn_w2']).transpose(0, 2, 1)),
        'b1c': np.ascontiguousarray(
            A(inputs['L0_ffn_b1']).reshape(8, 32, 128).transpose(2, 0, 1).reshape(128, 256)),
        'b2a': A(inputs['L0_ffn_b2']),
        'g1c': np.ascontiguousarray(A(inputs['L0_ffn_ln1_g']).reshape(8, 128).T),
        'b1lc': np.ascontiguousarray(A(inputs['L0_ffn_ln1_b']).reshape(8, 128).T),
        'g2B': np.ascontiguousarray(np.tile(A(inputs['L0_ffn_ln2_g'])[None, :], (128, 1))),
        'b2B': np.ascontiguousarray(np.tile(A(inputs['L0_ffn_ln2_b'])[None, :], (128, 1))),
        'wkbT2': np.ascontiguousarray(np.concatenate([T(inputs['L0_attn_b_Wk'])] * 2, axis=1)),
        'wvbT': T(inputs['L0_attn_b_Wv']),
        'wfT': T(inputs['L0_attn_b_Wf']),
        'bf_row': A(inputs['L0_attn_b_bf']).reshape(1, 1024),
        'idn': np.eye(128, dtype=BFNP),
    }
    com['mq'], com['mk'] = _mqk()
    com['mcaus'], com['manti'] = _tri_masks()
    wqbT = T(inputs['L0_attn_b_Wq'])
    in_maps = []
    for c in range(8):
        m = dict(com)
        xs = np.zeros((8, 320, 1024), dtype=np.float32)
        xs[:, 64:] = x0[:, 256 * c:256 * (c + 1)]
        if c > 0:
            xs[:, :64] = x0[:, 256 * c - 64:256 * c]
        m['xs'] = xs
        m['mska'] = _masks_attn_a(c)
        m['wqbT'] = np.ascontiguousarray(wqbT[:, 128 * c:128 * (c + 1)])
        in_maps.append(m)
    return in_maps


VT1 = 163840


def _vt(kc, s, k):
    if k < 2:
        base = (VT1 + 2048 * (2 * s + k)) // 2
        return kc.arena[0:128, base:base + 1024]
    base = (VT1 + 32768 + 2048 * (s // 2)) // 2
    r0 = 64 * (s % 2)
    return kc.arena[r0:r0 + 64, base:base + 1024]


def run_s1(kc, h):
    block = kc.block
    kc.reg('ufm', 32768, 8 * 2560, BF16)
    kc.reg('xh', 73728, NHALO * 1024, BF16)
    kc.reg('utmpA', 81920, 2 * 1024, BF16)
    kc.reg('sqdA', 86016, 1024, F32)
    psA, psB = kc.t['psA'], kc.t['psB']

    @block.gpsimd
    def _(g):
        for t in range(NHALO):
            for k in range(2):
                kc.inc(g.dma_start(out=kc.t['xh'][64 * k:64 * (k + 1), 1024 * t:1024 * (t + 1)],
                                   in_=h['xs'].ap()[2 * t + k, 0:64, :]), 'dq', 16)

    @block.vector
    def _(v):
        kc.w(v, 'dq')
        kc.w(v, 'ds')

    @block.scalar
    def _(s):
        kc.w(s, 'dq')
    M1, R1 = emit_ln_stats(kc, 'xsb', list(range(NT)), 0, 'sqdA')
    M2, R2 = emit_ln_stats(kc, 'xh', list(range(NHALO)), 100, 'sqdA')

    evs = []
    for it in range(NT + NHALO):
        halo = it >= NT
        t = it - NT if halo else it

        @block.vector
        def _(v):
            kc.w(v, 'ss')
            if it >= 2:
                kc.w(v, 'st', evs[it - 2][0])
            src = kc.t['xh'] if halo else kc.t['xsb']
            mcol = (M2 + t) if halo else (M1 + t)
            rcol = (R2 + t) if halo else (R1 + t)
            ins = v.tensor_scalar(out=kc.t['utmpA'][:, 1024 * (it % 2):1024 * (it % 2 + 1)],
                                  in0=src[:, 1024 * t:1024 * (t + 1)],
                                  scalar1=kc.t['stats'][:, mcol:mcol + 1],
                                  scalar2=kc.t['stats'][:, rcol:rcol + 1],
                                  op0=ALU.subtract, op1=ALU.mult)
            kc.inc(ins, 'sv')

        @block.tensor
        def _(tn):
            kc.w(tn, 'sv')
            if it >= 2:
                kc.w(tn, 'sv', evs[it - 2][1])
            for f in range(8):
                ins = tn.matmul(out=psA[:, 1024 * (it % 2) + 128 * f:1024 * (it % 2) + 128 * (f + 1)],
                                lhsT=kc.t['utmpA'][:, 1024 * (it % 2) + 128 * f:
                                                  1024 * (it % 2) + 128 * (f + 1)],
                                rhs=kc.t['idnS'][:, :], start=True, stop=True)
            kc.inc(ins, 'st')
        st_t = kc.n['st']

        @block.vector
        def _(v):
            kc.w(v, 'st')
            if not halo:
                s, hv = t // 2, t % 2
                a0 = 320 * s + 64 + 128 * hv
                ov = kc.t['ufm'][:, :].rearrange("p (f a) -> p f a", a=2560)[:, :, a0:a0 + 128]
                iv = psA[:, 1024 * (it % 2):1024 * (it % 2 + 1)].rearrange("p (f a) -> p f a", a=128)
            else:
                ov = kc.t['ufm'][:, :].rearrange("p (f s i) -> p f s i", s=8, i=320)[
                    :, :, 2 * t:2 * t + 2, 0:64]
                iv = psA[:, 1024 * (it % 2):1024 * (it % 2 + 1)].rearrange(
                    "p (f k i) -> p f k i", k=2, i=64)
            ins = v.tensor_copy(out=ov, in_=iv)
            kc.inc(ins, 'sv')
        evs.append((st_t, kc.n['sv']))
    ufm_done = kc.n['sv']
    kc.reg('qfull', 73728, 8 * 2048, BF16)
    kc.reg('wbuf', 106496, 8 * 1024, BF16)

    @block.gpsimd
    def _(g):
        kc.w(g, 'st')
        kc.inc(g.dma_start(out=kc.t['wbuf'][:, :].rearrange("p (c f) -> p c f", f=1024),
                           in_=h['wqaT'].ap().rearrange("(c p) f -> p c f", p=128)), 'dq', 16)
    ld_wq = kc.n['dq']

    ev_q = []
    for oc in range(8):
        for grp in range(4):
            idx = oc * 4 + grp
            reg = idx % 4

            @block.tensor
            def _(tn):
                if idx == 0:
                    kc.w(tn, 'dq', ld_wq)
                    kc.w(tn, 'sv', ufm_done)
                if idx >= 4:
                    kc.w(tn, 'ss', ev_q[idx - 4])
                ua = kc.t['ufm'][:, :].rearrange("p (f s i) -> p f s i", f=8, i=320)
                rhs = ua[:, :, 2 * grp:2 * grp + 2, 64:320]
                ov = psB[:, 512 * reg:512 * (reg + 1)].rearrange("p (s i) -> p s i", s=2)
                for c in range(8):
                    ins = tn.matmul(out=ov, lhsT=kc.t['wbuf'][:, 1024 * c + 128 * oc:
                                                              1024 * c + 128 * (oc + 1)],
                                    rhs=rhs[:, c, :, :], start=(c == 0), stop=(c == 7))
                kc.inc(ins, 'st')

            @block.vector
            def _(v):
                kc.w(v, 'st')
                ins = v.tensor_scalar(out=kc.t['qfull'][:, 2048 * oc + 512 * grp:2048 * oc + 512 * (grp + 1)],
                                      in0=psB[:, 512 * reg:512 * (reg + 1)],
                                      scalar1=-0.125, scalar2=None, op0=ALU.mult)
                kc.inc(ins, 'ss')
            ev_q.append(kc.n['ss'])

    @block.gpsimd
    def _(g):
        kc.w(g, 'st')
        kc.inc(g.dma_start(out=kc.t['wbuf'][:, :].rearrange("p (c f) -> p c f", f=1024),
                           in_=h['wkaT'].ap().rearrange("(c p) f -> p c f", p=128)), 'dq', 16)
    ld_wk = kc.n['dq']
    kc.reg('kfm', 122880, 8 * 2560, BF16)

    ev_k = []
    for oc in range(8):
        for grp in range(5):
            idx = oc * 5 + grp
            reg = idx % 4

            @block.tensor
            def _(tn):
                if idx == 0:
                    kc.w(tn, 'dq', ld_wk)
                    kc.w(tn, 'ss', ev_q[-1])
                if idx >= 4:
                    kc.w(tn, 'sv', ev_k[idx - 4])
                for c in range(8):
                    ins = tn.matmul(out=psB[:, 512 * reg:512 * (reg + 1)],
                                    lhsT=kc.t['wbuf'][:, 1024 * c + 128 * oc:1024 * c + 128 * (oc + 1)],
                                    rhs=kc.t['ufm'][:, 2560 * c + 512 * grp:2560 * c + 512 * (grp + 1)],
                                    start=(c == 0), stop=(c == 7))
                kc.inc(ins, 'st')

            @block.vector
            def _(v):
                kc.w(v, 'st')
                ins = v.tensor_copy(out=kc.t['kfm'][:, 2560 * oc + 512 * grp:2560 * oc + 512 * (grp + 1)],
                                    in_=psB[:, 512 * reg:512 * (reg + 1)])
                kc.inc(ins, 'sv')
            ev_k.append(kc.n['sv'])

    @block.gpsimd
    def _(g):
        kc.w(g, 'st')
        kc.inc(g.dma_start(out=kc.t['wbuf'][:, :].rearrange("p (c f) -> p c f", f=1024),
                           in_=h['wvaT'].ap().rearrange("(c p) f -> p c f", p=128)), 'dq', 16)
    ld_wv = kc.n['dq']

    ev_v = []
    for s in range(8):
        for k in range(3):
            idx = 3 * s + k
            reg = idx % 2

            @block.tensor
            def _(tn):
                if idx == 0:
                    kc.w(tn, 'dq', ld_wv)
                    kc.w(tn, 'sv', ev_k[-1])
                elif idx == 1:
                    pass
                else:
                    kc.w(tn, 'sv', ev_v[idx - 2])
                rows = 128 if k < 2 else 64
                r0 = 64 * (s % 2) if k == 2 else 0
                tp = (0, r0) if k == 2 else None
                ua = kc.t['ufm'][:, :].rearrange("p (f a) -> p f a", a=2560)
                for half in range(2):
                    for c in range(8):
                        ins = tn.matmul(out=psB[r0:r0 + rows, 1024 * reg + 512 * half:
                                                1024 * reg + 512 * (half + 1)],
                                        lhsT=ua[:, c, 320 * s + 128 * k:320 * s + 128 * k + rows],
                                        rhs=kc.t['wbuf'][:, 1024 * c + 512 * half:
                                                         1024 * c + 512 * (half + 1)],
                                        start=(c == 0), stop=(c == 7), tile_position=tp)
                kc.inc(ins, 'st')

            @block.vector
            def _(v):
                kc.w(v, 'st')
                rows = 128 if k < 2 else 64
                r0 = 64 * (s % 2) if k == 2 else 0
                ins = v.tensor_copy(out=_vt(kc, s, k),
                                    in_=psB[r0:r0 + rows, 1024 * reg:1024 * (reg + 1)])
                kc.inc(ins, 'sv')
            ev_v.append(kc.n['sv'])

    @block.gpsimd
    def _(g):
        kc.w(g, 'st')
        kc.inc(g.dma_start(out=kc.t['wbuf'][:, :].rearrange("p (c f) -> p c f", f=1024),
                           in_=h['woaT'].ap().rearrange("(c p) f -> p c f", p=128)), 'dq', 16)
    ld_wo = kc.n['dq']
    kc.reg('mskaS', 32768, 2 * 192, BF16)
    kc.reg('pbuf', 33792, 2 * 256, BF16)
    kc.reg('ptb', 34816, 2 * 256, BF16)
    kc.reg('diag', 35840, 2 * 128, BF16)
    kc.reg('ctxb', 36864, 1024, BF16)
    kc.reg('acol', 39936, 64, F32)

    @block.sync
    def _(sy):
        kc.w(sy, 'st')
        kc.inc(sy.dma_start(out=kc.t['mskaS'][:, 0:192], in_=h['mska'].ap()[0, :, :]), 'ds', 16)
        kc.inc(sy.dma_start(out=kc.t['mskaS'][:, 192:384], in_=h['mska'].ap()[1, :, :]), 'ds', 16)

    SC = [1024, 1536]
    PT = [1216, 1728]
    for s in range(8):
        for qt in range(2):
            inst = 2 * s + qt
            mmpt = {}
            avmm = {}
            evpt = {}
            for hd in range(16):
                reg = hd % 2

                @block.tensor
                def _(tn):
                    if inst == 0 and hd == 0:
                        kc.w(tn, 'ss')
                    if hd >= 2:
                        kc.w(tn, 'ss', evpt[hd - 2])
                    elif inst > 0:
                        kc.w(tn, 'ss')
                    oc, rr = hd // 2, 64 * (hd % 2)
                    a0 = 320 * s + 128 * qt
                    q0 = 2048 * oc + 256 * s + 128 * qt
                    tn.matmul(out=psA[:, SC[reg]:SC[reg] + 192],
                              lhsT=kc.t['qfull'][rr:rr + 64, q0:q0 + 128],
                              rhs=kc.t['kfm'][rr:rr + 64, 2560 * oc + a0:2560 * oc + a0 + 192],
                              start=True, stop=False)
                    ins = tn.matmul(out=psA[:, SC[reg]:SC[reg] + 192],
                                    lhsT=kc.t['idnS'][:, :],
                                    rhs=kc.t['mskaS'][:, 192 * qt:192 * (qt + 1)],
                                    start=False, stop=True)
                    kc.inc(ins, 'st')

                @block.vector
                def _(v):
                    kc.w(v, 'st')
                    ins = v.tensor_reduce(out=kc.t['acol'][:, hd:hd + 1],
                                          in_=psA[:, SC[reg]:SC[reg] + 192],
                                          axis=AX.X, op=ALU.min)
                    kc.inc(ins, 'sv')

                @block.scalar
                def _(sc):
                    kc.w(sc, 'sv')
                    if hd >= 2:
                        kc.w(sc, 'st', mmpt[hd - 2])
                    ins = sc.activation(out=kc.t['pbuf'][:, 256 * reg:256 * reg + 192],
                                        in_=psA[:, SC[reg]:SC[reg] + 192],
                                        func=AF.Exp, scale=-1.0, bias=kc.t['acol'][:, hd:hd + 1])
                    kc.inc(ins, 'ss')

                @block.vector
                def _(v):
                    kc.w(v, 'ss')
                    ins = v.reduce_sum(out=kc.t['acol'][:, 16 + hd:17 + hd],
                                       in_=kc.t['pbuf'][:, 256 * reg:256 * reg + 192], axis=AX.X)
                    kc.inc(ins, 'sv')
                    kc.w(v, 'sv')
                    ins = v.reciprocal(out=kc.t['acol'][:, 32 + hd:33 + hd],
                                       in_=kc.t['acol'][:, 16 + hd:17 + hd])
                    kc.inc(ins, 'sv')
                    kc.w(v, 'sv')
                    if hd >= 2:
                        kc.w(v, 'st', mmpt[hd - 2])
                    ins = v.tensor_scalar(out=kc.t['diag'][:, 128 * reg:128 * (reg + 1)],
                                          in0=kc.t['idnS'][:, :],
                                          scalar1=kc.t['acol'][:, 32 + hd:33 + hd],
                                          scalar2=None, op0=ALU.mult)
                    kc.inc(ins, 'sv')

                @block.tensor
                def _(tn):
                    kc.w(tn, 'sv')
                    rB = 64 * (s % 2) if qt == 1 else 0
                    tn.matmul(out=psA[:, PT[reg]:PT[reg] + 128],
                              lhsT=kc.t['pbuf'][:, 256 * reg:256 * reg + 128],
                              rhs=kc.t['diag'][:, 128 * reg:128 * (reg + 1)],
                              start=True, stop=True)
                    ins = tn.matmul(out=psA[rB:rB + 64, PT[reg] + 128:PT[reg] + 256],
                                    lhsT=kc.t['pbuf'][:, 256 * reg + 128:256 * reg + 192],
                                    rhs=kc.t['diag'][:, 128 * reg:128 * (reg + 1)],
                                    start=True, stop=True, tile_position=(0, rB))
                    mmpt[hd] = kc.inc(ins, 'st')

                @block.scalar
                def _(sc):
                    kc.w(sc, 'st', mmpt[hd])
                    if hd >= 2:
                        kc.w(sc, 'st', avmm[hd - 2])
                    rB = 64 * (s % 2) if qt == 1 else 0
                    sc.activation(out=kc.t['ptb'][:, 256 * reg:256 * reg + 128],
                                  in_=psA[:, PT[reg]:PT[reg] + 128], func=AF.Relu)
                    ins = sc.activation(out=kc.t['ptb'][rB:rB + 64, 256 * reg + 128:256 * (reg + 1)],
                                        in_=psA[rB:rB + 64, PT[reg] + 128:PT[reg] + 256],
                                        func=AF.Relu)
                    evpt[hd] = kc.inc(ins, 'ss')

                @block.tensor
                def _(tn):
                    kc.w(tn, 'ss', evpt[hd])
                    oc, rr = hd // 2, 64 * (hd % 2)
                    vA = _vt(kc, s, qt)
                    vB = _vt(kc, s, qt + 1)
                    tn.matmul(out=psB[rr:rr + 64, 128 * oc:128 * (oc + 1)],
                              lhsT=vA[:, 64 * hd:64 * (hd + 1)],
                              rhs=kc.t['ptb'][:, 256 * reg:256 * reg + 128],
                              start=True, stop=False)
                    rB = 64 * (s % 2) if qt == 1 else 0
                    lhsTB = vB[0:64, 64 * hd:64 * (hd + 1)] if qt == 0 else vB[:, 64 * hd:64 * (hd + 1)]
                    ins = tn.matmul(out=psB[rr:rr + 64, 128 * oc:128 * (oc + 1)],
                                    lhsT=lhsTB,
                                    rhs=kc.t['ptb'][rB:rB + 64, 256 * reg + 128:256 * (reg + 1)],
                                    start=False, stop=True, tile_position=(rB, rr))
                    avmm[hd] = kc.inc(ins, 'st')

            @block.vector
            def _(v):
                kc.w(v, 'st')
                ins = v.tensor_copy(out=kc.t['ctxb'][:, :], in_=psB[:, 0:1024])
                kc.inc(ins, 'sv')

            @block.tensor
            def _(tn):
                kc.w(tn, 'sv')
                if inst == 0:
                    kc.w(tn, 'dq', ld_wo)
                for c in range(8):
                    for half in range(2):
                        ins = tn.matmul(out=psB[:, 1024 + 512 * half:1024 + 512 * (half + 1)],
                                        lhsT=kc.t['ctxb'][:, 128 * c:128 * (c + 1)],
                                        rhs=kc.t['wbuf'][:, 1024 * c + 512 * half:
                                                         1024 * c + 512 * (half + 1)],
                                        start=(c == 0), stop=(c == 7))
                kc.inc(ins, 'st')

            @block.vector
            def _(v):
                kc.w(v, 'st')
                t = 2 * s + qt
                ins = v.scalar_tensor_tensor(out=kc.t['xsb'][:, 1024 * t:1024 * (t + 1)],
                                             in0=psB[:, 1024:2048], scalar=1.0,
                                             in1=kc.t['xsb'][:, 1024 * t:1024 * (t + 1)],
                                             op0=ALU.mult, op1=ALU.add)
                kc.inc(ins, 'sv')


def _dbg_dump(kc, name, cols=16384):
    block = kc.block

    @block.vector
    def _(v):
        kc.w(v, 'st'); kc.w(v, 'ss'); kc.w(v, 'sg'); kc.w(v, 'dq'); kc.w(v, 'ds')
        ins = v.tensor_copy(out=kc.t['xsb'][:, 0:cols], in_=kc.t[name][:, 0:cols])
        kc.inc(ins, 'sv')


def run_ffn(kc, h, tag):
    block = kc.block
    dbg = getattr(kc, 'dbg', None)
    psA, psB = kc.t['psA'], kc.t['psB']
    dram = kc.dram
    UTM, UFM2, SQ2, UT2 = 32768, 65536, 98304, 102400
    WIN, QKV, TAT, CTXF = 106496, 118784, 135168, 147456
    XFM = 65536
    MQ, MK = 180224, 184320
    W1U, W2U, TMPY = 180224, 184320, 188416
    M0 = 192512
    kc.reg('utm', UTM, 16 * 1024, BF16)
    kc.reg('ufm2', UFM2, 8 * 2048, BF16)
    kc.reg('sqd2', SQ2, 1024, F32)
    kc.reg('winb', WIN, 2 * 3 * 1024, BF16)
    kc.reg('qj', QKV, 2048, BF16)
    kc.reg('kj', QKV + 4096, 2048, BF16)
    kc.reg('vj', QKV + 8192, 2048, BF16)
    kc.reg('vtj', QKV + 12288, 2048, BF16)
    kc.reg('Ajh', TAT, 1024, BF16)
    kc.reg('Atb', TAT + 2048, 1024, BF16)
    kc.reg('diags', TAT + 4096, 8 * 128, BF16)
    kc.reg('ctxf', CTXF, 8 * 2048, BF16)
    kc.reg('xfm', XFM, 8 * 2048, BF16)
    kc.reg('mqs', MQ, 2048, BF16, rows=17)
    kc.reg('mks', MK, 2048, BF16, rows=17)
    kc.reg('woutb', WIN, 8 * 1024, BF16)
    kc.reg('g2B', M0, 1024, BF16)
    kc.reg('b2B', M0 + 2048, 1024, BF16)
    kc.reg('b2a', M0 + 4096, 1024, BF16, rows=8)
    kc.reg('b1cs', M0 + 6144, 256, F32)
    kc.reg('bincs', M0 + 7168, 24, F32)
    kc.reg('binq', M0 + 7296, 8, F32)
    kc.reg('g1cs', M0 + 7360, 8, F32)
    kc.reg('b1ls', M0 + 7424, 8, F32)
    kc.reg('gwTs', M0 + 7488, 64, BF16)
    kc.reg('glog', M0 + 7680, 128, F32)
    kc.reg('comb', M0 + 8192, 128, F32)
    kc.reg('combT', M0 + 8704, 128, BF16, rows=8)
    kc.reg('outbr', M0 + 8960, 1024, BF16, rows=1)
    kc.reg('gbr', M0 + 11008, 8, BF16, rows=1)
    kc.reg('combB', M0 + 11072, 128, BF16)
    kc.reg('w2f', 106496, 32 * 1024, BF16)
    kc.reg('w1u', 172032, 4 * 2048, BF16)
    kc.reg('tmpy', 188416, 1024, F32)
    kc.reg('mtile', 106496, 2 * 1024, F32)
    kc.reg('t2all', 114688, 16 * 1024, F32)

    def hview(j):
        if j < 8:
            off = 4096 * j
        elif j < 16:
            off = 32768 + 4096 * (j - 8)
        elif j < 26:
            off = 106496 + 4096 * (j - 16)
        else:
            off = 147456 + 4096 * (j - 26)
        return kc.arena[0:128, off // 2:off // 2 + 2048]

    @block.sync
    def _(sy):
        kc.w(sy, 'st')
        kc.w(sy, 'sv')
        kc.w(sy, 'ss')
        for name, hh2 in [('mqs', 'mq'), ('mks', 'mk'), ('b1cs', 'b1c'), ('bincs', 'binc'),
                          ('g1cs', 'g1c'), ('b1ls', 'b1lc')]:
            kc.inc(sy.dma_start(out=kc.t[name][:, :], in_=h[hh2].ap()[:, :]), 'ds', 16)
    ld_c1 = kc.n['ds']

    @block.gpsimd
    def _(g):
        kc.w(g, 'st')
        kc.w(g, 'sv')
        for name, hh2 in [('g2B', 'g2B'), ('b2B', 'b2B'), ('b2a', 'b2a'),
                          ('outbr', 'outb_row'), ('gbr', 'gb_row')]:
            kc.inc(g.dma_start(out=kc.t[name][:, :], in_=h[hh2].ap()[:, :]), 'dq', 16)
        kc.inc(g.dma_start(out=kc.t['gwTs'][:, :].rearrange("p (c e) -> p c e", e=8),
                           in_=h['gwT'].ap().rearrange("(c p) e -> p c e", p=128)), 'dq', 16)
        for oc in (0, 8, 16):
            kc.inc(g.dma_start(
                out=kc.t['winb'][:, 1024 * (oc // 8):1024 * (oc // 8 + 1)]
                    .rearrange("p (c f) -> p c f", f=128),
                in_=h['winU'].ap()[oc, :, :].rearrange("(c p) f -> p c f", p=128)), 'dq', 16)
    ld_u0 = kc.n['dq']

    @block.vector
    def _(v):
        kc.w(v, 'ds', ld_c1)
        ins = v.tensor_scalar(out=kc.t['binq'][:, :], in0=kc.t['bincs'][:, 0:8],
                              scalar1=-0.125, scalar2=None, op0=ALU.mult)
        kc.inc(ins, 'sv')

    kc.scp = kc.nc.enter_named_scope('ffn_ln', False)[0]; kc.scpn = 'ffn_ln'
    M1, R1 = emit_ln_stats(kc, 'xsb', list(range(NT)), 0, 'sqd2')
    evs = []
    for t in range(NT):
        @block.vector
        def _(v):
            kc.w(v, 'ss')
            ins = v.tensor_scalar(out=kc.t['utm'][:, 1024 * t:1024 * (t + 1)],
                                  in0=kc.t['xsb'][:, 1024 * t:1024 * (t + 1)],
                                  scalar1=kc.t['stats'][:, M1 + t:M1 + t + 1],
                                  scalar2=kc.t['stats'][:, R1 + t:R1 + t + 1],
                                  op0=ALU.subtract, op1=ALU.mult)
            kc.inc(ins, 'sv')

        @block.tensor
        def _(tn):
            kc.w(tn, 'sv')
            if t >= 2:
                kc.w(tn, 'sv', evs[t - 2])
            for f in range(8):
                ins = tn.matmul(out=psA[:, 1024 * (t % 2) + 128 * f:1024 * (t % 2) + 128 * (f + 1)],
                                lhsT=kc.t['utm'][:, 1024 * t + 128 * f:1024 * t + 128 * (f + 1)],
                                rhs=kc.t['idnS'][:, :], start=True, stop=True)
            kc.inc(ins, 'st')

        @block.vector
        def _(v):
            kc.w(v, 'st')
            ov = kc.t['ufm2'][:, :].rearrange("p (f a) -> p f a", a=2048)[:, :, 128 * t:128 * (t + 1)]
            iv = psA[:, 1024 * (t % 2):1024 * (t % 2 + 1)].rearrange("p (f a) -> p f a", a=128)
            ins = v.tensor_copy(out=ov, in_=iv)
            kc.inc(ins, 'sv')
        evs.append(kc.n['sv'])
    ufm2_done = kc.n['sv']
    kc.nc.leave_named_scope(kc.scpn, kc.scp, False)
    kc.scp = kc.nc.enter_named_scope('ffn_inp_attn', False)[0]; kc.scpn = 'ffn_inp_attn'
    if dbg == 'src':
        return _dbg_dump(kc, 'utm')

    for j in range(8):
        if j < 7:
            @block.gpsimd
            def _(g):
                kc.w(g, 'st')
                for pl, oc in enumerate((j + 1, 9 + j, 17 + j)):
                    kc.inc(g.dma_start(
                        out=kc.t['winb'][:, 3072 * ((j + 1) % 2) + 1024 * pl:
                                         3072 * ((j + 1) % 2) + 1024 * (pl + 1)]
                            .rearrange("p (c f) -> p c f", f=128),
                        in_=h['winU'].ap()[oc, :, :].rearrange("(c p) f -> p c f", p=128)), 'dq', 16)

        u2 = kc.t['ufm2']
        for which, dst, pX, scale in [(0, 'qj', psA, -0.125), (1, 'kj', psB, 1.0), (2, 'vj', psA, 1.0)]:
            @block.tensor
            def _(tn):
                if j == 0 and which == 0:
                    kc.w(tn, 'dq', ld_u0)
                    kc.w(tn, 'sv', ufm2_done)
                elif which == 0 and j > 0:
                    kc.w(tn, 'dq')
                kc.w(tn, 'ss')
                plane = 3072 * (j % 2) + 1024 * which
                for c in range(8):
                    for g4 in range(4):
                        ins = tn.matmul(out=pX[:, 512 * g4:512 * (g4 + 1)],
                                        lhsT=kc.t['winb'][:, plane + 128 * c:plane + 128 * (c + 1)],
                                        rhs=u2[:, 2048 * c + 512 * g4:2048 * c + 512 * (g4 + 1)],
                                        start=(c == 0), stop=(c == 7))
                kc.inc(ins, 'st')

            @block.vector
            def _(v):
                kc.w(v, 'st')
                kc.w(v, 'ss')
                bias_ap = kc.t['binq'][:, j:j + 1] if which == 0 else \
                    kc.t['bincs'][:, 8 * which + j:8 * which + j + 1]
                iv = pX[:, 0:2048].rearrange("p (s T c) -> p s T c", s=8, c=16)
                ov = kc.t[dst][:, :].rearrange("p (T s c) -> p s T c", T=16, c=16)
                ins = v.tensor_scalar(out=ov, in0=iv, scalar1=bias_ap, scalar2=scale,
                                      op0=ALU.add, op1=ALU.mult)
                kc.inc(ins, 'ss')

        @block.tensor
        def _(tn):
            kc.w(tn, 'ss')
            for T in range(16):
                ins = tn.matmul(out=psB[:, 128 * T:128 * (T + 1)],
                                lhsT=kc.t['vj'][:, 128 * T:128 * (T + 1)],
                                rhs=kc.t['idnS'][:, :], start=True, stop=True)
            kc.inc(ins, 'st')

        @block.vector
        def _(v):
            kc.w(v, 'st')
            ins = v.tensor_copy(out=kc.t['vtj'][:, :], in_=psB[:, 0:2048])
            kc.inc(ins, 'sv')

        for H2 in range(2):
            for hh in range(2):
                rr = 64 * hh

                @block.tensor
                def _(tn):
                    kc.w(tn, 'sv')
                    kc.w(tn, 'ss')
                    for Tl in range(8):
                        T = 8 * H2 + Tl
                        tn.matmul(out=psA[:, 128 * Tl:128 * (Tl + 1)],
                                  lhsT=kc.t['mqs'][:, 128 * T:128 * (T + 1)],
                                  rhs=kc.t['mks'][:, 128 * T:128 * (T + 1)], start=True, stop=False)
                        ins = tn.matmul(out=psA[:, 128 * Tl:128 * (Tl + 1)],
                                        lhsT=kc.t['qj'][rr:rr + 64, 128 * T:128 * (T + 1)],
                                        rhs=kc.t['kj'][rr:rr + 64, 128 * T:128 * (T + 1)],
                                        start=False, stop=True)
                    kc.inc(ins, 'st')

                @block.vector
                def _(v):
                    kc.w(v, 'st')
                    ins = v.tensor_reduce(out=kc.t['stats'][:, 120:121], in_=psA[:, 0:1024],
                                          axis=AX.X, op=ALU.min)
                    kc.inc(ins, 'sv')

                @block.scalar
                def _(sc):
                    kc.w(sc, 'sv')
                    ins = sc.activation(out=kc.t['Ajh'][:, 0:1024], in_=psA[:, 0:1024],
                                        func=AF.Exp, scale=-1.0, bias=kc.t['stats'][:, 120:121])
                    kc.inc(ins, 'ss')

                @block.vector
                def _(v):
                    kc.w(v, 'ss')
                    ins = v.reduce_sum(out=kc.t['stats'][:, 104:112],
                                       in_=kc.t['Ajh'][:, 0:1024].rearrange("p (T c) -> p T c", c=128),
                                       axis=AX.X)
                    kc.inc(ins, 'sv')
                    kc.w(v, 'sv')
                    ins = v.reciprocal(out=kc.t['stats'][:, 112:120], in_=kc.t['stats'][:, 104:112])
                    kc.inc(ins, 'sv')
                    kc.w(v, 'sv')
                    for Tl in range(8):
                        ins = v.tensor_scalar(out=kc.t['diags'][:, 128 * Tl:128 * (Tl + 1)],
                                              in0=kc.t['idnS'][:, :],
                                              scalar1=kc.t['stats'][:, 112 + Tl:113 + Tl],
                                              scalar2=None, op0=ALU.mult)
                    kc.inc(ins, 'sv')

                @block.tensor
                def _(tn):
                    kc.w(tn, 'sv')
                    for Tl in range(8):
                        ins = tn.matmul(out=psB[:, 128 * Tl:128 * (Tl + 1)],
                                        lhsT=kc.t['Ajh'][:, 128 * Tl:128 * (Tl + 1)],
                                        rhs=kc.t['diags'][:, 128 * Tl:128 * (Tl + 1)],
                                        start=True, stop=True)
                    kc.inc(ins, 'st')

                @block.scalar
                def _(sc):
                    kc.w(sc, 'st')
                    ins = sc.activation(out=kc.t['Atb'][:, 0:1024], in_=psB[:, 0:1024], func=AF.Relu)
                    kc.inc(ins, 'ss')

                @block.tensor
                def _(tn):
                    kc.w(tn, 'ss')
                    for Tl in range(8):
                        T = 8 * H2 + Tl
                        ins = tn.matmul(out=psB[rr:rr + 64, 1024 + 128 * Tl:1024 + 128 * (Tl + 1)],
                                        lhsT=kc.t['vtj'][:, 128 * T + 64 * hh:128 * T + 64 * (hh + 1)],
                                        rhs=kc.t['Atb'][:, 128 * Tl:128 * (Tl + 1)],
                                        start=True, stop=True, tile_position=(0, rr))
                    kc.inc(ins, 'st')

            @block.vector
            def _(v):
                kc.w(v, 'st')
                iv = psB[:, 1024:2048].rearrange("p (T s c) -> p T s c", T=8, c=16)
                ov = kc.t['ctxf'][:, 2048 * j:2048 * (j + 1)] \
                    .rearrange("p (s h T c) -> p h T s c", s=8, h=2, c=16)[:, H2]
                ins = v.tensor_copy(out=ov, in_=iv)
                kc.inc(ins, 'sv')
    ctx_done = kc.n['sv']
    kc.nc.leave_named_scope(kc.scpn, kc.scp, False)
    kc.scp = kc.nc.enter_named_scope('ffn_outproj_ln1_gate', False)[0]; kc.scpn = 'ffn_outproj_ln1_gate'
    if dbg == 'ctx':
        return _dbg_dump(kc, 'ctxf')
    if dbg == 'qkv':
        return _dbg_dump(kc, 'qj', 2048)

    @block.gpsimd
    def _(g):
        kc.w(g, 'st')
        kc.inc(g.dma_start(out=kc.t['woutb'][:, :].rearrange("p (c f) -> p c f", f=1024),
                           in_=h['woutT'].ap().rearrange("(c p) f -> p c f", p=128)), 'dq', 16)
    ld_wo2 = kc.n['dq']

    ev_o = []
    for t in range(NT):
        @block.tensor
        def _(tn):
            if t == 0:
                kc.w(tn, 'dq', ld_wo2)
                kc.w(tn, 'sv', ctx_done)
            if t >= 2:
                kc.w(tn, 'sv', ev_o[t - 2])
            reg = t % 2
            for c in range(8):
                for half in range(2):
                    tn.matmul(out=psA[:, 1024 * reg + 512 * half:1024 * reg + 512 * (half + 1)],
                              lhsT=kc.t['ctxf'][:, 2048 * c + 128 * t:2048 * c + 128 * (t + 1)],
                              rhs=kc.t['woutb'][:, 1024 * c + 512 * half:1024 * c + 512 * (half + 1)],
                              start=(c == 0), stop=False)
            for half in range(2):
                ins = tn.matmul(out=psA[:, 1024 * reg + 512 * half:1024 * reg + 512 * (half + 1)],
                                lhsT=kc.t['ones_row'][:, :],
                                rhs=kc.t['outbr'][:, 512 * half:512 * (half + 1)],
                                start=False, stop=True)
            kc.inc(ins, 'st')

        @block.vector
        def _(v):
            kc.w(v, 'st')
            reg = t % 2
            ins = v.scalar_tensor_tensor(out=kc.t['utm'][:, 1024 * t:1024 * (t + 1)],
                                         in0=psA[:, 1024 * reg:1024 * (reg + 1)], scalar=1.0,
                                         in1=kc.t['utm'][:, 1024 * t:1024 * (t + 1)],
                                         op0=ALU.mult, op1=ALU.add)
            kc.inc(ins, 'sv')
        ev_o.append(kc.n['sv'])

    if dbg == 's1':
        return _dbg_dump(kc, 'utm')
    M2c, R2c = emit_ln_stats(kc, 'utm', list(range(NT)), 0, 'sqd2')
    kc.reg('utmp2', UT2, 2048, BF16)
    ev_x = []
    for t in range(NT):
        @block.vector
        def _(v):
            kc.w(v, 'ss')
            if t >= 2:
                kc.w(v, 'st', ev_x[t - 2][0])
            ins = v.tensor_scalar(out=kc.t['utmp2'][:, 1024 * (t % 2):1024 * (t % 2 + 1)],
                                  in0=kc.t['utm'][:, 1024 * t:1024 * (t + 1)],
                                  scalar1=kc.t['stats'][:, M2c + t:M2c + t + 1],
                                  scalar2=kc.t['stats'][:, R2c + t:R2c + t + 1],
                                  op0=ALU.subtract, op1=ALU.mult)
            kc.inc(ins, 'sv')

        @block.tensor
        def _(tn):
            kc.w(tn, 'sv')
            if t >= 2:
                kc.w(tn, 'sv', ev_x[t - 2][1])
            for f in range(8):
                ins = tn.matmul(out=psB[:, 1024 * (t % 2) + 128 * f:1024 * (t % 2) + 128 * (f + 1)],
                                lhsT=kc.t['utmp2'][:, 1024 * (t % 2) + 128 * f:
                                                   1024 * (t % 2) + 128 * (f + 1)],
                                rhs=kc.t['idnS'][:, :], start=True, stop=True)
            kc.inc(ins, 'st')
        st_t = kc.n['st']

        @block.vector
        def _(v):
            kc.w(v, 'st')
            for f in range(8):
                ins = v.tensor_scalar(
                    out=kc.t['xfm'][:, :].rearrange("p (f a) -> p f a", a=2048)[
                        :, f, 128 * t:128 * (t + 1)],
                    in0=psB[:, 1024 * (t % 2) + 128 * f:1024 * (t % 2) + 128 * (f + 1)],
                    scalar1=kc.t['g1cs'][:, f:f + 1], scalar2=kc.t['b1ls'][:, f:f + 1],
                    op0=ALU.mult, op1=ALU.add)
            kc.inc(ins, 'sv')
        ev_x.append((st_t, kc.n['sv']))
    xfm_done = kc.n['sv']

    if dbg == 'xin':
        return _dbg_dump(kc, 'xfm')
    for t in range(NT):
        @block.tensor
        def _(tn):
            if t == 0:
                kc.w(tn, 'sv', xfm_done)
                kc.w(tn, 'dq')
            if t >= 4:
                kc.w(tn, 'sv')
            for c in range(8):
                ins = tn.matmul(out=psA[:, 8 * t:8 * (t + 1)],
                                lhsT=kc.t['xfm'][:, 2048 * c + 128 * t:2048 * c + 128 * (t + 1)],
                                rhs=kc.t['gwTs'][:, 8 * c:8 * (c + 1)],
                                start=(c == 0), stop=False)
            ins = tn.matmul(out=psA[:, 8 * t:8 * (t + 1)],
                            lhsT=kc.t['ones_row'][:, :], rhs=kc.t['gbr'][:, :],
                            start=False, stop=True)
            kc.inc(ins, 'st')

    @block.vector
    def _(v):
        kc.w(v, 'st')
        ins = v.tensor_copy(out=kc.t['glog'][:, :], in_=psA[:, 0:128])
        kc.inc(ins, 'sv')
        kc.w(v, 'sv')
        gl = kc.t['glog'][:, :].rearrange("p (T e) -> p T e", e=8)
        ins = v.reduce_max(out=kc.t['stats'][:, 0:16], in_=gl, axis=AX.X)
        kc.inc(ins, 'sv')
        kc.w(v, 'sv')
        m1b = kc.t['stats'][:, 0:16].unsqueeze(2).broadcast_to((128, 16, 8))
        ins = v.tensor_tensor(out=kc.t['comb'][:, :].rearrange("p (T e) -> p T e", e=8),
                              in0=gl, in1=m1b, op=ALU.is_equal)
        kc.inc(ins, 'sv')
        kc.w(v, 'sv')
        ins = v.scalar_tensor_tensor(out=kc.t['sqd2'][:, 0:128].rearrange("p (T e) -> p T e", e=8),
                                     in0=kc.t['comb'][:, :].rearrange("p (T e) -> p T e", e=8),
                                     scalar=-30000.0, in1=gl, op0=ALU.mult, op1=ALU.add)
        kc.inc(ins, 'sv')
        kc.w(v, 'sv')
        ins = v.reduce_max(out=kc.t['stats'][:, 16:32],
                           in_=kc.t['sqd2'][:, 0:128].rearrange("p (T e) -> p T e", e=8),
                           axis=AX.X)
        kc.inc(ins, 'sv')
        kc.w(v, 'sv')
        m1b2 = kc.t['stats'][:, 0:16].unsqueeze(2).broadcast_to((128, 16, 8))
        ins = v.tensor_tensor(out=kc.t['sqd2'][:, 0:128].rearrange("p (T e) -> p T e", e=8),
                              in0=gl, in1=m1b2, op=ALU.subtract)
        kc.inc(ins, 'sv')

    @block.scalar
    def _(sc):
        kc.w(sc, 'sv')
        ins = sc.activation(out=kc.t['sqd2'][:, 0:128], in_=kc.t['sqd2'][:, 0:128], func=AF.Exp)
        kc.inc(ins, 'ss')

    @block.vector
    def _(v):
        kc.w(v, 'ss')
        gl = kc.t['glog'][:, :].rearrange("p (T e) -> p T e", e=8)
        m2b = kc.t['stats'][:, 16:32].unsqueeze(2).broadcast_to((128, 16, 8))
        ins = v.tensor_tensor(out=kc.t['comb'][:, :].rearrange("p (T e) -> p T e", e=8),
                              in0=gl, in1=m2b, op=ALU.is_ge)
        kc.inc(ins, 'sv')
        kc.w(v, 'sv')
        ins = v.tensor_tensor(out=kc.t['comb'][:, :], in0=kc.t['comb'][:, :],
                              in1=kc.t['sqd2'][:, 0:128], op=ALU.mult)
        kc.inc(ins, 'sv')
        kc.w(v, 'sv')
        ins = v.reduce_sum(out=kc.t['stats'][:, 32:48],
                           in_=kc.t['comb'][:, :].rearrange("p (T e) -> p T e", e=8), axis=AX.X)
        kc.inc(ins, 'sv')
        kc.w(v, 'sv')
        ins = v.reciprocal(out=kc.t['stats'][:, 48:64], in_=kc.t['stats'][:, 32:48])
        kc.inc(ins, 'sv')
        kc.w(v, 'sv')
        rb = kc.t['stats'][:, 48:64].unsqueeze(2).broadcast_to((128, 16, 8))
        ins = v.tensor_tensor(out=kc.t['comb'][:, :].rearrange("p (T e) -> p T e", e=8),
                              in0=kc.t['comb'][:, :].rearrange("p (T e) -> p T e", e=8),
                              in1=rb, op=ALU.mult)
        kc.inc(ins, 'sv')
    comb_done = kc.n['sv']

    if dbg == 'comb':
        return _dbg_dump(kc, 'comb', 128)
    @block.vector
    def _(v):
        kc.w(v, 'sv', comb_done)
        ins = v.tensor_copy(out=kc.t['combB'][:, :], in_=kc.t['comb'][:, :])
        kc.inc(ins, 'sv')
    combB_done = kc.n['sv']

    ev_b2 = []
    tmpy_dq = []
    for t in range(NT):
        @block.tensor
        def _(tn):
            kc.w(tn, 'sv', combB_done)
            if t >= 2:
                kc.w(tn, 'ss', ev_b2[t - 2])
            ins = tn.matmul(out=psB[0:8, 128 * (t % 2):128 * (t % 2) + 128],
                            lhsT=kc.t['combB'][:, 8 * t:8 * (t + 1)],
                            rhs=kc.t['idnS'][:, :], start=True, stop=True)
            kc.inc(ins, 'st')

        @block.vector
        def _(v):
            kc.w(v, 'st')
            ins = v.tensor_copy(out=kc.t['combT'][:, :],
                                in_=psB[0:8, 128 * (t % 2):128 * (t % 2) + 128])
            kc.inc(ins, 'sv')

        @block.tensor
        def _(tn):
            kc.w(tn, 'sv')
            for half in range(2):
                ins = tn.matmul(out=psA[:, 1024 * (t % 2) + 512 * half:
                                        1024 * (t % 2) + 512 * (half + 1)],
                                lhsT=kc.t['combT'][:, :],
                                rhs=kc.t['b2a'][:, 512 * half:512 * (half + 1)],
                                start=True, stop=True)
            kc.inc(ins, 'st')

        @block.vector
        def _(v):
            kc.w(v, 'st')
            if t >= 1:
                kc.w(v, 'dq', tmpy_dq[t - 1])
            ins = v.tensor_copy(out=kc.t['tmpy'][:, 0:1024],
                                in_=psA[:, 1024 * (t % 2):1024 * (t % 2 + 1)])
            ev_b2.append(kc.inc(ins, 'ss'))

        @block.gpsimd
        def _(g):
            kc.w(g, 'ss')
            kc.inc(g.dma_start(out=dram['moeacc'].ap()[:, 1024 * t:1024 * (t + 1)],
                               in_=kc.t['tmpy'][:, 0:1024]), 'dq', 16)
        tmpy_dq.append(kc.n['dq'])

    kc.nc.leave_named_scope(kc.scpn, kc.scp, False)
    kc.scp = kc.nc.enter_named_scope('ffn_moe', False)[0]; kc.scpn = 'ffn_moe'
    @block.sync
    def _(sy):
        kc.w(sy, 'sv')
        kc.inc(sy.dma_start(out=dram['spill'].ap()[:, :], in_=kc.t['xsb'][:, :]), 'ds', 16)
    spill_done = kc.n['ds']

    def hview(j):
        return kc.arena[0:128, (2048 * j) // 2:(2048 * j) // 2 + 1024]

    w1_consumed = {}
    w1_ld = {}
    hev = {}
    yev = {}
    w2_ld = {}
    y_done = {}
    last_tmpy_dq = [tmpy_dq[-1]]
    PSH = [psA, psB]

    def hreg(j):
        t_ = PSH[(j % 4) // 2]
        off = 1024 * (j % 2)
        return t_[:, off:off + 1024]

    for e in range(8):
        for th in range(2):
            for u in range(16):
                @block.gpsimd
                def _(g):
                    prev = (e, th, u - 4) if u >= 4 else \
                        ((e, th - 1, 12 + u) if th > 0 else
                         ((e - 1, 1, 12 + u) if e > 0 else None))
                    if prev is not None and prev in w1_consumed:
                        kc.w(g, 'st', w1_consumed[prev])
                    elif e == 0 and th == 0 and u == 0:
                        kc.w(g, 'ds', spill_done)
                        kc.w(g, 'sv')
                        kc.w(g, 'st')
                    kc.inc(g.dma_start(
                        out=kc.t['w1u'][:, 2048 * (u % 4):2048 * (u % 4 + 1)]
                            .rearrange("p (j x) -> p j x", j=2),
                        in_=h['w1U'].ap()[e, 2 * u:2 * u + 2, :, :]
                            .rearrange("j (c p) f -> p j c f", p=128)), 'dq', 16)
                    w1_ld[(e, th, u)] = kc.n['dq']
                    if th == 0:
                        if e == 0 and u == 0:
                            kc.w(g, 'st')
                        elif u == 0:
                            kc.w(g, 'st', y_done[e - 1])
                        kc.inc(g.dma_start(
                            out=kc.t['w2f'][:, 2048 * u:2048 * (u + 1)]
                                .rearrange("p (c f) -> p c f", f=1024),
                            in_=h['w2T'].ap()[e, 256 * u:256 * (u + 1), :]
                                .rearrange("(c p) f -> p c f", p=128)), 'dq', 16)
                        w2_ld[(e, u)] = kc.n['dq']

                for jj in range(2):
                    j = 2 * u + jj

                    @block.tensor
                    def _(tn):
                        kc.w(tn, 'dq', w1_ld[(e, th, u)])
                        if e == 0 and th == 0 and j == 0:
                            kc.w(tn, 'ds', spill_done)
                            kc.w(tn, 'sv')
                            kc.w(tn, 'ss')
                        prev4 = (th, j - 4) if j >= 4 else ((th - 1, 28 + j) if th > 0 else None)
                        if prev4 is not None and prev4 in hev:
                            kc.w(tn, 'ss', hev[prev4])
                        pj = hreg(j)
                        for c in range(8):
                            for g4 in range(2):
                                ins = tn.matmul(
                                    out=pj[:, 512 * g4:512 * (g4 + 1)],
                                    lhsT=kc.t['w1u'][:, 2048 * (u % 4) + 1024 * jj + 128 * c:
                                                     2048 * (u % 4) + 1024 * jj + 128 * (c + 1)],
                                    rhs=kc.t['xfm'][:, 2048 * c + 1024 * th + 512 * g4:
                                                    2048 * c + 1024 * th + 512 * (g4 + 1)],
                                    start=(c == 0), stop=(c == 7))
                        if jj == 1:
                            w1_consumed[(e, th, u)] = kc.inc(ins, 'st')
                        else:
                            kc.inc(ins, 'st')

                    @block.scalar
                    def _(sc):
                        kc.w(sc, 'st')
                        ins = sc.activation(out=hview(j), in_=hreg(j), func=AF.Relu,
                                            bias=kc.t['b1cs'][:, 32 * e + j:32 * e + j + 1])
                        hev[(th, j)] = kc.inc(ins, 'ss')

            for tl in range(8):
                @block.tensor
                def _(tn):
                    if tl == 0:
                        kc.w(tn, 'ss', hev[(th, 31)])
                        kc.w(tn, 'dq', w2_ld[(e, 15)])
                    slot = tl % 4
                    pt_ = PSH[slot // 2][:, 1024 * (slot % 2):1024 * (slot % 2) + 1024]
                    pidx = (e, th, tl - 4) if tl >= 4 else \
                        ((e, th - 1, 4 + tl) if th > 0 else ((e - 1, 1, 4 + tl) if e > 0 else None))
                    if pidx is not None and pidx in yev:
                        kc.w(tn, 'ss', yev[pidx])
                    for j2 in range(32):
                        for half in range(2):
                            ins = tn.matmul(
                                out=pt_[:, 512 * half:512 * (half + 1)],
                                lhsT=hview(j2)[:, 128 * tl:128 * (tl + 1)],
                                rhs=kc.t['w2f'][:, 1024 * j2 + 512 * half:
                                                1024 * j2 + 512 * (half + 1)],
                                start=(j2 == 0), stop=(j2 == 31))
                    kc.inc(ins, 'st')
                if th == 1 and tl == 7:
                    y_done[e] = kc.n['st']

                @block.vector
                def _(v):
                    kc.w(v, 'st')
                    kc.w(v, 'dq', last_tmpy_dq[0])
                    slot = tl % 4
                    pt_ = PSH[slot // 2][:, 1024 * (slot % 2):1024 * (slot % 2) + 1024]
                    t = 8 * th + tl
                    ins = v.tensor_scalar(out=kc.t['tmpy'][:, 0:1024],
                                          in0=pt_, scalar1=kc.t['comb'][:, 8 * t + e:8 * t + e + 1],
                                          scalar2=None, op0=ALU.mult)
                    yev[(e, th, tl)] = kc.inc(ins, 'ss')

                @block.gpsimd
                def _(g):
                    kc.w(g, 'ss', yev[(e, th, tl)])
                    t = 8 * th + tl
                    kc.inc(g.dma_start(out=dram['moeacc'].ap()[:, 1024 * t:1024 * (t + 1)],
                                       in_=kc.t['tmpy'][:, 0:1024],
                                       accum_op=ALU.add), 'dq', 16)
                last_tmpy_dq[0] = kc.n['dq']
    moe_done_dq = kc.n['dq']

    @block.sync
    def _(sy):
        kc.w(sy, 'st')
        kc.inc(sy.dma_start(out=kc.t['xsb'][:, :], in_=dram['spill'].ap()[:, :]), 'ds', 16)
    unspill_done = kc.n['ds']

    ev_f = []
    for t in range(NT):
        @block.gpsimd
        def _(g):
            if t == 0:
                kc.w(g, 'dq', moe_done_dq)
            if t >= 2:
                kc.w(g, 'sv')
            kc.inc(g.dma_start(out=kc.t['mtile'][:, 1024 * (t % 2):1024 * (t % 2 + 1)],
                               in_=dram['moeacc'].ap()[:, 1024 * t:1024 * (t + 1)]), 'dq', 16)

        @block.tensor
        def _(tn):
            if t == 0:
                kc.w(tn, 'ss')
            if t >= 2:
                kc.w(tn, 'sv', ev_f[t - 2])
            for f in range(8):
                ins = tn.matmul(out=psA[:, 1024 * (t % 2) + 128 * f:1024 * (t % 2) + 128 * (f + 1)],
                                lhsT=kc.t['xfm'][:, 2048 * f + 128 * t:2048 * f + 128 * (t + 1)],
                                rhs=kc.t['idnS'][:, :], start=True, stop=True)
            kc.inc(ins, 'st')

        @block.vector
        def _(v):
            kc.w(v, 'st')
            kc.w(v, 'dq')
            ins = v.scalar_tensor_tensor(out=kc.t['t2all'][:, 1024 * t:1024 * (t + 1)],
                                         in0=psA[:, 1024 * (t % 2):1024 * (t % 2 + 1)], scalar=1.0,
                                         in1=kc.t['mtile'][:, 1024 * (t % 2):1024 * (t % 2 + 1)],
                                         op0=ALU.mult, op1=ALU.add)
            kc.inc(ins, 'sv')
        ev_f.append(kc.n['sv'])

    if dbg == 'moe':
        return _dbg_dump(kc, 't2all')
    M3, R3 = emit_ln_stats(kc, 't2all', list(range(NT)), 0, 'sqd2')
    for t in range(NT):
        @block.vector
        def _(v):
            kc.w(v, 'ss')
            if t == 0:
                kc.w(v, 'ds', unspill_done)
            ins = v.tensor_scalar(out=kc.t['utmp2'][:, 1024 * (t % 2):1024 * (t % 2 + 1)],
                                  in0=kc.t['t2all'][:, 1024 * t:1024 * (t + 1)],
                                  scalar1=kc.t['stats'][:, M3 + t:M3 + t + 1],
                                  scalar2=kc.t['stats'][:, R3 + t:R3 + t + 1],
                                  op0=ALU.subtract, op1=ALU.mult)
            kc.inc(ins, 'sv')
            kc.w(v, 'sv')
            ins = v.tensor_tensor(out=kc.t['utmp2'][:, 1024 * (t % 2):1024 * (t % 2 + 1)],
                                  in0=kc.t['utmp2'][:, 1024 * (t % 2):1024 * (t % 2 + 1)],
                                  in1=kc.t['g2B'][:, :], op=ALU.mult)
            kc.inc(ins, 'sv')
            kc.w(v, 'sv')
            ins = v.tensor_tensor(out=kc.t['utmp2'][:, 1024 * (t % 2):1024 * (t % 2 + 1)],
                                  in0=kc.t['utmp2'][:, 1024 * (t % 2):1024 * (t % 2 + 1)],
                                  in1=kc.t['b2B'][:, :], op=ALU.add)
            kc.inc(ins, 'sv')
            kc.w(v, 'sv')
            ins = v.tensor_tensor(out=kc.t['xsb'][:, 1024 * t:1024 * (t + 1)],
                                  in0=kc.t['xsb'][:, 1024 * t:1024 * (t + 1)],
                                  in1=kc.t['utmp2'][:, 1024 * (t % 2):1024 * (t % 2 + 1)],
                                  op=ALU.add)
            kc.inc(ins, 'sv')
    kc.nc.leave_named_scope(kc.scpn, kc.scp, False)


def run_s3(kc, h, dram):
    block = kc.block
    psA, psB = kc.t['psA'], kc.t['psB']
    kc.reg('u2f', 49152, 8 * 2048, BF16)
    kc.reg('scr3', 197632, 2 * 1024, BF16)
    kc.reg('sqd3', 201728, 1024, F32)
    kc.reg('stg', 32768, 2 * 8 * 512, BF16)
    kc.reg('qfm', 49152, 16384, BF16)
    kc.reg('kfm3', 81920, 16384, BF16)
    kc.reg('vtm3', 114688, 128 * 64, BF16)
    kc.reg('pb3', 131072, 2 * 1280, BF16)
    kc.reg('pt3', 136192, 2 * 1280, BF16)
    kc.reg('dg3', 141312, 2 * 128, BF16)
    kc.reg('ctxall', 141824, 16384, BF16)
    kc.reg('wfb', 174592, 8 * 1024, BF16)
    kc.reg('mcauB', 190976, 128, BF16)
    kc.reg('mantB', 191232, 128, BF16)
    kc.reg('bfr', 191488, 1024, BF16, rows=1)
    kc.reg('wqs', 193536, 8 * 128, BF16)
    kc.reg('wks', 195584, 8 * 128, BF16)
    kc.reg('wvs', 205824, 8 * 64, BF16)
    kc.reg('ctxf3', 49152, 8 * 2048, BF16)

    @block.sync
    def _(sy):
        kc.w(sy, 'st'); kc.w(sy, 'sv'); kc.w(sy, 'ss')
        kc.inc(sy.dma_start(out=kc.t['mcauB'][:, :], in_=h['mcaus'].ap()[:, :]), 'ds', 16)
        kc.inc(sy.dma_start(out=kc.t['mantB'][:, :], in_=h['manti'].ap()[:, :]), 'ds', 16)

    @block.gpsimd
    def _(g):
        kc.w(g, 'st'); kc.w(g, 'sv'); kc.w(g, 'ss')
        kc.inc(g.dma_start(out=kc.t['bfr'][:, :], in_=h['bf_row'].ap()[:, :]), 'dq', 16)
        kc.inc(g.dma_start(out=kc.t['wqs'][:, :].rearrange("p (c f) -> p c f", f=128),
                           in_=h['wqbT'].ap().rearrange("(c p) f -> p c f", p=128)), 'dq', 16)
        kc.inc(g.dma_start(out=kc.t['wks'][:, :].rearrange("p (c f) -> p c f", f=128),
                           in_=h['wkbT2'].ap().rearrange("(c p) f -> p c f", p=128)), 'dq', 16)
        kc.inc(g.dma_start(out=kc.t['wvs'][:, :].rearrange("p (c f) -> p c f", f=64),
                           in_=h['wvbT'].ap().rearrange("(c p) f -> p c f", p=128)), 'dq', 16)
        kc.inc(g.dma_start(out=kc.t['wfb'][:, :].rearrange("p (c f) -> p c f", f=1024),
                           in_=h['wfT'].ap().rearrange("(c p) f -> p c f", p=128)), 'dq', 16)
    ld_s3 = kc.n['dq']

    kc.scp = kc.nc.enter_named_scope('s3_ln_ag', False)[0]; kc.scpn = 's3_ln_ag'
    M1, R1 = emit_ln_stats(kc, 'xsb', list(range(NT)), 0, 'sqd3')
    evs = []
    for t in range(NT):
        @block.vector
        def _(v):
            kc.w(v, 'ss')
            ins = v.tensor_scalar(out=kc.t['scr3'][:, 1024 * (t % 2):1024 * (t % 2 + 1)],
                                  in0=kc.t['xsb'][:, 1024 * t:1024 * (t + 1)],
                                  scalar1=kc.t['stats'][:, M1 + t:M1 + t + 1],
                                  scalar2=kc.t['stats'][:, R1 + t:R1 + t + 1],
                                  op0=ALU.subtract, op1=ALU.mult)
            kc.inc(ins, 'sv')

        @block.tensor
        def _(tn):
            kc.w(tn, 'sv')
            if t >= 2:
                kc.w(tn, 'sv', evs[t - 2])
            for f in range(8):
                ins = tn.matmul(out=psA[:, 1024 * (t % 2) + 128 * f:1024 * (t % 2) + 128 * (f + 1)],
                                lhsT=kc.t['scr3'][:, 1024 * (t % 2) + 128 * f:
                                                  1024 * (t % 2) + 128 * (f + 1)],
                                rhs=kc.t['idnS'][:, :], start=True, stop=True)
            kc.inc(ins, 'st')

        @block.vector
        def _(v):
            kc.w(v, 'st')
            ov = kc.t['u2f'][:, :].rearrange("p (f a) -> p f a", a=2048)[:, :, 128 * t:128 * (t + 1)]
            iv = psA[:, 1024 * (t % 2):1024 * (t % 2 + 1)].rearrange("p (f a) -> p f a", a=128)
            ins = v.tensor_copy(out=ov, in_=iv)
            kc.inc(ins, 'sv')
        evs.append(kc.n['sv'])

    @block.sync
    def _(sy):
        kc.w(sy, 'sv')
        for f in range(8):
            kc.inc(sy.dma_start(out=dram['u2c'].ap()[128 * f:128 * (f + 1), :],
                                in_=kc.t['u2f'][:, 2048 * f:2048 * (f + 1)]), 'ds', 16)
    contrib_done = kc.n['ds']

    @block.gpsimd
    def _(g):
        kc.w(g, 'ds', contrib_done)
        ins = g.collective_compute("AllGather", ALU.bypass,
                                   replica_groups=[list(range(8))],
                                   ins=[dram['u2c'].ap().opt()],
                                   outs=[dram['uall'].ap().opt()])
        kc.inc(ins, 'cc', 1)
    ag_done = kc.n['cc']

    kc.nc.leave_named_scope(kc.scpn, kc.scp, False)
    kc.scp = kc.nc.enter_named_scope('s3_proj', False)[0]; kc.scpn = 's3_proj'
    ev_p = []
    for g512 in range(32):
        r, cg = g512 // 4, g512 % 4

        @block.sync
        def _(sy):
            if g512 == 0:
                kc.w(sy, 'cc', ag_done)
            if g512 >= 2:
                kc.w(sy, 'st', ev_p[g512 - 2][0])
            for kcc in range(8):
                kc.inc(sy.dma_start(
                    out=kc.t['stg'][:, 4096 * (g512 % 2) + 512 * kcc:
                                    4096 * (g512 % 2) + 512 * (kcc + 1)],
                    in_=dram['uall'].ap()[1024 * r + 128 * kcc:1024 * r + 128 * (kcc + 1),
                                          512 * cg:512 * (cg + 1)]), 'ds', 16)

        @block.tensor
        def _(tn):
            kc.w(tn, 'ds')
            if g512 == 0:
                kc.w(tn, 'dq', ld_s3)
            if g512 >= 1:
                kc.w(tn, 'ss', ev_p[g512 - 1][1])
            base = 4096 * (g512 % 2)
            for kcc in range(8):
                tn.matmul(out=psB[:, 0:512],
                          lhsT=kc.t['wqs'][:, 128 * kcc:128 * (kcc + 1)],
                          rhs=kc.t['stg'][:, base + 512 * kcc:base + 512 * (kcc + 1)],
                          start=(kcc == 0), stop=(kcc == 7))
            for kcc in range(8):
                tn.matmul(out=psB[:, 512:1024],
                          lhsT=kc.t['wks'][:, 128 * kcc:128 * (kcc + 1)],
                          rhs=kc.t['stg'][:, base + 512 * kcc:base + 512 * (kcc + 1)],
                          start=(kcc == 0), stop=(kcc == 7))
            for ti in range(4):
                for kcc in range(8):
                    ins = tn.matmul(out=psB[:, 1024 + 64 * ti:1024 + 64 * (ti + 1)],
                                    lhsT=kc.t['stg'][:, base + 512 * kcc + 128 * ti:
                                                     base + 512 * kcc + 128 * (ti + 1)],
                                    rhs=kc.t['wvs'][:, 64 * kcc:64 * (kcc + 1)],
                                    start=(kcc == 0), stop=(kcc == 7))
            kc.inc(ins, 'st')
        ev_mm = kc.n['st']

        @block.vector
        def _(v):
            kc.w(v, 'st', ev_mm)
            v.tensor_scalar(out=kc.t['qfm'][:, 512 * g512:512 * (g512 + 1)],
                            in0=psB[:, 0:512], scalar1=-0.125, scalar2=None, op0=ALU.mult)
            v.tensor_copy(out=kc.t['kfm3'][:, 512 * g512:512 * (g512 + 1)],
                          in_=psB[:, 512:1024])
            ins = v.tensor_copy(out=kc.t['vtm3'][:, 256 * g512:256 * (g512 + 1)],
                                in_=psB[:, 1024:1280])
            kc.inc(ins, 'ss')
        ev_p.append((ev_mm, kc.n['ss']))
    proj_done = kc.n['ss']
    kc.nc.leave_named_scope(kc.scpn, kc.scp, False)
    kc.scp = kc.nc.enter_named_scope('s3_attn', False)[0]; kc.scpn = 's3_attn'
    if getattr(kc, 'dbg', None) == 'qb':
        return _dbg_dump(kc, 'qfm')
    if getattr(kc, 'dbg', None) == 'kb':
        return _dbg_dump(kc, 'kfm3')
    if getattr(kc, 'dbg', None) == 'vb':
        return _dbg_dump(kc, 'vtm3', 8192)

    insts = [(gq, s) for gq in range(16) for s in range(8)]
    SREG = [psA, psB]
    kc.reg('pb3b', 32768, 2 * 1280, BF16)
    kc.reg('dg3b', 38400, 2 * 128, BF16)
    kc.reg('pt3b', 40960, 2 * 1280, BF16)
    PBV = lambda n: kc.t['pb3'] if n % 2 == 0 else kc.t['pb3b']
    DGV = lambda n: kc.t['dg3'] if n % 2 == 0 else kc.t['dg3b']
    PTV = lambda n: kc.t['pt3'] if n % 2 == 0 else kc.t['pt3b']
    sc_mm = {}; min_ev = {}; exp_ev = {}; rc_ev = {}; dg_ev = {}
    ptA_mm = {}; pt_mm = {}; rA_ev = {}; rB_ev = {}; av_mm = {}; ctx_ev = {}
    for hd in range(2):
        for nn in (-2, -1):
            pt_mm[(nn, hd)] = kc.n['st']
            av_mm[(nn, hd)] = kc.n['st']
            rA_ev[(nn, hd)] = kc.n['ss']
            rB_ev[(nn, hd)] = kc.n['sv']
            dg_ev[(nn, hd)] = kc.n['sg']
    ctx_ev[-1] = kc.n['sv']

    def _kinfo(gq, s, nk, b):
        kc128 = gq - nk + 1 + b
        kcol = 2048 * (kc128 // 2) + 256 * s + 128 * (kc128 % 2)
        vtile = 16 * (kc128 // 2) + 2 * s + (kc128 % 2)
        return kcol, vtile

    def _emit_scores(n):
        gq, s = insts[n]
        nk = min(gq + 1, 9)
        qcol = 2048 * (gq // 2) + 256 * s + 128 * (gq % 2)
        for hd in range(2):
            @block.tensor
            def _(tn, gq=gq, s=s, nk=nk, qcol=qcol, hd=hd, n=n):
                if n == 0:
                    kc.w(tn, 'ss', proj_done)
                else:
                    kc.w(tn, 'ss', exp_ev[(n - 1, hd)])
                for b in range(nk):
                    kcol, _vt3 = _kinfo(gq, s, nk, b)
                    masked = (b == nk - 1) or (nk == 9 and b == 0)
                    ins = tn.matmul(out=SREG[hd][:, 128 * b:128 * (b + 1)],
                                    lhsT=kc.t['qfm'][64 * hd:64 * (hd + 1), qcol:qcol + 128],
                                    rhs=kc.t['kfm3'][64 * hd:64 * (hd + 1), kcol:kcol + 128],
                                    start=True, stop=not masked)
                    if masked:
                        mk = 'mcauB' if b == nk - 1 else 'mantB'
                        ins = tn.matmul(out=SREG[hd][:, 128 * b:128 * (b + 1)],
                                        lhsT=kc.t['idnS'][:, :], rhs=kc.t[mk][:, :],
                                        start=False, stop=True)
                sc_mm[(n, hd)] = kc.inc(ins, 'st')

    _emit_scores(0)
    N_INST = len(insts)
    for n in range(N_INST):
        gq, s = insts[n]
        nk = min(gq + 1, 9)
        nA = min(nk, 5)
        nB = nk - nA
        qcol = 2048 * (gq // 2) + 256 * s + 128 * (gq % 2)

        @block.vector
        def _(v, n=n):
            if n > 0:
                kc.w(v, 'st', av_mm[(n - 1, 1)])
                gq1, s1_ = insts[n - 1]
                qc1 = 2048 * (gq1 // 2) + 256 * s1_ + 128 * (gq1 % 2)
                ins = v.tensor_copy(out=kc.t['ctxall'][:, qc1:qc1 + 128],
                                    in_=psA[:, 1920:2048])
                ctx_ev[n - 1] = kc.inc(ins, 'sv')
            for hd in range(2):
                kc.w(v, 'st', sc_mm[(n, hd)])
                ins = v.tensor_reduce(out=kc.t['stats'][:, 120 + hd:121 + hd],
                                      in_=SREG[hd][:, 0:128 * nk], axis=AX.X, op=ALU.min)
                min_ev[(n, hd)] = kc.inc(ins, 'sv')

        @block.scalar
        def _(sc, n=n, nk=nk):
            for hd in range(2):
                kc.w(sc, 'sv', min_ev[(n, hd)])
                kc.w(sc, 'st', pt_mm[(n - 2, hd)])
                ins = sc.activation(out=PBV(n)[:, 1280 * hd:1280 * hd + 128 * nk],
                                    in_=SREG[hd][:, 0:128 * nk], func=AF.Exp, scale=-1.0,
                                    bias=kc.t['stats'][:, 120 + hd:121 + hd],
                                    accum_out=kc.t['stats'][:, 122 + hd:123 + hd])
                exp_ev[(n, hd)] = kc.inc(ins, 'ss')

        @block.vector
        def _(v, n=n):
            for hd in range(2):
                kc.w(v, 'ss', exp_ev[(n, hd)])
                kc.w(v, 'sg', dg_ev[(n - 1, hd)])
                ins = v.reciprocal(out=kc.t['stats'][:, 124 + hd:125 + hd],
                                   in_=kc.t['stats'][:, 122 + hd:123 + hd])
                rc_ev[(n, hd)] = kc.inc(ins, 'sv')

        @block.gpsimd
        def _(g, n=n):
            for hd in range(2):
                kc.w(g, 'sv', rc_ev[(n, hd)])
                kc.w(g, 'st', pt_mm[(n - 2, hd)])
                ins = g.tensor_scalar(out=DGV(n)[:, 128 * hd:128 * (hd + 1)],
                                      in0=kc.t['idnS'][:, :],
                                      scalar1=kc.t['stats'][:, 124 + hd:125 + hd],
                                      scalar2=None, op0=ALU.mult)
                dg_ev[(n, hd)] = kc.inc(ins, 'sg')

        for hd in range(2):
            @block.tensor
            def _(tn, n=n, hd=hd, nA=nA):
                kc.w(tn, 'sg', dg_ev[(n, hd)])
                kc.w(tn, 'ss', rA_ev[(n - 1, hd)])
                kc.w(tn, 'sv', rB_ev[(n - 1, hd)])
                for b in range(nA):
                    ins = tn.matmul(out=SREG[hd][:, 1152 + 128 * b:1152 + 128 * (b + 1)],
                                    lhsT=PBV(n)[:, 1280 * hd + 128 * b:
                                                1280 * hd + 128 * (b + 1)],
                                    rhs=DGV(n)[:, 128 * hd:128 * (hd + 1)],
                                    start=True, stop=True)
                ptA_mm[(n, hd)] = kc.inc(ins, 'st')
        if n + 1 < N_INST:
            _emit_scores(n + 1)

        @block.scalar
        def _(sc, n=n, nA=nA):
            for hd in range(2):
                kc.w(sc, 'st', ptA_mm[(n, hd)])
                kc.w(sc, 'st', av_mm[(n - 2, hd)])
                ins = sc.activation(out=PTV(n)[:, 1280 * hd:1280 * hd + 128 * nA],
                                    in_=SREG[hd][:, 1152:1152 + 128 * nA], func=AF.Relu)
                rA_ev[(n, hd)] = kc.inc(ins, 'ss')

        if nB > 0:
            for hd in range(2):
                @block.tensor
                def _(tn, n=n, hd=hd, nA=nA, nB=nB):
                    kc.w(tn, 'ss', rA_ev[(n, hd)])
                    for b in range(nB):
                        ins = tn.matmul(out=SREG[hd][:, 1152 + 128 * b:1152 + 128 * (b + 1)],
                                        lhsT=PBV(n)[:, 1280 * hd + 128 * (nA + b):
                                                    1280 * hd + 128 * (nA + b + 1)],
                                        rhs=DGV(n)[:, 128 * hd:128 * (hd + 1)],
                                        start=True, stop=True)
                    pt_mm[(n, hd)] = kc.inc(ins, 'st')

            @block.vector
            def _(v, n=n, nA=nA, nB=nB):
                for hd in range(2):
                    kc.w(v, 'st', pt_mm[(n, hd)])
                    kc.w(v, 'st', av_mm[(n - 2, hd)])
                    ins = v.tensor_copy(out=PTV(n)[:, 1280 * hd + 128 * nA:
                                                   1280 * hd + 128 * (nA + nB)],
                                        in_=SREG[hd][:, 1152:1152 + 128 * nB])
                    rB_ev[(n, hd)] = kc.inc(ins, 'sv')
        else:
            for hd in range(2):
                pt_mm[(n, hd)] = ptA_mm[(n, hd)]
                rB_ev[(n, hd)] = rB_ev[(n - 1, hd)]

        for hd in range(2):
            @block.tensor
            def _(tn, gq=gq, s=s, n=n, hd=hd, nk=nk, nA=nA, nB=nB):
                kc.w(tn, 'ss', rA_ev[(n, hd)])
                kc.w(tn, 'sv', ctx_ev[n - 1])
                if nB > 0:
                    kc.w(tn, 'sv', rB_ev[(n, hd)])
                for b in range(nk):
                    _kc, vtile = _kinfo(gq, s, nk, b)
                    ins = tn.matmul(out=psA[64 * hd:64 * (hd + 1), 1920:2048],
                                    lhsT=kc.t['vtm3'][:, 64 * vtile:64 * (vtile + 1)],
                                    rhs=PTV(n)[:, 1280 * hd + 128 * b:
                                               1280 * hd + 128 * (b + 1)],
                                    start=(b == 0), stop=(b == nk - 1),
                                    tile_position=(0, 64 * hd))
                av_mm[(n, hd)] = kc.inc(ins, 'st')

    @block.vector
    def _(v):
        kc.w(v, 'st', av_mm[(N_INST - 1, 1)])
        gq1, s1_ = insts[N_INST - 1]
        qc1 = 2048 * (gq1 // 2) + 256 * s1_ + 128 * (gq1 % 2)
        ins = v.tensor_copy(out=kc.t['ctxall'][:, qc1:qc1 + 128],
                            in_=psA[:, 1920:2048])
        kc.inc(ins, 'sv')
    att_done = kc.n['sv']
    kc.nc.leave_named_scope(kc.scpn, kc.scp, False)
    kc.scp = kc.nc.enter_named_scope('s3_a2a_wf', False)[0]; kc.scpn = 's3_a2a_wf'
    if getattr(kc, 'dbg', None) == 'att':
        return _dbg_dump(kc, 'ctxall')

    @block.sync
    def _(sy):
        kc.w(sy, 'sv', att_done)
        for f in range(8):
            kc.inc(sy.dma_start(out=dram['cta'].ap()[128 * f:128 * (f + 1), :],
                                in_=kc.t['ctxall'][:, 2048 * f:2048 * (f + 1)]), 'ds', 16)
    cta_done = kc.n['ds']

    @block.gpsimd
    def _(g):
        kc.w(g, 'ds', cta_done)
        with allow_ici_all_to_all():
            ins = g.collective_compute("AllToAll", ALU.bypass,
                                       replica_groups=[list(range(8))],
                                       ins=[dram['cta'].ap().opt()],
                                       outs=[dram['ctb'].ap().opt()])
        kc.inc(ins, 'cc2', 1)
    a2a_done = kc.n['cc2']

    @block.sync
    def _(sy):
        kc.w(sy, 'cc2', a2a_done)
        kc.w(sy, 'st')
        for f in range(8):
            kc.inc(sy.dma_start(out=kc.t['ctxf3'][:, 2048 * f:2048 * (f + 1)],
                                in_=dram['ctb'].ap()[128 * f:128 * (f + 1), :]), 'ds', 16)
    ctb_ld = kc.n['ds']

    ev_wf = []
    for t in range(NT):
        @block.tensor
        def _(tn):
            if t == 0:
                kc.w(tn, 'ds', ctb_ld)
            if t >= 2:
                kc.w(tn, 'sv', ev_wf[t - 2])
            reg = t % 2
            for c in range(8):
                tn.matmul(out=psB[:, 1024 * reg:1024 * reg + 512],
                          lhsT=kc.t['ctxf3'][:, 2048 * c + 128 * t:2048 * c + 128 * (t + 1)],
                          rhs=kc.t['wfb'][:, 1024 * c:1024 * c + 512],
                          start=(c == 0), stop=False)
                tn.matmul(out=psB[:, 1024 * reg + 512:1024 * reg + 1024],
                          lhsT=kc.t['ctxf3'][:, 2048 * c + 128 * t:2048 * c + 128 * (t + 1)],
                          rhs=kc.t['wfb'][:, 1024 * c + 512:1024 * (c + 1)],
                          start=(c == 0), stop=False)
            for half in range(2):
                ins = tn.matmul(out=psB[:, 1024 * reg + 512 * half:1024 * reg + 512 * (half + 1)],
                                lhsT=kc.t['ones_row'][:, :],
                                rhs=kc.t['bfr'][:, 512 * half:512 * (half + 1)],
                                start=False, stop=True)
            kc.inc(ins, 'st')

        @block.vector
        def _(v):
            kc.w(v, 'st')
            reg = t % 2
            ins = v.scalar_tensor_tensor(out=kc.t['xsb'][:, 1024 * t:1024 * (t + 1)],
                                         in0=psB[:, 1024 * reg:1024 * (reg + 1)], scalar=1.0,
                                         in1=kc.t['xsb'][:, 1024 * t:1024 * (t + 1)],
                                         op0=ALU.mult, op1=ALU.add)
            kc.inc(ins, 'sv')
        ev_wf.append(kc.n['sv'])
    kc.nc.leave_named_scope(kc.scpn, kc.scp, False)
def build(stages=4, dbg=None):
    nc = bass.Bass("TRN2", target_bir_lowering=False)
    es = ExitStack()
    h = {}
    for name, shape, dt in [
        ('xs', [8, 320, 1024], F32), ('wqaT', [1024, 1024], F32), ('wkaT', [1024, 1024], F32),
        ('wvaT', [1024, 1024], F32), ('woaT', [1024, 1024], F32),
        ('winU', [24, 1024, 128], F32), ('binc', [128, 24], F32), ('woutT', [1024, 1024], F32),
        ('outb_row', [1, 1024], F32), ('gwT', [1024, 8], F32), ('gb_row', [1, 8], F32),
        ('w1U', [8, 32, 1024, 128], F32), ('w2T', [8, 4096, 1024], F32),
        ('b1c', [128, 256], F32), ('b2a', [8, 1024], F32),
        ('g1c', [128, 8], F32), ('b1lc', [128, 8], F32),
        ('g2B', [128, 1024], F32), ('b2B', [128, 1024], F32),
        ('wqbT', [1024, 128], F32), ('wkbT2', [1024, 128], F32), ('wvbT', [1024, 64], F32),
        ('wfT', [1024, 1024], F32), ('bf_row', [1, 1024], F32),
        ('idn', [128, 128], BF16), ('mska', [2, 128, 192], BF16),
        ('mcaus', [128, 128], BF16), ('manti', [128, 128], BF16),
        ('mq', [17, 2048], BF16), ('mk', [17, 2048], BF16),
    ]:
        h[name] = nc.dram_tensor(name, shape, dt, kind="ExternalInput")
    y_h = nc.dram_tensor("y", [8, 256, 1024], F32, kind="ExternalOutput")
    dram = {
        'spill': nc.dram_tensor("spill", [128, NT * 1024], BF16),
        'u2c': nc.dram_tensor("u2c", [1024, 2048], BF16),
        'uall': nc.dram_tensor("uall", [8192, 2048], BF16, addr_space="Shared"),
        'cta': nc.dram_tensor("cta", [1024, 2048], BF16),
        'ctb': nc.dram_tensor("ctb", [1024, 2048], BF16),
        'moeacc': nc.dram_tensor("moeacc", [128, NT * 1024], F32),
    }

    with es:
        block = es.enter_context(nc.Block())
        kc = KC(nc, es, block)
        kc.h = h
        kc.dram = dram
        kc.dbg = dbg
        kc.mksem('dq', 'ds', 'sv', 'ss', 'st', 'sg', 'cc', 'cc2')

        kc.sb('idnS', [128, 128], BF16)
        kc.sb('epsc', [128, 1], F32)
        kc.sb('ones_row', [1, 128], BF16)
        kc.sb('stats', [128, 128], F32)
        kc.arena = kc.sb('arena', [128, 105728], BF16)
        kc.reg('xsb', 0, NT * 1024, BF16)
        kc.ps('psA', [128, 2048])
        kc.ps('psB', [128, 2048])

        @block.vector
        def _(v):
            v.memset(kc.t['epsc'][:, :], EPS)
            v.memset(kc.t['ones_row'][:, :], 1.0)

        @block.sync
        def _(sy):
            kc.inc(sy.dma_start(out=kc.t['idnS'][:, :], in_=h['idn'].ap()[:, :]), 'ds', 16)

        @block.gpsimd
        def _(g):
            for t in range(NT):
                s, hv = t // 2, t % 2
                kc.inc(g.dma_start(out=kc.t['xsb'][:, 1024 * t:1024 * (t + 1)],
                                   in_=h['xs'].ap()[s, 64 + 128 * hv:64 + 128 * (hv + 1), :]),
                       'dq', 16)

        with nc.named_scope('s1'):
            run_s1(kc, h)
        if stages >= 2:
            with nc.named_scope('ffn1'):
                run_ffn(kc, h, tag='f1')
        if stages >= 3:
            with nc.named_scope('s3'):
                run_s3(kc, h, dram)
        if stages >= 4:
            with nc.named_scope('ffn2'):
                run_ffn(kc, h, tag='f2')

        kc.reg('yout', 32768, 2 * 1024, F32)

        @block.vector
        def _(v):
            kc.w(v, 'sv'); kc.w(v, 'ss'); kc.w(v, 'st'); kc.w(v, 'sg')

        for t in range(NT):
            @block.vector
            def _(v, t=t):
                if t >= 2:
                    kc.w(v, 'ds')
                ins = v.tensor_copy(out=kc.t['yout'][:, 1024 * (t % 2):1024 * (t % 2 + 1)],
                                    in_=kc.t['xsb'][:, 1024 * t:1024 * (t + 1)])
                kc.inc(ins, 'sv')

            @block.sync
            def _(sy, t=t):
                kc.w(sy, 'sv')
                s, hv = t // 2, t % 2
                kc.inc(sy.dma_start(out=y_h.ap()[s, 128 * hv:128 * (hv + 1), :],
                                    in_=kc.t['yout'][:, 1024 * (t % 2):1024 * (t % 2 + 1)]),
                       'ds', 16)

        @block.sync
        def _(sy):
            kc.w(sy, 'ds')
            kc.w(sy, 'dq')
            kc.w(sy, 'sg')
    return nc


def kernel(**inputs):
    stages = getattr(kernel, 'STAGES', 4)
    in_maps = host_prep(inputs)
    nc = build(stages, getattr(kernel, 'DBG', None))
    if getattr(kernel, 'TRACE', False):
        res = run_bass_kernel_spmd(nc, in_maps, core_ids=list(range(8)), trace=True)
        kernel.last_scope_times = res.per_core_scope_times
        kernel.last_exec_ns = res.exec_time_ns
    else:
        res = run_bass_kernel_spmd(nc, in_maps, core_ids=list(range(8)))
    return np.concatenate([r["y"] for r in res.results], axis=1)


# version 10 (final)
# speedup vs baseline: 1.1557x; 1.0019x over previous
import numpy as np
import ml_dtypes
from contextlib import ExitStack

import concourse.bass as bass
import concourse.mybir as mybir
from concourse.bass_utils import run_bass_kernel_spmd
from concourse.aot_env import allow_ici_all_to_all

F32 = mybir.dt.float32
BF16 = mybir.dt.bfloat16
AF = mybir.ActivationFunctionType
ALU = mybir.AluOpType
AX = mybir.AxisListType
BFNP = ml_dtypes.bfloat16

NT = 16
NHALO = 4
D = 1024
EPS = 1e-5


class KC:
    def __init__(self, nc, es, block):
        self.nc, self.es, self.block = nc, es, block
        self.sem, self.n, self.t = {}, {}, {}

    def mksem(self, *names):
        for name in names:
            self.sem[name] = self.es.enter_context(self.nc.semaphore(name))
            self.n[name] = 0

    def sb(self, name, shape, dtype, stack=None):
        self.t[name] = (stack or self.es).enter_context(self.nc.sbuf_tensor(name, shape, dtype))
        return self.t[name]

    def reg(self, name, off_b, cols, dtype, rows=128):
        assert off_b % 64 == 0
        if dtype == BF16:
            v = self.arena[0:rows, off_b // 2: off_b // 2 + cols]
        else:
            v = self.arena[0:rows, off_b // 2: off_b // 2 + 2 * cols].bitcast(F32)
        self.t[name] = v
        return v

    def ps(self, name, shape, dtype=F32):
        self.t[name] = self.es.enter_context(self.nc.psum_tensor(name, shape, dtype))
        return self.t[name]

    def inc(self, instr, s, k=1):
        instr.then_inc(self.sem[s], k)
        self.n[s] += k
        return self.n[s]

    def inc2(self, instr, s1, s2):
        instr.then_inc(self.sem[s1], 1).then_inc(self.sem[s2], 1)
        self.n[s1] += 1
        self.n[s2] += 1
        return self.n[s2]

    def w(self, eng, s, v=None):
        eng.wait_ge(self.sem[s], self.n[s] if v is None else v)


def emit_ln_stats(kc, xbuf, tiles, scol, sqd='sqd'):
    block, st = kc.block, kc.t['stats']
    nt = len(tiles)
    S0, S1, S2, S3, S4 = (scol, scol + nt, scol + 2 * nt, scol + 3 * nt, scol + 4 * nt)

    @block.vector
    def _(v):
        kc.w(v, 'sv')
        for i, tl in enumerate(tiles):
            ins = v.reduce_sum(out=st[:, S0 + i:S0 + i + 1],
                               in_=kc.t[xbuf][:, 1024 * tl:1024 * (tl + 1)], axis=AX.X)
        kc.inc(ins, 'sv')

    @block.scalar
    def _(s):
        kc.w(s, 'sv')
        kc.w(s, 'ss')
        for i, tl in enumerate(tiles):
            ins = s.activation(out=kc.t[sqd][:, :], in_=kc.t[xbuf][:, 1024 * tl:1024 * (tl + 1)],
                               func=AF.Square, accum_out=st[:, S1 + i:S1 + i + 1])
        kc.inc(ins, 'ss')

    @block.vector
    def _(v):
        kc.w(v, 'sv')
        kc.w(v, 'ss')
        v.tensor_scalar(out=st[:, S2:S2 + nt], in0=st[:, S0:S0 + nt],
                        scalar1=1.0 / D, scalar2=None, op0=ALU.mult)
        ins = v.tensor_scalar(out=st[:, S3:S3 + nt], in0=st[:, S1:S1 + nt],
                              scalar1=1.0 / D, scalar2=None, op0=ALU.mult)
        kc.inc(ins, 'sv')
        kc.w(v, 'sv')
        ins = v.tensor_tensor(out=st[:, S4:S4 + nt], in0=st[:, S2:S2 + nt],
                              in1=st[:, S2:S2 + nt], op=ALU.mult)
        kc.inc(ins, 'sv')
        kc.w(v, 'sv')
        ins = v.tensor_tensor(out=st[:, S3:S3 + nt], in0=st[:, S3:S3 + nt],
                              in1=st[:, S4:S4 + nt], op=ALU.subtract)
        kc.inc(ins, 'sv')
        kc.w(v, 'sv')
        ins = v.tensor_scalar(out=st[:, S3:S3 + nt], in0=st[:, S3:S3 + nt],
                              scalar1=0.0, scalar2=None, op0=ALU.max)
        kc.inc(ins, 'sv')

    @block.scalar
    def _(s):
        kc.w(s, 'sv')
        ins = s.activation(out=st[:, S4:S4 + nt], in_=st[:, S3:S3 + nt],
                           func=AF.Ln, bias=kc.t['epsc'][:, 0:1])
        kc.inc(ins, 'ss')
        kc.w(s, 'ss')
        ins = s.activation(out=st[:, S4:S4 + nt], in_=st[:, S4:S4 + nt],
                           func=AF.Exp, scale=-0.5)
        kc.inc(ins, 'ss')
    return S2, S4


def _masks_attn_a(core):
    j = np.arange(192)[None, :]
    r = np.arange(128)[:, None]
    m = np.zeros((2, 128, 192), dtype=np.float32)
    for qt in range(2):
        ok = (j >= r + 1) & (j <= r + 64)
        if core == 0 and qt == 0:
            ok = ok & (j >= 64)
        m[qt] = np.where(ok, 0.0, MBIG)
    return m.astype(BFNP)


MBIG = 30000.0


def _tri_masks():
    r = np.arange(128)
    causB = np.where(r[None, :] > r[:, None], MBIG, 0.0).astype(np.float32)
    antiB = np.where(r[None, :] < r[:, None], MBIG, 0.0).astype(np.float32)
    return causB.astype(BFNP), antiB.astype(BFNP)


def _mqk():
    A = 128.0
    colp = np.arange(2048)
    T, rem = colp // 128, colp % 128
    s, pp = rem // 16, rem % 16
    mq = np.zeros((17, 2048), dtype=np.float32)
    for i in range(16):
        mq[i] = (pp == i) * A
    mq[16] = A
    mk = -mq.copy()
    mk[16] = A
    return mq.astype(BFNP), mk.astype(BFNP)


def host_prep(inputs):
    x0 = np.asarray(inputs['L0_attn_a_x'])
    T = lambda a: np.ascontiguousarray(np.asarray(a).T)
    A = lambda a: np.ascontiguousarray(np.asarray(a, dtype=np.float32))
    com = {
        'wqaT': T(inputs['L0_attn_a_Wq']), 'wkaT': T(inputs['L0_attn_a_Wk']),
        'wvaT': T(inputs['L0_attn_a_Wv']), 'woaT': T(inputs['L0_attn_a_Wo']),
        'winU': np.ascontiguousarray(np.asarray(inputs['L0_ffn_in_proj_w'], dtype=np.float32)
                                     .reshape(24, 128, 1024).transpose(0, 2, 1)),
        'binc': np.ascontiguousarray(A(inputs['L0_ffn_in_proj_b']).reshape(24, 128).T),
        'woutT': T(inputs['L0_ffn_out_w']),
        'outb_row': A(inputs['L0_ffn_out_b']).reshape(1, 1024),
        'gwT': T(inputs['L0_ffn_gate_w']),
        'gb_row': A(inputs['L0_ffn_gate_b']).reshape(1, 8),
        'w1U': np.ascontiguousarray(np.asarray(inputs['L0_ffn_w1']).transpose(0, 2, 1)
                                    .reshape(8, 1024, 32, 128).transpose(0, 2, 1, 3)),
        'w2T': np.ascontiguousarray(np.asarray(inputs['L0_ffn_w2']).transpose(0, 2, 1)),
        'b1c': np.ascontiguousarray(
            A(inputs['L0_ffn_b1']).reshape(8, 32, 128).transpose(2, 0, 1).reshape(128, 256)),
        'b2a': A(inputs['L0_ffn_b2']),
        'g1c': np.ascontiguousarray(A(inputs['L0_ffn_ln1_g']).reshape(8, 128).T),
        'b1lc': np.ascontiguousarray(A(inputs['L0_ffn_ln1_b']).reshape(8, 128).T),
        'g2B': np.ascontiguousarray(np.tile(A(inputs['L0_ffn_ln2_g'])[None, :], (128, 1))),
        'b2B': np.ascontiguousarray(np.tile(A(inputs['L0_ffn_ln2_b'])[None, :], (128, 1))),
        'wkbT2': np.ascontiguousarray(np.concatenate([T(inputs['L0_attn_b_Wk'])] * 2, axis=1)),
        'wvbT': T(inputs['L0_attn_b_Wv']),
        'wfT': T(inputs['L0_attn_b_Wf']),
        'bf_row': A(inputs['L0_attn_b_bf']).reshape(1, 1024),
        'idn': np.eye(128, dtype=BFNP),
    }
    com['mq'], com['mk'] = _mqk()
    com['mcaus'], com['manti'] = _tri_masks()
    wqbT = T(inputs['L0_attn_b_Wq'])
    in_maps = []
    for c in range(8):
        m = dict(com)
        xs = np.zeros((8, 320, 1024), dtype=np.float32)
        xs[:, 64:] = x0[:, 256 * c:256 * (c + 1)]
        if c > 0:
            xs[:, :64] = x0[:, 256 * c - 64:256 * c]
        m['xs'] = xs
        m['mska'] = _masks_attn_a(c)
        m['wqbT'] = np.ascontiguousarray(wqbT[:, 128 * c:128 * (c + 1)])
        in_maps.append(m)
    return in_maps


VT1 = 163840


def _vt(kc, s, k):
    if k < 2:
        base = (VT1 + 2048 * (2 * s + k)) // 2
        return kc.arena[0:128, base:base + 1024]
    base = (VT1 + 32768 + 2048 * (s // 2)) // 2
    r0 = 64 * (s % 2)
    return kc.arena[r0:r0 + 64, base:base + 1024]


def run_s1(kc, h):
    block = kc.block
    kc.reg('ufm', 32768, 8 * 2560, BF16)
    kc.reg('xh', 73728, NHALO * 1024, BF16)
    kc.reg('utmpA', 81920, 2 * 1024, BF16)
    kc.reg('sqdA', 86016, 1024, F32)
    psA, psB = kc.t['psA'], kc.t['psB']

    @block.gpsimd
    def _(g):
        for t in range(NHALO):
            for k in range(2):
                kc.inc(g.dma_start(out=kc.t['xh'][64 * k:64 * (k + 1), 1024 * t:1024 * (t + 1)],
                                   in_=h['xs'].ap()[2 * t + k, 0:64, :]), 'dq', 16)

    @block.vector
    def _(v):
        kc.w(v, 'dq')
        kc.w(v, 'ds')

    @block.scalar
    def _(s):
        kc.w(s, 'dq')
    M1, R1 = emit_ln_stats(kc, 'xsb', list(range(NT)), 0, 'sqdA')
    M2, R2 = emit_ln_stats(kc, 'xh', list(range(NHALO)), 100, 'sqdA')

    evs = []
    for it in range(NT + NHALO):
        halo = it >= NT
        t = it - NT if halo else it

        @block.vector
        def _(v):
            kc.w(v, 'ss')
            if it >= 2:
                kc.w(v, 'st', evs[it - 2][0])
            src = kc.t['xh'] if halo else kc.t['xsb']
            mcol = (M2 + t) if halo else (M1 + t)
            rcol = (R2 + t) if halo else (R1 + t)
            ins = v.tensor_scalar(out=kc.t['utmpA'][:, 1024 * (it % 2):1024 * (it % 2 + 1)],
                                  in0=src[:, 1024 * t:1024 * (t + 1)],
                                  scalar1=kc.t['stats'][:, mcol:mcol + 1],
                                  scalar2=kc.t['stats'][:, rcol:rcol + 1],
                                  op0=ALU.subtract, op1=ALU.mult)
            kc.inc(ins, 'sv')

        @block.tensor
        def _(tn):
            kc.w(tn, 'sv')
            if it >= 2:
                kc.w(tn, 'sv', evs[it - 2][1])
            for f in range(8):
                ins = tn.matmul(out=psA[:, 1024 * (it % 2) + 128 * f:1024 * (it % 2) + 128 * (f + 1)],
                                lhsT=kc.t['utmpA'][:, 1024 * (it % 2) + 128 * f:
                                                  1024 * (it % 2) + 128 * (f + 1)],
                                rhs=kc.t['idnS'][:, :], start=True, stop=True)
            kc.inc(ins, 'st')
        st_t = kc.n['st']

        @block.vector
        def _(v):
            kc.w(v, 'st')
            if not halo:
                s, hv = t // 2, t % 2
                a0 = 320 * s + 64 + 128 * hv
                ov = kc.t['ufm'][:, :].rearrange("p (f a) -> p f a", a=2560)[:, :, a0:a0 + 128]
                iv = psA[:, 1024 * (it % 2):1024 * (it % 2 + 1)].rearrange("p (f a) -> p f a", a=128)
            else:
                ov = kc.t['ufm'][:, :].rearrange("p (f s i) -> p f s i", s=8, i=320)[
                    :, :, 2 * t:2 * t + 2, 0:64]
                iv = psA[:, 1024 * (it % 2):1024 * (it % 2 + 1)].rearrange(
                    "p (f k i) -> p f k i", k=2, i=64)
            ins = v.tensor_copy(out=ov, in_=iv)
            kc.inc(ins, 'sv')
        evs.append((st_t, kc.n['sv']))
    ufm_done = kc.n['sv']
    kc.reg('qfull', 73728, 8 * 2048, BF16)
    kc.reg('wbuf', 106496, 8 * 1024, BF16)

    @block.gpsimd
    def _(g):
        kc.w(g, 'st')
        kc.inc(g.dma_start(out=kc.t['wbuf'][:, :].rearrange("p (c f) -> p c f", f=1024),
                           in_=h['wqaT'].ap().rearrange("(c p) f -> p c f", p=128)), 'dq', 16)
    ld_wq = kc.n['dq']

    ev_q = []
    for oc in range(8):
        for grp in range(4):
            idx = oc * 4 + grp
            reg = idx % 4

            @block.tensor
            def _(tn):
                if idx == 0:
                    kc.w(tn, 'dq', ld_wq)
                    kc.w(tn, 'sv', ufm_done)
                if idx >= 4:
                    kc.w(tn, 'ss', ev_q[idx - 4])
                ua = kc.t['ufm'][:, :].rearrange("p (f s i) -> p f s i", f=8, i=320)
                rhs = ua[:, :, 2 * grp:2 * grp + 2, 64:320]
                ov = psB[:, 512 * reg:512 * (reg + 1)].rearrange("p (s i) -> p s i", s=2)
                for c in range(8):
                    ins = tn.matmul(out=ov, lhsT=kc.t['wbuf'][:, 1024 * c + 128 * oc:
                                                              1024 * c + 128 * (oc + 1)],
                                    rhs=rhs[:, c, :, :], start=(c == 0), stop=(c == 7))
                kc.inc(ins, 'st')

            @block.vector
            def _(v):
                kc.w(v, 'st')
                ins = v.tensor_scalar(out=kc.t['qfull'][:, 2048 * oc + 512 * grp:2048 * oc + 512 * (grp + 1)],
                                      in0=psB[:, 512 * reg:512 * (reg + 1)],
                                      scalar1=-0.125, scalar2=None, op0=ALU.mult)
                kc.inc(ins, 'ss')
            ev_q.append(kc.n['ss'])

    @block.gpsimd
    def _(g):
        kc.w(g, 'st')
        kc.inc(g.dma_start(out=kc.t['wbuf'][:, :].rearrange("p (c f) -> p c f", f=1024),
                           in_=h['wkaT'].ap().rearrange("(c p) f -> p c f", p=128)), 'dq', 16)
    ld_wk = kc.n['dq']
    kc.reg('kfm', 122880, 8 * 2560, BF16)

    ev_k = []
    for oc in range(8):
        for grp in range(5):
            idx = oc * 5 + grp
            reg = idx % 4

            @block.tensor
            def _(tn):
                if idx == 0:
                    kc.w(tn, 'dq', ld_wk)
                    kc.w(tn, 'ss', ev_q[-1])
                if idx >= 4:
                    kc.w(tn, 'sv', ev_k[idx - 4])
                for c in range(8):
                    ins = tn.matmul(out=psB[:, 512 * reg:512 * (reg + 1)],
                                    lhsT=kc.t['wbuf'][:, 1024 * c + 128 * oc:1024 * c + 128 * (oc + 1)],
                                    rhs=kc.t['ufm'][:, 2560 * c + 512 * grp:2560 * c + 512 * (grp + 1)],
                                    start=(c == 0), stop=(c == 7))
                kc.inc(ins, 'st')

            @block.vector
            def _(v):
                kc.w(v, 'st')
                ins = v.tensor_copy(out=kc.t['kfm'][:, 2560 * oc + 512 * grp:2560 * oc + 512 * (grp + 1)],
                                    in_=psB[:, 512 * reg:512 * (reg + 1)])
                kc.inc(ins, 'sv')
            ev_k.append(kc.n['sv'])

    @block.gpsimd
    def _(g):
        kc.w(g, 'st')
        kc.inc(g.dma_start(out=kc.t['wbuf'][:, :].rearrange("p (c f) -> p c f", f=1024),
                           in_=h['wvaT'].ap().rearrange("(c p) f -> p c f", p=128)), 'dq', 16)
    ld_wv = kc.n['dq']

    ev_v = []
    for s in range(8):
        for k in range(3):
            idx = 3 * s + k
            reg = idx % 2

            @block.tensor
            def _(tn):
                if idx == 0:
                    kc.w(tn, 'dq', ld_wv)
                    kc.w(tn, 'sv', ev_k[-1])
                elif idx == 1:
                    pass
                else:
                    kc.w(tn, 'sv', ev_v[idx - 2])
                rows = 128 if k < 2 else 64
                r0 = 64 * (s % 2) if k == 2 else 0
                tp = (0, r0) if k == 2 else None
                ua = kc.t['ufm'][:, :].rearrange("p (f a) -> p f a", a=2560)
                for half in range(2):
                    for c in range(8):
                        ins = tn.matmul(out=psB[r0:r0 + rows, 1024 * reg + 512 * half:
                                                1024 * reg + 512 * (half + 1)],
                                        lhsT=ua[:, c, 320 * s + 128 * k:320 * s + 128 * k + rows],
                                        rhs=kc.t['wbuf'][:, 1024 * c + 512 * half:
                                                         1024 * c + 512 * (half + 1)],
                                        start=(c == 0), stop=(c == 7), tile_position=tp)
                kc.inc(ins, 'st')

            @block.vector
            def _(v):
                kc.w(v, 'st')
                rows = 128 if k < 2 else 64
                r0 = 64 * (s % 2) if k == 2 else 0
                ins = v.tensor_copy(out=_vt(kc, s, k),
                                    in_=psB[r0:r0 + rows, 1024 * reg:1024 * (reg + 1)])
                kc.inc(ins, 'sv')
            ev_v.append(kc.n['sv'])

    @block.gpsimd
    def _(g):
        kc.w(g, 'st')
        kc.inc(g.dma_start(out=kc.t['wbuf'][:, :].rearrange("p (c f) -> p c f", f=1024),
                           in_=h['woaT'].ap().rearrange("(c p) f -> p c f", p=128)), 'dq', 16)
    ld_wo = kc.n['dq']
    kc.reg('mskaS', 32768, 2 * 192, BF16)
    kc.reg('pbuf', 33792, 2 * 256, BF16)
    kc.reg('ptb', 34816, 2 * 256, BF16)
    kc.reg('diag', 35840, 2 * 128, BF16)
    kc.reg('ctxb', 36864, 1024, BF16)
    kc.reg('acol', 39936, 64, F32)

    @block.sync
    def _(sy):
        kc.w(sy, 'st')
        kc.inc(sy.dma_start(out=kc.t['mskaS'][:, 0:192], in_=h['mska'].ap()[0, :, :]), 'ds', 16)
        kc.inc(sy.dma_start(out=kc.t['mskaS'][:, 192:384], in_=h['mska'].ap()[1, :, :]), 'ds', 16)

    SC = [1024, 1536]
    PT = [1216, 1728]
    for s in range(8):
        for qt in range(2):
            inst = 2 * s + qt
            mmpt = {}
            avmm = {}
            evpt = {}
            for hd in range(16):
                reg = hd % 2

                @block.tensor
                def _(tn):
                    if inst == 0 and hd == 0:
                        kc.w(tn, 'ss')
                    if hd >= 2:
                        kc.w(tn, 'ss', evpt[hd - 2])
                    elif inst > 0:
                        kc.w(tn, 'ss')
                    oc, rr = hd // 2, 64 * (hd % 2)
                    a0 = 320 * s + 128 * qt
                    q0 = 2048 * oc + 256 * s + 128 * qt
                    tn.matmul(out=psA[:, SC[reg]:SC[reg] + 192],
                              lhsT=kc.t['qfull'][rr:rr + 64, q0:q0 + 128],
                              rhs=kc.t['kfm'][rr:rr + 64, 2560 * oc + a0:2560 * oc + a0 + 192],
                              start=True, stop=False)
                    ins = tn.matmul(out=psA[:, SC[reg]:SC[reg] + 192],
                                    lhsT=kc.t['idnS'][:, :],
                                    rhs=kc.t['mskaS'][:, 192 * qt:192 * (qt + 1)],
                                    start=False, stop=True)
                    kc.inc(ins, 'st')

                @block.vector
                def _(v):
                    kc.w(v, 'st')
                    ins = v.tensor_reduce(out=kc.t['acol'][:, hd:hd + 1],
                                          in_=psA[:, SC[reg]:SC[reg] + 192],
                                          axis=AX.X, op=ALU.min)
                    kc.inc(ins, 'sv')

                @block.scalar
                def _(sc):
                    kc.w(sc, 'sv')
                    if hd >= 2:
                        kc.w(sc, 'st', mmpt[hd - 2])
                    ins = sc.activation(out=kc.t['pbuf'][:, 256 * reg:256 * reg + 192],
                                        in_=psA[:, SC[reg]:SC[reg] + 192],
                                        func=AF.Exp, scale=-1.0, bias=kc.t['acol'][:, hd:hd + 1])
                    kc.inc(ins, 'ss')

                @block.vector
                def _(v):
                    kc.w(v, 'ss')
                    ins = v.reduce_sum(out=kc.t['acol'][:, 16 + hd:17 + hd],
                                       in_=kc.t['pbuf'][:, 256 * reg:256 * reg + 192], axis=AX.X)
                    kc.inc(ins, 'sv')
                    kc.w(v, 'sv')
                    ins = v.reciprocal(out=kc.t['acol'][:, 32 + hd:33 + hd],
                                       in_=kc.t['acol'][:, 16 + hd:17 + hd])
                    kc.inc(ins, 'sv')
                    kc.w(v, 'sv')
                    if hd >= 2:
                        kc.w(v, 'st', mmpt[hd - 2])
                    ins = v.tensor_scalar(out=kc.t['diag'][:, 128 * reg:128 * (reg + 1)],
                                          in0=kc.t['idnS'][:, :],
                                          scalar1=kc.t['acol'][:, 32 + hd:33 + hd],
                                          scalar2=None, op0=ALU.mult)
                    kc.inc(ins, 'sv')

                @block.tensor
                def _(tn):
                    kc.w(tn, 'sv')
                    rB = 64 * (s % 2) if qt == 1 else 0
                    tn.matmul(out=psA[:, PT[reg]:PT[reg] + 128],
                              lhsT=kc.t['pbuf'][:, 256 * reg:256 * reg + 128],
                              rhs=kc.t['diag'][:, 128 * reg:128 * (reg + 1)],
                              start=True, stop=True)
                    ins = tn.matmul(out=psA[rB:rB + 64, PT[reg] + 128:PT[reg] + 256],
                                    lhsT=kc.t['pbuf'][:, 256 * reg + 128:256 * reg + 192],
                                    rhs=kc.t['diag'][:, 128 * reg:128 * (reg + 1)],
                                    start=True, stop=True, tile_position=(0, rB))
                    mmpt[hd] = kc.inc(ins, 'st')

                @block.scalar
                def _(sc):
                    kc.w(sc, 'st', mmpt[hd])
                    if hd >= 2:
                        kc.w(sc, 'st', avmm[hd - 2])
                    rB = 64 * (s % 2) if qt == 1 else 0
                    sc.activation(out=kc.t['ptb'][:, 256 * reg:256 * reg + 128],
                                  in_=psA[:, PT[reg]:PT[reg] + 128], func=AF.Relu)
                    ins = sc.activation(out=kc.t['ptb'][rB:rB + 64, 256 * reg + 128:256 * (reg + 1)],
                                        in_=psA[rB:rB + 64, PT[reg] + 128:PT[reg] + 256],
                                        func=AF.Relu)
                    evpt[hd] = kc.inc(ins, 'ss')

                @block.tensor
                def _(tn):
                    kc.w(tn, 'ss', evpt[hd])
                    oc, rr = hd // 2, 64 * (hd % 2)
                    vA = _vt(kc, s, qt)
                    vB = _vt(kc, s, qt + 1)
                    tn.matmul(out=psB[rr:rr + 64, 128 * oc:128 * (oc + 1)],
                              lhsT=vA[:, 64 * hd:64 * (hd + 1)],
                              rhs=kc.t['ptb'][:, 256 * reg:256 * reg + 128],
                              start=True, stop=False)
                    rB = 64 * (s % 2) if qt == 1 else 0
                    lhsTB = vB[0:64, 64 * hd:64 * (hd + 1)] if qt == 0 else vB[:, 64 * hd:64 * (hd + 1)]
                    ins = tn.matmul(out=psB[rr:rr + 64, 128 * oc:128 * (oc + 1)],
                                    lhsT=lhsTB,
                                    rhs=kc.t['ptb'][rB:rB + 64, 256 * reg + 128:256 * (reg + 1)],
                                    start=False, stop=True, tile_position=(rB, rr))
                    avmm[hd] = kc.inc(ins, 'st')

            @block.vector
            def _(v):
                kc.w(v, 'st')
                ins = v.tensor_copy(out=kc.t['ctxb'][:, :], in_=psB[:, 0:1024])
                kc.inc(ins, 'sv')

            @block.tensor
            def _(tn):
                kc.w(tn, 'sv')
                if inst == 0:
                    kc.w(tn, 'dq', ld_wo)
                for c in range(8):
                    for half in range(2):
                        ins = tn.matmul(out=psB[:, 1024 + 512 * half:1024 + 512 * (half + 1)],
                                        lhsT=kc.t['ctxb'][:, 128 * c:128 * (c + 1)],
                                        rhs=kc.t['wbuf'][:, 1024 * c + 512 * half:
                                                         1024 * c + 512 * (half + 1)],
                                        start=(c == 0), stop=(c == 7))
                kc.inc(ins, 'st')

            @block.vector
            def _(v):
                kc.w(v, 'st')
                t = 2 * s + qt
                ins = v.scalar_tensor_tensor(out=kc.t['xsb'][:, 1024 * t:1024 * (t + 1)],
                                             in0=psB[:, 1024:2048], scalar=1.0,
                                             in1=kc.t['xsb'][:, 1024 * t:1024 * (t + 1)],
                                             op0=ALU.mult, op1=ALU.add)
                kc.inc(ins, 'sv')


def _dbg_dump(kc, name, cols=16384):
    block = kc.block

    @block.vector
    def _(v):
        kc.w(v, 'st'); kc.w(v, 'ss'); kc.w(v, 'sg'); kc.w(v, 'dq'); kc.w(v, 'ds')
        ins = v.tensor_copy(out=kc.t['xsb'][:, 0:cols], in_=kc.t[name][:, 0:cols])
        kc.inc(ins, 'sv')


def run_ffn(kc, h, tag):
    block = kc.block
    dbg = getattr(kc, 'dbg', None)
    psA, psB = kc.t['psA'], kc.t['psB']
    dram = kc.dram
    UTM, UFM2, SQ2, UT2 = 32768, 65536, 98304, 102400
    WIN, QKV, TAT, CTXF = 106496, 118784, 135168, 147456
    XFM = 65536
    MQ, MK = 180224, 184320
    W1U, W2U, TMPY = 180224, 184320, 188416
    M0 = 192512
    kc.reg('utm', UTM, 16 * 1024, BF16)
    kc.reg('ufm2', UFM2, 8 * 2048, BF16)
    kc.reg('sqd2', SQ2, 1024, F32)
    kc.reg('winb', WIN, 2 * 3 * 1024, BF16)
    kc.reg('qj', QKV, 2048, BF16)
    kc.reg('kj', QKV + 4096, 2048, BF16)
    kc.reg('vj', QKV + 8192, 2048, BF16)
    kc.reg('vtj', QKV + 12288, 2048, BF16)
    kc.reg('Ajh', TAT, 1024, BF16)
    kc.reg('Atb', TAT + 2048, 1024, BF16)
    kc.reg('diags', TAT + 4096, 8 * 128, BF16)
    kc.reg('ctxf', CTXF, 8 * 2048, BF16)
    kc.reg('xfm', XFM, 8 * 2048, BF16)
    kc.reg('mqs', MQ, 2048, BF16, rows=17)
    kc.reg('mks', MK, 2048, BF16, rows=17)
    kc.reg('woutb', WIN, 8 * 1024, BF16)
    kc.reg('g2B', M0, 1024, BF16)
    kc.reg('b2B', M0 + 2048, 1024, BF16)
    kc.reg('b2a', M0 + 4096, 1024, BF16, rows=8)
    kc.reg('b1cs', M0 + 6144, 256, F32)
    kc.reg('bincs', M0 + 7168, 24, F32)
    kc.reg('binq', M0 + 7296, 8, F32)
    kc.reg('g1cs', M0 + 7360, 8, F32)
    kc.reg('b1ls', M0 + 7424, 8, F32)
    kc.reg('gwTs', M0 + 7488, 64, BF16)
    kc.reg('glog', M0 + 7680, 128, F32)
    kc.reg('comb', M0 + 8192, 128, F32)
    kc.reg('combT', M0 + 8704, 128, BF16, rows=8)
    kc.reg('outbr', M0 + 8960, 1024, BF16, rows=1)
    kc.reg('gbr', M0 + 11008, 8, BF16, rows=1)
    kc.reg('combB', M0 + 11072, 128, BF16)
    kc.reg('w2f', 106496, 32 * 1024, BF16)
    kc.reg('w1u', 172032, 4 * 2048, BF16)
    kc.reg('tmpy', 188416, 1024, F32)
    kc.reg('mtile', 106496, 2 * 1024, F32)
    kc.reg('t2all', 114688, 16 * 1024, F32)

    def hview(j):
        if j < 8:
            off = 4096 * j
        elif j < 16:
            off = 32768 + 4096 * (j - 8)
        elif j < 26:
            off = 106496 + 4096 * (j - 16)
        else:
            off = 147456 + 4096 * (j - 26)
        return kc.arena[0:128, off // 2:off // 2 + 2048]

    @block.sync
    def _(sy):
        kc.w(sy, 'st')
        kc.w(sy, 'sv')
        kc.w(sy, 'ss')
        for name, hh2 in [('mqs', 'mq'), ('mks', 'mk'), ('b1cs', 'b1c'), ('bincs', 'binc'),
                          ('g1cs', 'g1c'), ('b1ls', 'b1lc')]:
            kc.inc(sy.dma_start(out=kc.t[name][:, :], in_=h[hh2].ap()[:, :]), 'ds', 16)
    ld_c1 = kc.n['ds']

    @block.gpsimd
    def _(g):
        kc.w(g, 'st')
        kc.w(g, 'sv')
        for name, hh2 in [('g2B', 'g2B'), ('b2B', 'b2B'), ('b2a', 'b2a'),
                          ('outbr', 'outb_row'), ('gbr', 'gb_row')]:
            kc.inc(g.dma_start(out=kc.t[name][:, :], in_=h[hh2].ap()[:, :]), 'dq', 16)
        kc.inc(g.dma_start(out=kc.t['gwTs'][:, :].rearrange("p (c e) -> p c e", e=8),
                           in_=h['gwT'].ap().rearrange("(c p) e -> p c e", p=128)), 'dq', 16)
        for oc in (0, 8, 16):
            kc.inc(g.dma_start(
                out=kc.t['winb'][:, 1024 * (oc // 8):1024 * (oc // 8 + 1)]
                    .rearrange("p (c f) -> p c f", f=128),
                in_=h['winU'].ap()[oc, :, :].rearrange("(c p) f -> p c f", p=128)), 'dq', 16)
    ld_u0 = kc.n['dq']

    @block.vector
    def _(v):
        kc.w(v, 'ds', ld_c1)
        ins = v.tensor_scalar(out=kc.t['binq'][:, :], in0=kc.t['bincs'][:, 0:8],
                              scalar1=-0.125, scalar2=None, op0=ALU.mult)
        kc.inc(ins, 'sv')

    kc.scp = kc.nc.enter_named_scope('ffn_ln', False)[0]; kc.scpn = 'ffn_ln'
    M1, R1 = emit_ln_stats(kc, 'xsb', list(range(NT)), 0, 'sqd2')
    evs = []
    for t in range(NT):
        @block.vector
        def _(v):
            kc.w(v, 'ss')
            ins = v.tensor_scalar(out=kc.t['utm'][:, 1024 * t:1024 * (t + 1)],
                                  in0=kc.t['xsb'][:, 1024 * t:1024 * (t + 1)],
                                  scalar1=kc.t['stats'][:, M1 + t:M1 + t + 1],
                                  scalar2=kc.t['stats'][:, R1 + t:R1 + t + 1],
                                  op0=ALU.subtract, op1=ALU.mult)
            kc.inc(ins, 'sv')

        @block.tensor
        def _(tn):
            kc.w(tn, 'sv')
            if t >= 2:
                kc.w(tn, 'sv', evs[t - 2])
            for f in range(8):
                ins = tn.matmul(out=psA[:, 1024 * (t % 2) + 128 * f:1024 * (t % 2) + 128 * (f + 1)],
                                lhsT=kc.t['utm'][:, 1024 * t + 128 * f:1024 * t + 128 * (f + 1)],
                                rhs=kc.t['idnS'][:, :], start=True, stop=True)
            kc.inc(ins, 'st')

        @block.vector
        def _(v):
            kc.w(v, 'st')
            ov = kc.t['ufm2'][:, :].rearrange("p (f a) -> p f a", a=2048)[:, :, 128 * t:128 * (t + 1)]
            iv = psA[:, 1024 * (t % 2):1024 * (t % 2 + 1)].rearrange("p (f a) -> p f a", a=128)
            ins = v.tensor_copy(out=ov, in_=iv)
            kc.inc(ins, 'sv')
        evs.append(kc.n['sv'])
    ufm2_done = kc.n['sv']
    kc.nc.leave_named_scope(kc.scpn, kc.scp, False)
    kc.scp = kc.nc.enter_named_scope('ffn_inp_attn', False)[0]; kc.scpn = 'ffn_inp_attn'
    if dbg == 'src':
        return _dbg_dump(kc, 'utm')

    for j in range(8):
        if j < 7:
            @block.gpsimd
            def _(g):
                kc.w(g, 'st')
                for pl, oc in enumerate((j + 1, 9 + j, 17 + j)):
                    kc.inc(g.dma_start(
                        out=kc.t['winb'][:, 3072 * ((j + 1) % 2) + 1024 * pl:
                                         3072 * ((j + 1) % 2) + 1024 * (pl + 1)]
                            .rearrange("p (c f) -> p c f", f=128),
                        in_=h['winU'].ap()[oc, :, :].rearrange("(c p) f -> p c f", p=128)), 'dq', 16)

        u2 = kc.t['ufm2']
        for which, dst, pX, scale in [(0, 'qj', psA, -0.125), (1, 'kj', psB, 1.0), (2, 'vj', psA, 1.0)]:
            @block.tensor
            def _(tn):
                if j == 0 and which == 0:
                    kc.w(tn, 'dq', ld_u0)
                    kc.w(tn, 'sv', ufm2_done)
                elif which == 0 and j > 0:
                    kc.w(tn, 'dq')
                kc.w(tn, 'ss')
                plane = 3072 * (j % 2) + 1024 * which
                for c in range(8):
                    for g4 in range(4):
                        ins = tn.matmul(out=pX[:, 512 * g4:512 * (g4 + 1)],
                                        lhsT=kc.t['winb'][:, plane + 128 * c:plane + 128 * (c + 1)],
                                        rhs=u2[:, 2048 * c + 512 * g4:2048 * c + 512 * (g4 + 1)],
                                        start=(c == 0), stop=(c == 7))
                kc.inc(ins, 'st')

            @block.vector
            def _(v):
                kc.w(v, 'st')
                kc.w(v, 'ss')
                bias_ap = kc.t['binq'][:, j:j + 1] if which == 0 else \
                    kc.t['bincs'][:, 8 * which + j:8 * which + j + 1]
                iv = pX[:, 0:2048].rearrange("p (s T c) -> p s T c", s=8, c=16)
                ov = kc.t[dst][:, :].rearrange("p (T s c) -> p s T c", T=16, c=16)
                ins = v.tensor_scalar(out=ov, in0=iv, scalar1=bias_ap, scalar2=scale,
                                      op0=ALU.add, op1=ALU.mult)
                kc.inc(ins, 'ss')

        @block.tensor
        def _(tn):
            kc.w(tn, 'ss')
            for T in range(16):
                ins = tn.matmul(out=psB[:, 128 * T:128 * (T + 1)],
                                lhsT=kc.t['vj'][:, 128 * T:128 * (T + 1)],
                                rhs=kc.t['idnS'][:, :], start=True, stop=True)
            kc.inc(ins, 'st')

        @block.vector
        def _(v):
            kc.w(v, 'st')
            ins = v.tensor_copy(out=kc.t['vtj'][:, :], in_=psB[:, 0:2048])
            kc.inc(ins, 'sv')

        for H2 in range(2):
            for hh in range(2):
                rr = 64 * hh

                @block.tensor
                def _(tn):
                    kc.w(tn, 'sv')
                    kc.w(tn, 'ss')
                    for Tl in range(8):
                        T = 8 * H2 + Tl
                        tn.matmul(out=psA[:, 128 * Tl:128 * (Tl + 1)],
                                  lhsT=kc.t['mqs'][:, 128 * T:128 * (T + 1)],
                                  rhs=kc.t['mks'][:, 128 * T:128 * (T + 1)], start=True, stop=False)
                        ins = tn.matmul(out=psA[:, 128 * Tl:128 * (Tl + 1)],
                                        lhsT=kc.t['qj'][rr:rr + 64, 128 * T:128 * (T + 1)],
                                        rhs=kc.t['kj'][rr:rr + 64, 128 * T:128 * (T + 1)],
                                        start=False, stop=True)
                    kc.inc(ins, 'st')

                @block.vector
                def _(v):
                    kc.w(v, 'st')
                    ins = v.tensor_reduce(out=kc.t['stats'][:, 120:121], in_=psA[:, 0:1024],
                                          axis=AX.X, op=ALU.min)
                    kc.inc(ins, 'sv')

                @block.scalar
                def _(sc):
                    kc.w(sc, 'sv')
                    ins = sc.activation(out=kc.t['Ajh'][:, 0:1024], in_=psA[:, 0:1024],
                                        func=AF.Exp, scale=-1.0, bias=kc.t['stats'][:, 120:121])
                    kc.inc(ins, 'ss')

                @block.vector
                def _(v):
                    kc.w(v, 'ss')
                    ins = v.reduce_sum(out=kc.t['stats'][:, 104:112],
                                       in_=kc.t['Ajh'][:, 0:1024].rearrange("p (T c) -> p T c", c=128),
                                       axis=AX.X)
                    kc.inc(ins, 'sv')
                    kc.w(v, 'sv')
                    ins = v.reciprocal(out=kc.t['stats'][:, 112:120], in_=kc.t['stats'][:, 104:112])
                    kc.inc(ins, 'sv')
                    kc.w(v, 'sv')
                    for Tl in range(8):
                        ins = v.tensor_scalar(out=kc.t['diags'][:, 128 * Tl:128 * (Tl + 1)],
                                              in0=kc.t['idnS'][:, :],
                                              scalar1=kc.t['stats'][:, 112 + Tl:113 + Tl],
                                              scalar2=None, op0=ALU.mult)
                    kc.inc(ins, 'sv')

                @block.tensor
                def _(tn):
                    kc.w(tn, 'sv')
                    for Tl in range(8):
                        ins = tn.matmul(out=psB[:, 128 * Tl:128 * (Tl + 1)],
                                        lhsT=kc.t['Ajh'][:, 128 * Tl:128 * (Tl + 1)],
                                        rhs=kc.t['diags'][:, 128 * Tl:128 * (Tl + 1)],
                                        start=True, stop=True)
                    kc.inc(ins, 'st')

                @block.scalar
                def _(sc):
                    kc.w(sc, 'st')
                    ins = sc.activation(out=kc.t['Atb'][:, 0:1024], in_=psB[:, 0:1024], func=AF.Relu)
                    kc.inc(ins, 'ss')

                @block.tensor
                def _(tn):
                    kc.w(tn, 'ss')
                    for Tl in range(8):
                        T = 8 * H2 + Tl
                        ins = tn.matmul(out=psB[rr:rr + 64, 1024 + 128 * Tl:1024 + 128 * (Tl + 1)],
                                        lhsT=kc.t['vtj'][:, 128 * T + 64 * hh:128 * T + 64 * (hh + 1)],
                                        rhs=kc.t['Atb'][:, 128 * Tl:128 * (Tl + 1)],
                                        start=True, stop=True, tile_position=(0, rr))
                    kc.inc(ins, 'st')

            @block.vector
            def _(v):
                kc.w(v, 'st')
                iv = psB[:, 1024:2048].rearrange("p (T s c) -> p T s c", T=8, c=16)
                ov = kc.t['ctxf'][:, 2048 * j:2048 * (j + 1)] \
                    .rearrange("p (s h T c) -> p h T s c", s=8, h=2, c=16)[:, H2]
                ins = v.tensor_copy(out=ov, in_=iv)
                kc.inc(ins, 'sv')
    ctx_done = kc.n['sv']
    kc.nc.leave_named_scope(kc.scpn, kc.scp, False)
    kc.scp = kc.nc.enter_named_scope('ffn_outproj_ln1_gate', False)[0]; kc.scpn = 'ffn_outproj_ln1_gate'
    if dbg == 'ctx':
        return _dbg_dump(kc, 'ctxf')
    if dbg == 'qkv':
        return _dbg_dump(kc, 'qj', 2048)

    @block.gpsimd
    def _(g):
        kc.w(g, 'st')
        kc.inc(g.dma_start(out=kc.t['woutb'][:, :].rearrange("p (c f) -> p c f", f=1024),
                           in_=h['woutT'].ap().rearrange("(c p) f -> p c f", p=128)), 'dq', 16)
    ld_wo2 = kc.n['dq']

    ev_o = []
    for t in range(NT):
        @block.tensor
        def _(tn):
            if t == 0:
                kc.w(tn, 'dq', ld_wo2)
                kc.w(tn, 'sv', ctx_done)
            if t >= 2:
                kc.w(tn, 'sv', ev_o[t - 2])
            reg = t % 2
            for c in range(8):
                for half in range(2):
                    tn.matmul(out=psA[:, 1024 * reg + 512 * half:1024 * reg + 512 * (half + 1)],
                              lhsT=kc.t['ctxf'][:, 2048 * c + 128 * t:2048 * c + 128 * (t + 1)],
                              rhs=kc.t['woutb'][:, 1024 * c + 512 * half:1024 * c + 512 * (half + 1)],
                              start=(c == 0), stop=False)
            for half in range(2):
                ins = tn.matmul(out=psA[:, 1024 * reg + 512 * half:1024 * reg + 512 * (half + 1)],
                                lhsT=kc.t['ones_row'][:, :],
                                rhs=kc.t['outbr'][:, 512 * half:512 * (half + 1)],
                                start=False, stop=True)
            kc.inc(ins, 'st')

        @block.vector
        def _(v):
            kc.w(v, 'st')
            reg = t % 2
            ins = v.scalar_tensor_tensor(out=kc.t['utm'][:, 1024 * t:1024 * (t + 1)],
                                         in0=psA[:, 1024 * reg:1024 * (reg + 1)], scalar=1.0,
                                         in1=kc.t['utm'][:, 1024 * t:1024 * (t + 1)],
                                         op0=ALU.mult, op1=ALU.add)
            kc.inc(ins, 'sv')
        ev_o.append(kc.n['sv'])

    if dbg == 's1':
        return _dbg_dump(kc, 'utm')
    M2c, R2c = emit_ln_stats(kc, 'utm', list(range(NT)), 0, 'sqd2')
    kc.reg('utmp2', UT2, 2048, BF16)
    ev_x = []
    for t in range(NT):
        @block.vector
        def _(v):
            kc.w(v, 'ss')
            if t >= 2:
                kc.w(v, 'st', ev_x[t - 2][0])
            ins = v.tensor_scalar(out=kc.t['utmp2'][:, 1024 * (t % 2):1024 * (t % 2 + 1)],
                                  in0=kc.t['utm'][:, 1024 * t:1024 * (t + 1)],
                                  scalar1=kc.t['stats'][:, M2c + t:M2c + t + 1],
                                  scalar2=kc.t['stats'][:, R2c + t:R2c + t + 1],
                                  op0=ALU.subtract, op1=ALU.mult)
            kc.inc(ins, 'sv')

        @block.tensor
        def _(tn):
            kc.w(tn, 'sv')
            if t >= 2:
                kc.w(tn, 'sv', ev_x[t - 2][1])
            for f in range(8):
                ins = tn.matmul(out=psB[:, 1024 * (t % 2) + 128 * f:1024 * (t % 2) + 128 * (f + 1)],
                                lhsT=kc.t['utmp2'][:, 1024 * (t % 2) + 128 * f:
                                                   1024 * (t % 2) + 128 * (f + 1)],
                                rhs=kc.t['idnS'][:, :], start=True, stop=True)
            kc.inc(ins, 'st')
        st_t = kc.n['st']

        @block.vector
        def _(v):
            kc.w(v, 'st')
            for f in range(8):
                ins = v.tensor_scalar(
                    out=kc.t['xfm'][:, :].rearrange("p (f a) -> p f a", a=2048)[
                        :, f, 128 * t:128 * (t + 1)],
                    in0=psB[:, 1024 * (t % 2) + 128 * f:1024 * (t % 2) + 128 * (f + 1)],
                    scalar1=kc.t['g1cs'][:, f:f + 1], scalar2=kc.t['b1ls'][:, f:f + 1],
                    op0=ALU.mult, op1=ALU.add)
            kc.inc(ins, 'sv')
        ev_x.append((st_t, kc.n['sv']))
    xfm_done = kc.n['sv']

    if dbg == 'xin':
        return _dbg_dump(kc, 'xfm')
    for t in range(NT):
        @block.tensor
        def _(tn):
            if t == 0:
                kc.w(tn, 'sv', xfm_done)
                kc.w(tn, 'dq')
            if t >= 4:
                kc.w(tn, 'sv')
            for c in range(8):
                ins = tn.matmul(out=psA[:, 8 * t:8 * (t + 1)],
                                lhsT=kc.t['xfm'][:, 2048 * c + 128 * t:2048 * c + 128 * (t + 1)],
                                rhs=kc.t['gwTs'][:, 8 * c:8 * (c + 1)],
                                start=(c == 0), stop=False)
            ins = tn.matmul(out=psA[:, 8 * t:8 * (t + 1)],
                            lhsT=kc.t['ones_row'][:, :], rhs=kc.t['gbr'][:, :],
                            start=False, stop=True)
            kc.inc(ins, 'st')

    @block.vector
    def _(v):
        kc.w(v, 'st')
        ins = v.tensor_copy(out=kc.t['glog'][:, :], in_=psA[:, 0:128])
        kc.inc(ins, 'sv')
        kc.w(v, 'sv')
        gl = kc.t['glog'][:, :].rearrange("p (T e) -> p T e", e=8)
        ins = v.reduce_max(out=kc.t['stats'][:, 0:16], in_=gl, axis=AX.X)
        kc.inc(ins, 'sv')
        kc.w(v, 'sv')
        m1b = kc.t['stats'][:, 0:16].unsqueeze(2).broadcast_to((128, 16, 8))
        ins = v.tensor_tensor(out=kc.t['comb'][:, :].rearrange("p (T e) -> p T e", e=8),
                              in0=gl, in1=m1b, op=ALU.is_equal)
        kc.inc(ins, 'sv')
        kc.w(v, 'sv')
        ins = v.scalar_tensor_tensor(out=kc.t['sqd2'][:, 0:128].rearrange("p (T e) -> p T e", e=8),
                                     in0=kc.t['comb'][:, :].rearrange("p (T e) -> p T e", e=8),
                                     scalar=-30000.0, in1=gl, op0=ALU.mult, op1=ALU.add)
        kc.inc(ins, 'sv')
        kc.w(v, 'sv')
        ins = v.reduce_max(out=kc.t['stats'][:, 16:32],
                           in_=kc.t['sqd2'][:, 0:128].rearrange("p (T e) -> p T e", e=8),
                           axis=AX.X)
        kc.inc(ins, 'sv')
        kc.w(v, 'sv')
        m1b2 = kc.t['stats'][:, 0:16].unsqueeze(2).broadcast_to((128, 16, 8))
        ins = v.tensor_tensor(out=kc.t['sqd2'][:, 0:128].rearrange("p (T e) -> p T e", e=8),
                              in0=gl, in1=m1b2, op=ALU.subtract)
        kc.inc(ins, 'sv')

    @block.scalar
    def _(sc):
        kc.w(sc, 'sv')
        ins = sc.activation(out=kc.t['sqd2'][:, 0:128], in_=kc.t['sqd2'][:, 0:128], func=AF.Exp)
        kc.inc(ins, 'ss')

    @block.vector
    def _(v):
        kc.w(v, 'ss')
        gl = kc.t['glog'][:, :].rearrange("p (T e) -> p T e", e=8)
        m2b = kc.t['stats'][:, 16:32].unsqueeze(2).broadcast_to((128, 16, 8))
        ins = v.tensor_tensor(out=kc.t['comb'][:, :].rearrange("p (T e) -> p T e", e=8),
                              in0=gl, in1=m2b, op=ALU.is_ge)
        kc.inc(ins, 'sv')
        kc.w(v, 'sv')
        ins = v.tensor_tensor(out=kc.t['comb'][:, :], in0=kc.t['comb'][:, :],
                              in1=kc.t['sqd2'][:, 0:128], op=ALU.mult)
        kc.inc(ins, 'sv')
        kc.w(v, 'sv')
        ins = v.reduce_sum(out=kc.t['stats'][:, 32:48],
                           in_=kc.t['comb'][:, :].rearrange("p (T e) -> p T e", e=8), axis=AX.X)
        kc.inc(ins, 'sv')
        kc.w(v, 'sv')
        ins = v.reciprocal(out=kc.t['stats'][:, 48:64], in_=kc.t['stats'][:, 32:48])
        kc.inc(ins, 'sv')
        kc.w(v, 'sv')
        rb = kc.t['stats'][:, 48:64].unsqueeze(2).broadcast_to((128, 16, 8))
        ins = v.tensor_tensor(out=kc.t['comb'][:, :].rearrange("p (T e) -> p T e", e=8),
                              in0=kc.t['comb'][:, :].rearrange("p (T e) -> p T e", e=8),
                              in1=rb, op=ALU.mult)
        kc.inc(ins, 'sv')
    comb_done = kc.n['sv']

    if dbg == 'comb':
        return _dbg_dump(kc, 'comb', 128)
    @block.vector
    def _(v):
        kc.w(v, 'sv', comb_done)
        ins = v.tensor_copy(out=kc.t['combB'][:, :], in_=kc.t['comb'][:, :])
        kc.inc(ins, 'sv')
    combB_done = kc.n['sv']

    ev_b2 = []
    tmpy_dq = []
    for t in range(NT):
        @block.tensor
        def _(tn):
            kc.w(tn, 'sv', combB_done)
            if t >= 2:
                kc.w(tn, 'ss', ev_b2[t - 2])
            ins = tn.matmul(out=psB[0:8, 128 * (t % 2):128 * (t % 2) + 128],
                            lhsT=kc.t['combB'][:, 8 * t:8 * (t + 1)],
                            rhs=kc.t['idnS'][:, :], start=True, stop=True)
            kc.inc(ins, 'st')

        @block.vector
        def _(v):
            kc.w(v, 'st')
            ins = v.tensor_copy(out=kc.t['combT'][:, :],
                                in_=psB[0:8, 128 * (t % 2):128 * (t % 2) + 128])
            kc.inc(ins, 'sv')

        @block.tensor
        def _(tn):
            kc.w(tn, 'sv')
            for half in range(2):
                ins = tn.matmul(out=psA[:, 1024 * (t % 2) + 512 * half:
                                        1024 * (t % 2) + 512 * (half + 1)],
                                lhsT=kc.t['combT'][:, :],
                                rhs=kc.t['b2a'][:, 512 * half:512 * (half + 1)],
                                start=True, stop=True)
            kc.inc(ins, 'st')

        @block.vector
        def _(v):
            kc.w(v, 'st')
            if t >= 1:
                kc.w(v, 'dq', tmpy_dq[t - 1])
            ins = v.tensor_copy(out=kc.t['tmpy'][:, 0:1024],
                                in_=psA[:, 1024 * (t % 2):1024 * (t % 2 + 1)])
            ev_b2.append(kc.inc(ins, 'ss'))

        @block.gpsimd
        def _(g):
            kc.w(g, 'ss')
            kc.inc(g.dma_start(out=dram['moeacc'].ap()[:, 1024 * t:1024 * (t + 1)],
                               in_=kc.t['tmpy'][:, 0:1024]), 'dq', 16)
        tmpy_dq.append(kc.n['dq'])

    kc.nc.leave_named_scope(kc.scpn, kc.scp, False)
    kc.scp = kc.nc.enter_named_scope('ffn_moe', False)[0]; kc.scpn = 'ffn_moe'
    @block.sync
    def _(sy):
        kc.w(sy, 'sv')
        kc.inc(sy.dma_start(out=dram['spill'].ap()[:, :], in_=kc.t['xsb'][:, :]), 'ds', 16)
    spill_done = kc.n['ds']

    def hview(j):
        return kc.arena[0:128, (2048 * j) // 2:(2048 * j) // 2 + 1024]

    w1_consumed = {}
    w1_ld = {}
    hev = {}
    yev = {}
    w2_ld = {}
    y_done = {}
    last_tmpy_dq = [tmpy_dq[-1]]
    PSH = [psA, psB]

    def hreg(j):
        t_ = PSH[(j % 4) // 2]
        off = 1024 * (j % 2)
        return t_[:, off:off + 1024]

    for e in range(8):
        for th in range(2):
            for u in range(16):
                @block.gpsimd
                def _(g):
                    prev = (e, th, u - 4) if u >= 4 else \
                        ((e, th - 1, 12 + u) if th > 0 else
                         ((e - 1, 1, 12 + u) if e > 0 else None))
                    if prev is not None and prev in w1_consumed:
                        kc.w(g, 'st', w1_consumed[prev])
                    elif e == 0 and th == 0 and u == 0:
                        kc.w(g, 'ds', spill_done)
                        kc.w(g, 'sv')
                        kc.w(g, 'st')
                    kc.inc(g.dma_start(
                        out=kc.t['w1u'][:, 2048 * (u % 4):2048 * (u % 4 + 1)]
                            .rearrange("p (j x) -> p j x", j=2),
                        in_=h['w1U'].ap()[e, 2 * u:2 * u + 2, :, :]
                            .rearrange("j (c p) f -> p j c f", p=128)), 'dq', 16)
                    w1_ld[(e, th, u)] = kc.n['dq']
                    if th == 0:
                        if e == 0 and u == 0:
                            kc.w(g, 'st')
                        elif u == 0:
                            kc.w(g, 'st', y_done[e - 1])
                        kc.inc(g.dma_start(
                            out=kc.t['w2f'][:, 2048 * u:2048 * (u + 1)]
                                .rearrange("p (c f) -> p c f", f=1024),
                            in_=h['w2T'].ap()[e, 256 * u:256 * (u + 1), :]
                                .rearrange("(c p) f -> p c f", p=128)), 'dq', 16)
                        w2_ld[(e, u)] = kc.n['dq']

                for jj in range(2):
                    j = 2 * u + jj

                    @block.tensor
                    def _(tn):
                        kc.w(tn, 'dq', w1_ld[(e, th, u)])
                        if e == 0 and th == 0 and j == 0:
                            kc.w(tn, 'ds', spill_done)
                            kc.w(tn, 'sv')
                            kc.w(tn, 'ss')
                        prev4 = (th, j - 4) if j >= 4 else ((th - 1, 28 + j) if th > 0 else None)
                        if prev4 is not None and prev4 in hev:
                            kc.w(tn, 'ss', hev[prev4])
                        pj = hreg(j)
                        for c in range(8):
                            for g4 in range(2):
                                ins = tn.matmul(
                                    out=pj[:, 512 * g4:512 * (g4 + 1)],
                                    lhsT=kc.t['w1u'][:, 2048 * (u % 4) + 1024 * jj + 128 * c:
                                                     2048 * (u % 4) + 1024 * jj + 128 * (c + 1)],
                                    rhs=kc.t['xfm'][:, 2048 * c + 1024 * th + 512 * g4:
                                                    2048 * c + 1024 * th + 512 * (g4 + 1)],
                                    start=(c == 0), stop=(c == 7))
                        if jj == 1:
                            w1_consumed[(e, th, u)] = kc.inc(ins, 'st')
                        else:
                            kc.inc(ins, 'st')

                    @block.scalar
                    def _(sc):
                        kc.w(sc, 'st')
                        ins = sc.activation(out=hview(j), in_=hreg(j), func=AF.Relu,
                                            bias=kc.t['b1cs'][:, 32 * e + j:32 * e + j + 1])
                        hev[(th, j)] = kc.inc(ins, 'ss')

            for tl in range(8):
                @block.tensor
                def _(tn):
                    if tl == 0:
                        kc.w(tn, 'ss', hev[(th, 31)])
                        kc.w(tn, 'dq', w2_ld[(e, 15)])
                    slot = tl % 4
                    pt_ = PSH[slot // 2][:, 1024 * (slot % 2):1024 * (slot % 2) + 1024]
                    pidx = (e, th, tl - 4) if tl >= 4 else \
                        ((e, th - 1, 4 + tl) if th > 0 else ((e - 1, 1, 4 + tl) if e > 0 else None))
                    if pidx is not None and pidx in yev:
                        kc.w(tn, 'ss', yev[pidx])
                    for j2 in range(32):
                        for half in range(2):
                            ins = tn.matmul(
                                out=pt_[:, 512 * half:512 * (half + 1)],
                                lhsT=hview(j2)[:, 128 * tl:128 * (tl + 1)],
                                rhs=kc.t['w2f'][:, 1024 * j2 + 512 * half:
                                                1024 * j2 + 512 * (half + 1)],
                                start=(j2 == 0), stop=(j2 == 31))
                    kc.inc(ins, 'st')
                if th == 1 and tl == 7:
                    y_done[e] = kc.n['st']

                @block.vector
                def _(v):
                    kc.w(v, 'st')
                    kc.w(v, 'dq', last_tmpy_dq[0])
                    slot = tl % 4
                    pt_ = PSH[slot // 2][:, 1024 * (slot % 2):1024 * (slot % 2) + 1024]
                    t = 8 * th + tl
                    ins = v.tensor_scalar(out=kc.t['tmpy'][:, 0:1024],
                                          in0=pt_, scalar1=kc.t['comb'][:, 8 * t + e:8 * t + e + 1],
                                          scalar2=None, op0=ALU.mult)
                    yev[(e, th, tl)] = kc.inc(ins, 'ss')

                @block.gpsimd
                def _(g):
                    kc.w(g, 'ss', yev[(e, th, tl)])
                    t = 8 * th + tl
                    kc.inc(g.dma_start(out=dram['moeacc'].ap()[:, 1024 * t:1024 * (t + 1)],
                                       in_=kc.t['tmpy'][:, 0:1024],
                                       accum_op=ALU.add), 'dq', 16)
                last_tmpy_dq[0] = kc.n['dq']
    moe_done_dq = kc.n['dq']

    @block.sync
    def _(sy):
        kc.w(sy, 'st')
        kc.inc(sy.dma_start(out=kc.t['xsb'][:, :], in_=dram['spill'].ap()[:, :]), 'ds', 16)
    unspill_done = kc.n['ds']

    ev_f = []
    for t in range(NT):
        @block.gpsimd
        def _(g):
            if t == 0:
                kc.w(g, 'dq', moe_done_dq)
            if t >= 2:
                kc.w(g, 'sv')
            kc.inc(g.dma_start(out=kc.t['mtile'][:, 1024 * (t % 2):1024 * (t % 2 + 1)],
                               in_=dram['moeacc'].ap()[:, 1024 * t:1024 * (t + 1)]), 'dq', 16)

        @block.tensor
        def _(tn):
            if t == 0:
                kc.w(tn, 'ss')
            if t >= 2:
                kc.w(tn, 'sv', ev_f[t - 2])
            for f in range(8):
                ins = tn.matmul(out=psA[:, 1024 * (t % 2) + 128 * f:1024 * (t % 2) + 128 * (f + 1)],
                                lhsT=kc.t['xfm'][:, 2048 * f + 128 * t:2048 * f + 128 * (t + 1)],
                                rhs=kc.t['idnS'][:, :], start=True, stop=True)
            kc.inc(ins, 'st')

        @block.vector
        def _(v):
            kc.w(v, 'st')
            kc.w(v, 'dq')
            ins = v.scalar_tensor_tensor(out=kc.t['t2all'][:, 1024 * t:1024 * (t + 1)],
                                         in0=psA[:, 1024 * (t % 2):1024 * (t % 2 + 1)], scalar=1.0,
                                         in1=kc.t['mtile'][:, 1024 * (t % 2):1024 * (t % 2 + 1)],
                                         op0=ALU.mult, op1=ALU.add)
            kc.inc(ins, 'sv')
        ev_f.append(kc.n['sv'])

    if dbg == 'moe':
        return _dbg_dump(kc, 't2all')
    M3, R3 = emit_ln_stats(kc, 't2all', list(range(NT)), 0, 'sqd2')
    for t in range(NT):
        @block.vector
        def _(v):
            kc.w(v, 'ss')
            if t == 0:
                kc.w(v, 'ds', unspill_done)
            ins = v.tensor_scalar(out=kc.t['utmp2'][:, 1024 * (t % 2):1024 * (t % 2 + 1)],
                                  in0=kc.t['t2all'][:, 1024 * t:1024 * (t + 1)],
                                  scalar1=kc.t['stats'][:, M3 + t:M3 + t + 1],
                                  scalar2=kc.t['stats'][:, R3 + t:R3 + t + 1],
                                  op0=ALU.subtract, op1=ALU.mult)
            kc.inc(ins, 'sv')
            kc.w(v, 'sv')
            ins = v.tensor_tensor(out=kc.t['utmp2'][:, 1024 * (t % 2):1024 * (t % 2 + 1)],
                                  in0=kc.t['utmp2'][:, 1024 * (t % 2):1024 * (t % 2 + 1)],
                                  in1=kc.t['g2B'][:, :], op=ALU.mult)
            kc.inc(ins, 'sv')
            kc.w(v, 'sv')
            ins = v.tensor_tensor(out=kc.t['utmp2'][:, 1024 * (t % 2):1024 * (t % 2 + 1)],
                                  in0=kc.t['utmp2'][:, 1024 * (t % 2):1024 * (t % 2 + 1)],
                                  in1=kc.t['b2B'][:, :], op=ALU.add)
            kc.inc(ins, 'sv')
            kc.w(v, 'sv')
            ins = v.tensor_tensor(out=kc.t['xsb'][:, 1024 * t:1024 * (t + 1)],
                                  in0=kc.t['xsb'][:, 1024 * t:1024 * (t + 1)],
                                  in1=kc.t['utmp2'][:, 1024 * (t % 2):1024 * (t % 2 + 1)],
                                  op=ALU.add)
            kc.inc(ins, 'sv')
    kc.nc.leave_named_scope(kc.scpn, kc.scp, False)


def run_s3(kc, h, dram):
    block = kc.block
    psA, psB = kc.t['psA'], kc.t['psB']
    kc.reg('u2f', 49152, 8 * 2048, BF16)
    kc.reg('scr3', 197632, 2 * 1024, BF16)
    kc.reg('sqd3', 201728, 1024, F32)
    kc.reg('stg', 32768, 2 * 8 * 512, BF16)
    kc.reg('qfm', 49152, 16384, BF16)
    kc.reg('kfm3', 81920, 16384, BF16)
    kc.reg('vtm3', 114688, 128 * 64, BF16)
    kc.reg('pb3', 131072, 2 * 1280, BF16)
    kc.reg('pt3', 136192, 2 * 1280, BF16)
    kc.reg('dg3', 141312, 2 * 128, BF16)
    kc.reg('ctxall', 141824, 16384, BF16)
    kc.reg('wfb', 174592, 8 * 1024, BF16)
    kc.reg('mcauB', 190976, 128, BF16)
    kc.reg('mantB', 191232, 128, BF16)
    kc.reg('bfr', 191488, 1024, BF16, rows=1)
    kc.reg('wqs', 193536, 8 * 128, BF16)
    kc.reg('wks', 195584, 8 * 128, BF16)
    kc.reg('wvs', 205824, 8 * 64, BF16)
    kc.reg('ctxf3', 49152, 8 * 2048, BF16)

    @block.sync
    def _(sy):
        kc.w(sy, 'st'); kc.w(sy, 'sv'); kc.w(sy, 'ss')
        kc.inc(sy.dma_start(out=kc.t['mcauB'][:, :], in_=h['mcaus'].ap()[:, :]), 'ds', 16)
        kc.inc(sy.dma_start(out=kc.t['mantB'][:, :], in_=h['manti'].ap()[:, :]), 'ds', 16)

    @block.gpsimd
    def _(g):
        kc.w(g, 'st'); kc.w(g, 'sv'); kc.w(g, 'ss')
        kc.inc(g.dma_start(out=kc.t['bfr'][:, :], in_=h['bf_row'].ap()[:, :]), 'dq', 16)
        kc.inc(g.dma_start(out=kc.t['wqs'][:, :].rearrange("p (c f) -> p c f", f=128),
                           in_=h['wqbT'].ap().rearrange("(c p) f -> p c f", p=128)), 'dq', 16)
        kc.inc(g.dma_start(out=kc.t['wks'][:, :].rearrange("p (c f) -> p c f", f=128),
                           in_=h['wkbT2'].ap().rearrange("(c p) f -> p c f", p=128)), 'dq', 16)
        kc.inc(g.dma_start(out=kc.t['wvs'][:, :].rearrange("p (c f) -> p c f", f=64),
                           in_=h['wvbT'].ap().rearrange("(c p) f -> p c f", p=128)), 'dq', 16)
        kc.inc(g.dma_start(out=kc.t['wfb'][:, :].rearrange("p (c f) -> p c f", f=1024),
                           in_=h['wfT'].ap().rearrange("(c p) f -> p c f", p=128)), 'dq', 16)
    ld_s3 = kc.n['dq']

    kc.scp = kc.nc.enter_named_scope('s3_ln_ag', False)[0]; kc.scpn = 's3_ln_ag'
    M1, R1 = emit_ln_stats(kc, 'xsb', list(range(NT)), 0, 'sqd3')
    evs = []
    for t in range(NT):
        @block.vector
        def _(v):
            kc.w(v, 'ss')
            ins = v.tensor_scalar(out=kc.t['scr3'][:, 1024 * (t % 2):1024 * (t % 2 + 1)],
                                  in0=kc.t['xsb'][:, 1024 * t:1024 * (t + 1)],
                                  scalar1=kc.t['stats'][:, M1 + t:M1 + t + 1],
                                  scalar2=kc.t['stats'][:, R1 + t:R1 + t + 1],
                                  op0=ALU.subtract, op1=ALU.mult)
            kc.inc(ins, 'sv')

        @block.tensor
        def _(tn):
            kc.w(tn, 'sv')
            if t >= 2:
                kc.w(tn, 'sv', evs[t - 2])
            for f in range(8):
                ins = tn.matmul(out=psA[:, 1024 * (t % 2) + 128 * f:1024 * (t % 2) + 128 * (f + 1)],
                                lhsT=kc.t['scr3'][:, 1024 * (t % 2) + 128 * f:
                                                  1024 * (t % 2) + 128 * (f + 1)],
                                rhs=kc.t['idnS'][:, :], start=True, stop=True)
            kc.inc(ins, 'st')

        @block.vector
        def _(v):
            kc.w(v, 'st')
            ov = kc.t['u2f'][:, :].rearrange("p (f a) -> p f a", a=2048)[:, :, 128 * t:128 * (t + 1)]
            iv = psA[:, 1024 * (t % 2):1024 * (t % 2 + 1)].rearrange("p (f a) -> p f a", a=128)
            ins = v.tensor_copy(out=ov, in_=iv)
            kc.inc(ins, 'sv')
        evs.append(kc.n['sv'])

    @block.sync
    def _(sy):
        kc.w(sy, 'sv')
        for f in range(8):
            kc.inc(sy.dma_start(out=dram['u2c'].ap()[128 * f:128 * (f + 1), :],
                                in_=kc.t['u2f'][:, 2048 * f:2048 * (f + 1)]), 'ds', 16)
    contrib_done = kc.n['ds']

    @block.gpsimd
    def _(g):
        kc.w(g, 'ds', contrib_done)
        ins = g.collective_compute("AllGather", ALU.bypass,
                                   replica_groups=[list(range(8))],
                                   ins=[dram['u2c'].ap().opt()],
                                   outs=[dram['uall'].ap().opt()])
        kc.inc(ins, 'cc', 1)
    ag_done = kc.n['cc']

    kc.nc.leave_named_scope(kc.scpn, kc.scp, False)
    kc.scp = kc.nc.enter_named_scope('s3_proj', False)[0]; kc.scpn = 's3_proj'
    ev_p = []
    for g512 in range(32):
        r, cg = g512 // 4, g512 % 4

        @block.sync
        def _(sy):
            if g512 == 0:
                kc.w(sy, 'cc', ag_done)
            if g512 >= 2:
                kc.w(sy, 'st', ev_p[g512 - 2][0])
            for kcc in range(8):
                kc.inc(sy.dma_start(
                    out=kc.t['stg'][:, 4096 * (g512 % 2) + 512 * kcc:
                                    4096 * (g512 % 2) + 512 * (kcc + 1)],
                    in_=dram['uall'].ap()[1024 * r + 128 * kcc:1024 * r + 128 * (kcc + 1),
                                          512 * cg:512 * (cg + 1)]), 'ds', 16)

        PSP = psB if g512 % 2 == 0 else psA

        @block.tensor
        def _(tn, PSP=PSP):
            kc.w(tn, 'ds')
            if g512 == 0:
                kc.w(tn, 'dq', ld_s3)
            if g512 >= 2:
                kc.w(tn, 'ss', ev_p[g512 - 2][1])
            base = 4096 * (g512 % 2)
            for kcc in range(8):
                tn.matmul(out=PSP[:, 0:512],
                          lhsT=kc.t['wqs'][:, 128 * kcc:128 * (kcc + 1)],
                          rhs=kc.t['stg'][:, base + 512 * kcc:base + 512 * (kcc + 1)],
                          start=(kcc == 0), stop=(kcc == 7))
            for kcc in range(8):
                tn.matmul(out=PSP[:, 512:1024],
                          lhsT=kc.t['wks'][:, 128 * kcc:128 * (kcc + 1)],
                          rhs=kc.t['stg'][:, base + 512 * kcc:base + 512 * (kcc + 1)],
                          start=(kcc == 0), stop=(kcc == 7))
            for ti in range(4):
                for kcc in range(8):
                    ins = tn.matmul(out=PSP[:, 1024 + 64 * ti:1024 + 64 * (ti + 1)],
                                    lhsT=kc.t['stg'][:, base + 512 * kcc + 128 * ti:
                                                     base + 512 * kcc + 128 * (ti + 1)],
                                    rhs=kc.t['wvs'][:, 64 * kcc:64 * (kcc + 1)],
                                    start=(kcc == 0), stop=(kcc == 7))
            kc.inc(ins, 'st')
        ev_mm = kc.n['st']

        @block.vector
        def _(v, PSP=PSP):
            kc.w(v, 'st', ev_mm)
            v.tensor_scalar(out=kc.t['qfm'][:, 512 * g512:512 * (g512 + 1)],
                            in0=PSP[:, 0:512], scalar1=-0.125, scalar2=None, op0=ALU.mult)
            v.tensor_copy(out=kc.t['kfm3'][:, 512 * g512:512 * (g512 + 1)],
                          in_=PSP[:, 512:1024])
            ins = v.tensor_copy(out=kc.t['vtm3'][:, 256 * g512:256 * (g512 + 1)],
                                in_=PSP[:, 1024:1280])
            kc.inc(ins, 'ss')
        ev_p.append((ev_mm, kc.n['ss']))
    proj_done = kc.n['ss']
    kc.nc.leave_named_scope(kc.scpn, kc.scp, False)
    kc.scp = kc.nc.enter_named_scope('s3_attn', False)[0]; kc.scpn = 's3_attn'
    if getattr(kc, 'dbg', None) == 'qb':
        return _dbg_dump(kc, 'qfm')
    if getattr(kc, 'dbg', None) == 'kb':
        return _dbg_dump(kc, 'kfm3')
    if getattr(kc, 'dbg', None) == 'vb':
        return _dbg_dump(kc, 'vtm3', 8192)

    insts = [(gq, s) for gq in range(16) for s in range(8)]
    SREG = [psA, psB]
    kc.reg('pb3b', 32768, 2 * 1280, BF16)
    kc.reg('dg3b', 38400, 2 * 128, BF16)
    kc.reg('pt3b', 40960, 2 * 1280, BF16)
    PBV = lambda n: kc.t['pb3'] if n % 2 == 0 else kc.t['pb3b']
    DGV = lambda n: kc.t['dg3'] if n % 2 == 0 else kc.t['dg3b']
    PTV = lambda n: kc.t['pt3'] if n % 2 == 0 else kc.t['pt3b']
    sc_mm = {}; min_ev = {}; exp_ev = {}; rc_ev = {}; dg_ev = {}
    ptA_mm = {}; pt_mm = {}; rA_ev = {}; rB_ev = {}; av_mm = {}; ctx_ev = {}
    for hd in range(2):
        for nn in (-2, -1):
            pt_mm[(nn, hd)] = kc.n['st']
            av_mm[(nn, hd)] = kc.n['st']
            rA_ev[(nn, hd)] = kc.n['ss']
            rB_ev[(nn, hd)] = kc.n['sv']
            dg_ev[(nn, hd)] = kc.n['sg']
    ctx_ev[-1] = kc.n['sv']

    def _kinfo(gq, s, nk, b):
        kc128 = gq - nk + 1 + b
        kcol = 2048 * (kc128 // 2) + 256 * s + 128 * (kc128 % 2)
        vtile = 16 * (kc128 // 2) + 2 * s + (kc128 % 2)
        return kcol, vtile

    def _emit_scores(n):
        gq, s = insts[n]
        nk = min(gq + 1, 9)
        qcol = 2048 * (gq // 2) + 256 * s + 128 * (gq % 2)
        for hd in range(2):
            @block.tensor
            def _(tn, gq=gq, s=s, nk=nk, qcol=qcol, hd=hd, n=n):
                if n == 0:
                    kc.w(tn, 'ss', proj_done)
                else:
                    kc.w(tn, 'ss', exp_ev[(n - 1, hd)])
                for b in range(nk):
                    kcol, _vt3 = _kinfo(gq, s, nk, b)
                    masked = (b == nk - 1) or (nk == 9 and b == 0)
                    ins = tn.matmul(out=SREG[hd][:, 128 * b:128 * (b + 1)],
                                    lhsT=kc.t['qfm'][64 * hd:64 * (hd + 1), qcol:qcol + 128],
                                    rhs=kc.t['kfm3'][64 * hd:64 * (hd + 1), kcol:kcol + 128],
                                    start=True, stop=not masked)
                    if masked:
                        mk = 'mcauB' if b == nk - 1 else 'mantB'
                        ins = tn.matmul(out=SREG[hd][:, 128 * b:128 * (b + 1)],
                                        lhsT=kc.t['idnS'][:, :], rhs=kc.t[mk][:, :],
                                        start=False, stop=True)
                sc_mm[(n, hd)] = kc.inc(ins, 'st')

    _emit_scores(0)
    N_INST = len(insts)
    for n in range(N_INST):
        gq, s = insts[n]
        nk = min(gq + 1, 9)
        nA = min(nk, 5)
        nB = nk - nA
        qcol = 2048 * (gq // 2) + 256 * s + 128 * (gq % 2)

        @block.vector
        def _(v, n=n):
            if n > 0:
                kc.w(v, 'st', av_mm[(n - 1, 1)])
                gq1, s1_ = insts[n - 1]
                qc1 = 2048 * (gq1 // 2) + 256 * s1_ + 128 * (gq1 % 2)
                ins = v.tensor_copy(out=kc.t['ctxall'][:, qc1:qc1 + 128],
                                    in_=psA[:, 1920:2048])
                ctx_ev[n - 1] = kc.inc(ins, 'sv')
            for hd in range(2):
                kc.w(v, 'st', sc_mm[(n, hd)])
                ins = v.tensor_reduce(out=kc.t['stats'][:, 120 + hd:121 + hd],
                                      in_=SREG[hd][:, 0:128 * nk], axis=AX.X, op=ALU.min)
                min_ev[(n, hd)] = kc.inc(ins, 'sv')

        @block.scalar
        def _(sc, n=n, nk=nk):
            for hd in range(2):
                kc.w(sc, 'sv', min_ev[(n, hd)])
                kc.w(sc, 'st', pt_mm[(n - 2, hd)])
                ins = sc.activation(out=PBV(n)[:, 1280 * hd:1280 * hd + 128 * nk],
                                    in_=SREG[hd][:, 0:128 * nk], func=AF.Exp, scale=-1.0,
                                    bias=kc.t['stats'][:, 120 + hd:121 + hd],
                                    accum_out=kc.t['stats'][:, 122 + hd:123 + hd])
                exp_ev[(n, hd)] = kc.inc(ins, 'ss')

        @block.vector
        def _(v, n=n):
            for hd in range(2):
                kc.w(v, 'ss', exp_ev[(n, hd)])
                kc.w(v, 'sg', dg_ev[(n - 1, hd)])
                ins = v.reciprocal(out=kc.t['stats'][:, 124 + hd:125 + hd],
                                   in_=kc.t['stats'][:, 122 + hd:123 + hd])
                rc_ev[(n, hd)] = kc.inc(ins, 'sv')

        @block.gpsimd
        def _(g, n=n):
            for hd in range(2):
                kc.w(g, 'sv', rc_ev[(n, hd)])
                kc.w(g, 'st', pt_mm[(n - 2, hd)])
                ins = g.tensor_scalar(out=DGV(n)[:, 128 * hd:128 * (hd + 1)],
                                      in0=kc.t['idnS'][:, :],
                                      scalar1=kc.t['stats'][:, 124 + hd:125 + hd],
                                      scalar2=None, op0=ALU.mult)
                dg_ev[(n, hd)] = kc.inc(ins, 'sg')

        for hd in range(2):
            @block.tensor
            def _(tn, n=n, hd=hd, nA=nA):
                kc.w(tn, 'sg', dg_ev[(n, hd)])
                kc.w(tn, 'ss', rA_ev[(n - 1, hd)])
                kc.w(tn, 'sv', rB_ev[(n - 1, hd)])
                for b in range(nA):
                    ins = tn.matmul(out=SREG[hd][:, 1152 + 128 * b:1152 + 128 * (b + 1)],
                                    lhsT=PBV(n)[:, 1280 * hd + 128 * b:
                                                1280 * hd + 128 * (b + 1)],
                                    rhs=DGV(n)[:, 128 * hd:128 * (hd + 1)],
                                    start=True, stop=True)
                ptA_mm[(n, hd)] = kc.inc(ins, 'st')
        if n + 1 < N_INST:
            _emit_scores(n + 1)

        @block.scalar
        def _(sc, n=n, nA=nA):
            for hd in range(2):
                kc.w(sc, 'st', ptA_mm[(n, hd)])
                kc.w(sc, 'st', av_mm[(n - 2, hd)])
                ins = sc.activation(out=PTV(n)[:, 1280 * hd:1280 * hd + 128 * nA],
                                    in_=SREG[hd][:, 1152:1152 + 128 * nA], func=AF.Relu)
                rA_ev[(n, hd)] = kc.inc(ins, 'ss')

        if nB > 0:
            for hd in range(2):
                @block.tensor
                def _(tn, n=n, hd=hd, nA=nA, nB=nB):
                    kc.w(tn, 'ss', rA_ev[(n, hd)])
                    for b in range(nB):
                        ins = tn.matmul(out=SREG[hd][:, 1152 + 128 * b:1152 + 128 * (b + 1)],
                                        lhsT=PBV(n)[:, 1280 * hd + 128 * (nA + b):
                                                    1280 * hd + 128 * (nA + b + 1)],
                                        rhs=DGV(n)[:, 128 * hd:128 * (hd + 1)],
                                        start=True, stop=True)
                    pt_mm[(n, hd)] = kc.inc(ins, 'st')

            @block.vector
            def _(v, n=n, nA=nA, nB=nB):
                for hd in range(2):
                    kc.w(v, 'st', pt_mm[(n, hd)])
                    kc.w(v, 'st', av_mm[(n - 2, hd)])
                    ins = v.tensor_copy(out=PTV(n)[:, 1280 * hd + 128 * nA:
                                                   1280 * hd + 128 * (nA + nB)],
                                        in_=SREG[hd][:, 1152:1152 + 128 * nB])
                    rB_ev[(n, hd)] = kc.inc(ins, 'sv')
        else:
            for hd in range(2):
                pt_mm[(n, hd)] = ptA_mm[(n, hd)]
                rB_ev[(n, hd)] = rB_ev[(n - 1, hd)]

        for hd in range(2):
            @block.tensor
            def _(tn, gq=gq, s=s, n=n, hd=hd, nk=nk, nA=nA, nB=nB):
                kc.w(tn, 'ss', rA_ev[(n, hd)])
                kc.w(tn, 'sv', ctx_ev[n - 1])
                if nB > 0:
                    kc.w(tn, 'sv', rB_ev[(n, hd)])
                for b in range(nk):
                    _kc, vtile = _kinfo(gq, s, nk, b)
                    ins = tn.matmul(out=psA[64 * hd:64 * (hd + 1), 1920:2048],
                                    lhsT=kc.t['vtm3'][:, 64 * vtile:64 * (vtile + 1)],
                                    rhs=PTV(n)[:, 1280 * hd + 128 * b:
                                               1280 * hd + 128 * (b + 1)],
                                    start=(b == 0), stop=(b == nk - 1),
                                    tile_position=(0, 64 * hd))
                av_mm[(n, hd)] = kc.inc(ins, 'st')

    @block.vector
    def _(v):
        kc.w(v, 'st', av_mm[(N_INST - 1, 1)])
        gq1, s1_ = insts[N_INST - 1]
        qc1 = 2048 * (gq1 // 2) + 256 * s1_ + 128 * (gq1 % 2)
        ins = v.tensor_copy(out=kc.t['ctxall'][:, qc1:qc1 + 128],
                            in_=psA[:, 1920:2048])
        kc.inc(ins, 'sv')
    att_done = kc.n['sv']
    kc.nc.leave_named_scope(kc.scpn, kc.scp, False)
    kc.scp = kc.nc.enter_named_scope('s3_a2a_wf', False)[0]; kc.scpn = 's3_a2a_wf'
    if getattr(kc, 'dbg', None) == 'att':
        return _dbg_dump(kc, 'ctxall')

    @block.sync
    def _(sy):
        kc.w(sy, 'sv', att_done)
        for f in range(8):
            kc.inc(sy.dma_start(out=dram['cta'].ap()[128 * f:128 * (f + 1), :],
                                in_=kc.t['ctxall'][:, 2048 * f:2048 * (f + 1)]), 'ds', 16)
    cta_done = kc.n['ds']

    @block.gpsimd
    def _(g):
        kc.w(g, 'ds', cta_done)
        with allow_ici_all_to_all():
            ins = g.collective_compute("AllToAll", ALU.bypass,
                                       replica_groups=[list(range(8))],
                                       ins=[dram['cta'].ap().opt()],
                                       outs=[dram['ctb'].ap().opt()])
        kc.inc(ins, 'cc2', 1)
    a2a_done = kc.n['cc2']

    @block.sync
    def _(sy):
        kc.w(sy, 'cc2', a2a_done)
        kc.w(sy, 'st')
        for f in range(8):
            kc.inc(sy.dma_start(out=kc.t['ctxf3'][:, 2048 * f:2048 * (f + 1)],
                                in_=dram['ctb'].ap()[128 * f:128 * (f + 1), :]), 'ds', 16)
    ctb_ld = kc.n['ds']

    ev_wf = []
    for t in range(NT):
        @block.tensor
        def _(tn):
            if t == 0:
                kc.w(tn, 'ds', ctb_ld)
            if t >= 2:
                kc.w(tn, 'sv', ev_wf[t - 2])
            reg = t % 2
            for c in range(8):
                tn.matmul(out=psB[:, 1024 * reg:1024 * reg + 512],
                          lhsT=kc.t['ctxf3'][:, 2048 * c + 128 * t:2048 * c + 128 * (t + 1)],
                          rhs=kc.t['wfb'][:, 1024 * c:1024 * c + 512],
                          start=(c == 0), stop=False)
                tn.matmul(out=psB[:, 1024 * reg + 512:1024 * reg + 1024],
                          lhsT=kc.t['ctxf3'][:, 2048 * c + 128 * t:2048 * c + 128 * (t + 1)],
                          rhs=kc.t['wfb'][:, 1024 * c + 512:1024 * (c + 1)],
                          start=(c == 0), stop=False)
            for half in range(2):
                ins = tn.matmul(out=psB[:, 1024 * reg + 512 * half:1024 * reg + 512 * (half + 1)],
                                lhsT=kc.t['ones_row'][:, :],
                                rhs=kc.t['bfr'][:, 512 * half:512 * (half + 1)],
                                start=False, stop=True)
            kc.inc(ins, 'st')

        @block.vector
        def _(v):
            kc.w(v, 'st')
            reg = t % 2
            ins = v.scalar_tensor_tensor(out=kc.t['xsb'][:, 1024 * t:1024 * (t + 1)],
                                         in0=psB[:, 1024 * reg:1024 * (reg + 1)], scalar=1.0,
                                         in1=kc.t['xsb'][:, 1024 * t:1024 * (t + 1)],
                                         op0=ALU.mult, op1=ALU.add)
            kc.inc(ins, 'sv')
        ev_wf.append(kc.n['sv'])
    kc.nc.leave_named_scope(kc.scpn, kc.scp, False)
def build(stages=4, dbg=None):
    nc = bass.Bass("TRN2", target_bir_lowering=False)
    es = ExitStack()
    h = {}
    for name, shape, dt in [
        ('xs', [8, 320, 1024], F32), ('wqaT', [1024, 1024], F32), ('wkaT', [1024, 1024], F32),
        ('wvaT', [1024, 1024], F32), ('woaT', [1024, 1024], F32),
        ('winU', [24, 1024, 128], F32), ('binc', [128, 24], F32), ('woutT', [1024, 1024], F32),
        ('outb_row', [1, 1024], F32), ('gwT', [1024, 8], F32), ('gb_row', [1, 8], F32),
        ('w1U', [8, 32, 1024, 128], F32), ('w2T', [8, 4096, 1024], F32),
        ('b1c', [128, 256], F32), ('b2a', [8, 1024], F32),
        ('g1c', [128, 8], F32), ('b1lc', [128, 8], F32),
        ('g2B', [128, 1024], F32), ('b2B', [128, 1024], F32),
        ('wqbT', [1024, 128], F32), ('wkbT2', [1024, 128], F32), ('wvbT', [1024, 64], F32),
        ('wfT', [1024, 1024], F32), ('bf_row', [1, 1024], F32),
        ('idn', [128, 128], BF16), ('mska', [2, 128, 192], BF16),
        ('mcaus', [128, 128], BF16), ('manti', [128, 128], BF16),
        ('mq', [17, 2048], BF16), ('mk', [17, 2048], BF16),
    ]:
        h[name] = nc.dram_tensor(name, shape, dt, kind="ExternalInput")
    y_h = nc.dram_tensor("y", [8, 256, 1024], F32, kind="ExternalOutput")
    dram = {
        'spill': nc.dram_tensor("spill", [128, NT * 1024], BF16),
        'u2c': nc.dram_tensor("u2c", [1024, 2048], BF16),
        'uall': nc.dram_tensor("uall", [8192, 2048], BF16, addr_space="Shared"),
        'cta': nc.dram_tensor("cta", [1024, 2048], BF16),
        'ctb': nc.dram_tensor("ctb", [1024, 2048], BF16),
        'moeacc': nc.dram_tensor("moeacc", [128, NT * 1024], F32),
    }

    with es:
        block = es.enter_context(nc.Block())
        kc = KC(nc, es, block)
        kc.h = h
        kc.dram = dram
        kc.dbg = dbg
        kc.mksem('dq', 'ds', 'sv', 'ss', 'st', 'sg', 'cc', 'cc2')

        kc.sb('idnS', [128, 128], BF16)
        kc.sb('epsc', [128, 1], F32)
        kc.sb('ones_row', [1, 128], BF16)
        kc.sb('stats', [128, 128], F32)
        kc.arena = kc.sb('arena', [128, 105728], BF16)
        kc.reg('xsb', 0, NT * 1024, BF16)
        kc.ps('psA', [128, 2048])
        kc.ps('psB', [128, 2048])

        @block.vector
        def _(v):
            v.memset(kc.t['epsc'][:, :], EPS)
            v.memset(kc.t['ones_row'][:, :], 1.0)

        @block.sync
        def _(sy):
            kc.inc(sy.dma_start(out=kc.t['idnS'][:, :], in_=h['idn'].ap()[:, :]), 'ds', 16)

        @block.gpsimd
        def _(g):
            for t in range(NT):
                s, hv = t // 2, t % 2
                kc.inc(g.dma_start(out=kc.t['xsb'][:, 1024 * t:1024 * (t + 1)],
                                   in_=h['xs'].ap()[s, 64 + 128 * hv:64 + 128 * (hv + 1), :]),
                       'dq', 16)

        with nc.named_scope('s1'):
            run_s1(kc, h)
        if stages >= 2:
            with nc.named_scope('ffn1'):
                run_ffn(kc, h, tag='f1')
        if stages >= 3:
            with nc.named_scope('s3'):
                run_s3(kc, h, dram)
        if stages >= 4:
            with nc.named_scope('ffn2'):
                run_ffn(kc, h, tag='f2')

        kc.reg('yout', 32768, 2 * 1024, F32)

        @block.vector
        def _(v):
            kc.w(v, 'sv'); kc.w(v, 'ss'); kc.w(v, 'st'); kc.w(v, 'sg')

        for t in range(NT):
            @block.vector
            def _(v, t=t):
                if t >= 2:
                    kc.w(v, 'ds')
                ins = v.tensor_copy(out=kc.t['yout'][:, 1024 * (t % 2):1024 * (t % 2 + 1)],
                                    in_=kc.t['xsb'][:, 1024 * t:1024 * (t + 1)])
                kc.inc(ins, 'sv')

            @block.sync
            def _(sy, t=t):
                kc.w(sy, 'sv')
                s, hv = t // 2, t % 2
                kc.inc(sy.dma_start(out=y_h.ap()[s, 128 * hv:128 * (hv + 1), :],
                                    in_=kc.t['yout'][:, 1024 * (t % 2):1024 * (t % 2 + 1)]),
                       'ds', 16)

        @block.sync
        def _(sy):
            kc.w(sy, 'ds')
            kc.w(sy, 'dq')
            kc.w(sy, 'sg')
    return nc


def kernel(**inputs):
    stages = getattr(kernel, 'STAGES', 4)
    in_maps = host_prep(inputs)
    nc = build(stages, getattr(kernel, 'DBG', None))
    if getattr(kernel, 'TRACE', False):
        res = run_bass_kernel_spmd(nc, in_maps, core_ids=list(range(8)), trace=True)
        kernel.last_scope_times = res.per_core_scope_times
        kernel.last_exec_ns = res.exec_time_ns
    else:
        res = run_bass_kernel_spmd(nc, in_maps, core_ids=list(range(8)))
    return np.concatenate([r["y"] for r in res.results], axis=1)
